# version 03
# baseline (speedup 1.0000x reference)
import numpy as np
import ml_dtypes
import concourse.bass as bass
import concourse.mybir as mybir
from concourse.bass_utils import run_bass_kernel_spmd
from contextlib import ExitStack

F32 = mybir.dt.float32
BF16 = mybir.dt.bfloat16
AF = mybir.ActivationFunctionType
ALU = mybir.AluOpType
AX = mybir.AxisListType
BF = ml_dtypes.bfloat16

D = 2048
HQ = 16
BLK = 256
MAIN = 4096
LB = 2048
EXT = MAIN + LB
NBM = MAIN // BLK
NBL = LB // BLK
NBE = NBM + NBL
E = 1024
NEXP = 6
NE = 8
SED = 2048
MT = 512
NMT = MAIN // MT

KEYS = ("tensor", "vector", "scalar", "sync", "gpsimd")


class SemC:
    __slots__ = ("h", "v")

    def __init__(self, h):
        self.h = h
        self.v = 0

    def bump(self, n=1):
        self.v += n
        return self.v


def build(dbg=(), stop=None):
    import os as _os0
    nc = bass.Bass("TRN2", target_bir_lowering=False, detect_race_conditions=(_os0.environ.get("NORACE") is None))
    top = ExitStack()
    Et = top.enter_context

    x_ext = nc.dram_tensor("x_ext", [EXT, D], F32, kind="ExternalInput").ap()
    slope = nc.dram_tensor("slope", [1, HQ], F32, kind="ExternalInput").ap()
    wqT = nc.dram_tensor("wqT", [D, 4 * D], F32, kind="ExternalInput").ap()
    woT = nc.dram_tensor("woT", [D, D], F32, kind="ExternalInput").ap()
    wg_in = nc.dram_tensor("wg_in", [2, D, NE], F32, kind="ExternalInput").ap()
    bg_in = nc.dram_tensor("bg_in", [2, NE], F32, kind="ExternalInput").ap()
    wup_in = nc.dram_tensor("wup_in", [NEXP, D, E], F32, kind="ExternalInput").ap()
    bup_in = nc.dram_tensor("bup_in", [NEXP, E], F32, kind="ExternalInput").ap()
    wsw_in = nc.dram_tensor("wsw_in", [NEXP, E, E], F32, kind="ExternalInput").ap()
    bsw_in = nc.dram_tensor("bsw_in", [NEXP, E], F32, kind="ExternalInput").ap()
    wdn_in = nc.dram_tensor("wdn_in", [NEXP, E, D], F32, kind="ExternalInput").ap()
    bdn_in = nc.dram_tensor("bdn_in", [NEXP, D], F32, kind="ExternalInput").ap()
    wsu_in = nc.dram_tensor("wsu_in", [D, SED], F32, kind="ExternalInput").ap()
    bsu_in = nc.dram_tensor("bsu_in", [1, SED], F32, kind="ExternalInput").ap()
    wsa_in = nc.dram_tensor("wsa_in", [E, E], F32, kind="ExternalInput").ap()
    bsa_in = nc.dram_tensor("bsa_in", [1, E], F32, kind="ExternalInput").ap()
    wsd_in = nc.dram_tensor("wsd_in", [SED, D], F32, kind="ExternalInput").ap()
    bsd_in = nc.dram_tensor("bsd_in", [1, D], F32, kind="ExternalInput").ap()
    y_out = nc.dram_tensor("y", [MAIN, D], F32, kind="ExternalOutput").ap()

    s_xln = nc.dram_tensor("s_xln", [EXT, D], BF16).ap()
    s_qfm = nc.dram_tensor("s_qfm", [D, MAIN], BF16).ap()
    s_kfm = nc.dram_tensor("s_kfm", [D, MAIN], BF16).ap()
    s_ktm = nc.dram_tensor("s_ktm", [HQ, 128, EXT // 128, 128], BF16).ap()
    s_vtm = nc.dram_tensor("s_vtm", [HQ, 128, EXT // 128, 128], BF16).ap()
    s_ufm = nc.dram_tensor("s_ufm", [D, MAIN], BF16).ap()
    s_ofm = nc.dram_tensor("s_ofm", [D, MAIN], F32).ap()
    s_xr2 = nc.dram_tensor("s_xr2", [MAIN, D], F32).ap()
    s_xl2 = nc.dram_tensor("s_xl2", [MAIN, D], BF16).ap()
    s_o1 = nc.dram_tensor("s_o1", [2, D, MT], BF16).ap()
    z_up = nc.dram_tensor("z_up", [NEXP, D, E], BF16).ap()
    z_sw = nc.dram_tensor("z_sw", [NEXP, E, E], BF16).ap()
    z_dn = nc.dram_tensor("z_dn", [NEXP, 2, E, E], BF16).ap()
    z_su = nc.dram_tensor("z_su", [D, SED], BF16).ap()
    z_sa = nc.dram_tensor("z_sa", [E, E], BF16).ap()
    z_sd = nc.dram_tensor("z_sd", [2, SED, E], BF16).ap()

    scratch_map = {
        "s_xln": s_xln, "s_qfm": s_qfm, "s_kfm": s_kfm, "s_ktm": s_ktm,
        "s_vtm": s_vtm, "s_ufm": s_ufm, "s_ofm": s_ofm, "s_xr2": s_xr2,
        "s_xl2": s_xl2,
    }
    dbg_outs = {}
    for name in dbg:
        t = scratch_map[name]
        dbg_outs[name] = nc.dram_tensor("dbg_" + name, list(t.shape),
                                        t.dtype, kind="ExternalOutput").ap()

    il_id_bf = nc.inline_tensor(np.eye(128, dtype=BF), "il_id_bf").ap()
    il_id_f = nc.inline_tensor(np.eye(128, dtype=np.float32), "il_id_f").ap()
    il_ones_r_f = nc.inline_tensor(np.ones((1, 128), np.float32), "il_ones_r_f").ap()
    il_ones_c_f = nc.inline_tensor(np.ones((128, 1), np.float32), "il_ones_c_f").ap()
    il_ar1_r = nc.inline_tensor(np.arange(1, 257, dtype=np.float32)[None, :], "il_ar1_r").ap()
    il_ar0_c = nc.inline_tensor(np.arange(0, 128, dtype=np.float32)[:, None], "il_ar0_c").ap()
    ltm_np = np.zeros((1, 64), np.float32)
    for e_ in range(8):
        for j_ in range(8):
            if j_ < e_:
                ltm_np[0, e_ * 8 + j_] = 1.0
    il_ltm = nc.inline_tensor(ltm_np, "il_ltm").ap()
    sel_np = np.zeros((8, NEXP * 128), BF)
    for n_ in range(NEXP):
        sel_np[n_, n_ * 128:(n_ + 1) * 128] = 1.0
    il_sel = nc.inline_tensor(sel_np, "il_sel").ap()

    id_bf = Et(nc.sbuf_tensor("c_id_bf", [128, 128], BF16))
    id_f = Et(nc.sbuf_tensor("c_id_f", [128, 128], F32))
    ones_r_f = Et(nc.sbuf_tensor("c_ones_r_f", [1, 128], F32))
    ones_c_f = Et(nc.sbuf_tensor("c_ones_c_f", [128, 1], F32))
    ar1_r = Et(nc.sbuf_tensor("c_ar1_r", [1, 256], F32))
    ar0_c = Et(nc.sbuf_tensor("c_ar0_c", [128, 1], F32))
    ltm_t = Et(nc.sbuf_tensor("c_ltm", [128, 64], F32))
    sel_t = Et(nc.sbuf_tensor("c_sel", [8, NEXP * 128], BF16))
    slp_r = Et(nc.sbuf_tensor("c_slp_r", [1, HQ], F32))
    sigb = Et(nc.sbuf_tensor("c_sigb", [128, HQ], F32))
    lamb = Et(nc.sbuf_tensor("c_lamb", [128, HQ], F32))
    kdc = Et(nc.sbuf_tensor("c_kdc", [128, 2 * HQ], F32))
    ar1_bc = Et(nc.sbuf_tensor("c_ar1_bc", [128, 256], F32))
    dc_t = Et(nc.sbuf_tensor("c_dc", [128, 512], F32))
    msk_t = Et(nc.sbuf_tensor("c_msk", [128, 512], F32))
    eps5 = Et(nc.sbuf_tensor("c_eps5", [128, 1], F32))
    eps6 = Et(nc.sbuf_tensor("c_eps6", [128, 1], F32))
    bias_cols = Et(nc.sbuf_tensor("c_bias_cols", [128, 232], F32))
    bdn_bf = Et(nc.sbuf_tensor("c_bdn_bf", [NEXP, D], BF16))
    bg_bc = Et(nc.sbuf_tensor("c_bg_bc", [128, 2 * NE], F32))
    wg_sb = Et(nc.sbuf_tensor("c_wg_sb", [128, 2 * 16 * NE], BF16))
    bgrow = Et(nc.sbuf_tensor("c_bgrow", [1, 2 * NE], F32))
    BUP0, BSW0, BDN0, BSU0, BSA0, BSD0 = 0, 48, 96, 192, 208, 216

    qS = SemC(Et(nc.semaphore("sem_qS")))
    qP = SemC(Et(nc.semaphore("sem_qP")))
    qG = SemC(Et(nc.semaphore("sem_qG")))
    qY = SemC(Et(nc.semaphore("sem_qY")))
    eT = SemC(Et(nc.semaphore("sem_eT")))
    eV = SemC(Et(nc.semaphore("sem_eV")))
    eS = SemC(Et(nc.semaphore("sem_eS")))

    rec = {k: [] for k in KEYS}

    def _emit(key, f):
        rec[key].append(f)

    def T(f):
        _emit("tensor", f)

    def V(f):
        _emit("vector", f)

    def S(f):
        _emit("scalar", f)

    EMIT = {k: (lambda f, kk=k: _emit(kk, f)) for k in KEYS}

    def wt(key, sem, tgt):
        if tgt is None or tgt <= 0:
            return
        EMIT[key](lambda e, h=sem.h, v=tgt: e.wait_ge(h, v))

    def dma(key, out_ap, in_ap, sem, **kw):
        tgt = sem.bump(16)
        EMIT[key](lambda e, o=out_ap, i=in_ap, h=sem.h, k=dict(kw):
                  e.dma_start(out=o, in_=i, **k).then_inc(h, 16))
        return tgt

    def tnop_inc():
        f = rec["tensor"].pop()
        rec["tensor"].append(lambda e, f=f: f(e).then_inc(eT.h, 1))
        return eT.bump()

    def vinc_last():
        f = rec["vector"].pop()
        rec["vector"].append(lambda e, f=f: f(e).then_inc(eV.h, 1))
        return eV.bump()

    def flush(tag):
        nonlocal rec
        with nc.Block(tag) as blk:
            for key in KEYS:
                lst = rec[key]
                if lst:
                    getattr(blk, key)(lambda eng, L=tuple(lst): [f(eng) for f in L])
        rec = {k: [] for k in KEYS}

    nc._dbg_sems = {"qS": qS, "qP": qP, "qG": qG, "qY": qY, "eT": eT, "eV": eV, "eS": eS}

    def bar(tag):
        wt("sync", qS, qS.v)
        wt("scalar", qP, qP.v)
        wt("gpsimd", qG, qG.v)
        wt("gpsimd", qY, qY.v)
        try:
            wt("gpsimd", qC, qC.v)
        except NameError:
            pass
        flush(tag)
        nc.all_engine_barrier()

    def finish():
        if dbg_outs:
            for name, out_ap in dbg_outs.items():
                t = scratch_map[name]
                dma("gpsimd", out_ap, t, qY)
            wt("gpsimd", qY, qY.v)
            flush("dbg")
            nc.all_engine_barrier()
        wt("gpsimd", qY, qY.v)
        flush("fin")

    dma("sync", id_bf[:, :], il_id_bf, qS)
    dma("sync", id_f[:, :], il_id_f, qS)
    dma("sync", ones_r_f[:, :], il_ones_r_f, qS)
    dma("sync", ones_c_f[:, :], il_ones_c_f, qS)
    dma("sync", ar1_r[:, :], il_ar1_r, qS)
    dma("sync", ar0_c[:, :], il_ar0_c, qS)
    dma("sync", ltm_t[:, :], il_ltm.broadcast_to([128, 64]), qS)
    dma("sync", sel_t[:, :], il_sel, qS)
    dma("sync", slp_r[:, :], slope, qS)
    dma("sync", bgrow[:, :], bg_in.rearrange("a b -> (a b)").unsqueeze(0), qS)
    dma("gpsimd", bdn_bf[:, :], bdn_in, qG)
    for j in range(2):
        for c in range(16):
            dma("gpsimd", wg_sb[:, (j * 16 + c) * NE:(j * 16 + c + 1) * NE],
                wg_in[j, c * 128:(c + 1) * 128, :], qG)
    ld_all = qS.v
    wt("vector", qS, ld_all)
    wt("vector", qG, qG.v)
    V(lambda v: v.memset(eps5[:, :], 1e-5))
    V(lambda v: v.memset(eps6[:, :], 1e-6).then_inc(eV.h, 1))
    eV.bump()

    p0 = ExitStack()
    E0 = p0.enter_context
    if stop == "S0":
        bar("S0")
        finish()
        return nc
    psP = E0(nc.psum_tensor("psP0", [128, 512], F32))
    psP2 = E0(nc.psum_tensor("psP0b", [128, 512], F32))

    wt("tensor", qS, ld_all)
    T(lambda t: t.matmul(psP[:, 0:HQ], ones_r_f[0:1, :], slp_r[0:1, :], start=True, stop=True))
    for j in range(2):
        T(lambda t, j=j: t.matmul(psP[:, 32 + j * NE:32 + (j + 1) * NE], ones_r_f[0:1, :],
                                  bgrow[0:1, j * NE:(j + 1) * NE], start=True, stop=True))
    T(lambda t: t.matmul(psP2[:, 0:256], ones_r_f[0:1, :], ar1_r[0:1, :], start=True, stop=True))
    t_p0 = tnop_inc()
    wt("vector", eT, t_p0)
    V(lambda v: v.tensor_copy(sigb[:, :], psP[:, 0:HQ]))
    V(lambda v: v.tensor_copy(bg_bc[:, :], psP[:, 32:32 + 2 * NE]))
    V(lambda v: v.tensor_copy(ar1_bc[:, :], psP2[:, 0:256]).then_inc(eV.h, 1))
    v_p0 = eV.bump()
    wt("vector", eV, v_p0)
    V(lambda v: v.tensor_scalar(lamb[:, :], sigb[:, :], -256.0, None, ALU.mult))
    for h in range(HQ):
        for par in range(2):
            col = kdc[:, h * 2 + par: h * 2 + par + 1]
            V(lambda v, c=col, p=par: v.tensor_scalar(c, ar0_c[:, :], -1.0, float(255 - p * 128), ALU.mult, ALU.add))
    v_p1 = vinc_last()
    wt("vector", eV, v_p1)
    for h in range(HQ):
        for par in range(2):
            col = kdc[:, h * 2 + par: h * 2 + par + 1]
            V(lambda v, c=col, hh=h: v.tensor_tensor(c, c, sigb[:, hh:hh + 1], ALU.mult))
    for par in range(2):
        sl = slice(par * 256, (par + 1) * 256)
        V(lambda v, sl=sl, p=par: v.tensor_scalar(dc_t[:, sl], ar1_bc[:, :], ar0_c[:, 0:1],
                                                  float(-1.0 - p * 128), ALU.subtract, ALU.add))
    v_p2 = vinc_last()
    wt("vector", eV, v_p2)
    for par in range(2):
        sl = slice(par * 256, (par + 1) * 256)
        V(lambda v, sl=sl: v.tensor_scalar(msk_t[:, sl], dc_t[:, sl], 0.0, None, ALU.is_ge))
    v_p3 = vinc_last()
    wt("vector", eV, v_p3)
    for par in range(2):
        sl = slice(par * 256, (par + 1) * 256)
        V(lambda v, sl=sl: v.tensor_scalar_max(dc_t[:, sl], dc_t[:, sl], 0.0))
    v_p4 = vinc_last()
    wt("scalar", eV, v_p4)
    S(lambda s: s.activation(lamb[:, :], lamb[:, :], AF.Exp))
    S(lambda s: s.activation(kdc[:, :], kdc[:, :], AF.Exp, scale=-1.0).then_inc(eS.h, 1))
    eS.bump()

    if stop == "S2":
        bar("S2")
        finish()
        return nc
    bstage = E0(nc.sbuf_tensor("c_bstage", [1, 12288], F32))
    bias_jobs = [
        (bup_in.rearrange("n e -> (n e)").unsqueeze(0), 48, BUP0),
        (bsw_in.rearrange("n e -> (n e)").unsqueeze(0), 48, BSW0),
        (bdn_in.rearrange("n d -> (n d)").unsqueeze(0), 96, BDN0),
        (bsu_in, 16, BSU0),
        (bsa_in, 8, BSA0),
        (bsd_in, 16, BSD0),
    ]
    _kb = _os0.environ.get("KBIAS")
    if _kb is not None:
        bias_jobs = bias_jobs[: int(_kb)]
    wt("tensor", eV, v_p0)
    _bm = _os0.environ.get("BIASMODE", "")
    for bsrc, ncols, base in bias_jobs:
        wt("sync", eT, eT.v)
        if _bm != "mmonly":
            tgt = dma("sync", bstage[:, 0:ncols * 128], bsrc, qS)
            wt("tensor", qS, tgt)
        off0 = 0
        k = 0
        while k < ncols:
            if _bm == "dmaonly":
                break
            n4 = min(4, ncols - k)
            for i in range(n4):
                T(lambda t, o=off0 + (k + i) * 128, g=i:
                  t.matmul(psP2[:, g:g + 1], bstage[0:1, o:o + 128], ones_c_f[0:1, 0:1],
                           start=True, stop=True))
            tv = tnop_inc()
            wt("vector", eT, tv)
            V(lambda v, b=base + k, n=n4: v.tensor_copy(
                bias_cols[:, b:b + n], psP2[:, 0:n]).then_inc(eV.h, 1))
            tvv = eV.bump()
            wt("tensor", eV, tvv)
            k += n4

    if stop == "S3":
        bar("S3")
        finish()
        return nc
    xt0 = E0(nc.sbuf_tensor("a0_x", [128, 2 * D], F32))
    xsq = E0(nc.sbuf_tensor("a0_sq", [128, 2 * D], F32))
    lstg = E0(nc.sbuf_tensor("a0_stg", [128, 2 * D], BF16))
    scol = E0(nc.sbuf_tensor("a0_scol", [128, 8], F32))
    x_used = [None, None]
    st_done = [None, None]
    for i in range(EXT // 128):
        sl = i % 2
        xv = xt0[:, sl * D:(sl + 1) * D]
        sq = xsq[:, sl * D:(sl + 1) * D]
        st = lstg[:, sl * D:(sl + 1) * D]
        c_s1 = scol[:, sl * 4 + 0: sl * 4 + 1]
        c_s2 = scol[:, sl * 4 + 1: sl * 4 + 2]
        c_m = scol[:, sl * 4 + 2: sl * 4 + 3]
        c_v = scol[:, sl * 4 + 3: sl * 4 + 4]
        if x_used[sl] is not None:
            wt("sync", eV, x_used[sl])
        xl = dma("sync", xv, x_ext[i * 128:(i + 1) * 128, :], qS)
        wt("vector", qS, xl)
        V(lambda v, a=xv, b=sq: v.tensor_tensor(b, a, a, ALU.mult).then_inc(eV.h, 1))
        t1 = eV.bump()
        wt("vector", eV, t1)
        V(lambda v, b=sq, c=c_s2: v.reduce_sum(c, b, axis=AX.X))
        V(lambda v, a=xv, c=c_s1: v.reduce_sum(c, a, axis=AX.X).then_inc(eV.h, 1))
        t2 = eV.bump()
        wt("vector", eV, t2)
        V(lambda v, c=c_s1, m=c_m: v.tensor_scalar(m, c, 1.0 / D, None, ALU.mult))
        V(lambda v, c=c_s2, vv=c_v: v.tensor_scalar(vv, c, 1.0 / D, None, ALU.mult).then_inc(eV.h, 1))
        t3 = eV.bump()
        wt("vector", eV, t3)
        V(lambda v, m=c_m, tmp=c_s1: v.tensor_tensor(tmp, m, m, ALU.mult).then_inc(eV.h, 1))
        t4 = eV.bump()
        wt("vector", eV, t4)
        V(lambda v, vv=c_v, tmp=c_s1: v.tensor_tensor(vv, vv, tmp, ALU.subtract).then_inc(eV.h, 1))
        t5 = eV.bump()
        wt("scalar", eV, t5)
        S(lambda s, vv=c_v: s.activation(vv, vv, AF.Ln, bias=eps5[:, 0:1]))
        S(lambda s, vv=c_v: s.activation(vv, vv, AF.Exp, scale=-0.5).then_inc(eS.h, 1))
        t6 = eS.bump()
        if st_done[sl] is not None:
            wt("vector", qP, st_done[sl])
        wt("vector", eS, t6)
        V(lambda v, a=xv, o=st, m=c_m, r=c_v:
          v.tensor_scalar(o, a, m[:, 0:1], r[:, 0:1], ALU.subtract, ALU.mult).then_inc(eV.h, 1))
        x_used[sl] = eV.bump()
        wt("scalar", eV, x_used[sl])
        st_done[sl] = dma("scalar", s_xln[i * 128:(i + 1) * 128, :], st, qP)
    bar("P1_ln1_casts")
    if stop == "P1_ln1_casts":
        finish()
        return nc
    p0.close()

    pa = ExitStack()
    Ea = pa.enter_context
    xfm = Ea(nc.sbuf_tensor("pa_xfm", [128, 16 * MAIN], BF16))
    wsl = Ea(nc.sbuf_tensor("pa_wsl", [128, 2 * 16 * 512], BF16))
    qstg = Ea(nc.sbuf_tensor("pa_stg", [128, 4 * 512], BF16))
    ktst = Ea(nc.sbuf_tensor("pa_ktst", [128, 2 * 512], BF16))
    vst2 = Ea(nc.sbuf_tensor("pa_vst2", [128, 2 * 4 * 512], BF16))
    psA = [Ea(nc.psum_tensor("pa_ps%d" % i, [128, 512], F32)) for i in range(4)]
    psKT = Ea(nc.psum_tensor("pa_pskt", [128, 1024], BF16))

    xfm_tgt = None
    for c in range(16):
        for tt8 in range(8):
            xfm_tgt = dma("sync", xfm[:, c * MAIN + tt8 * 512: c * MAIN + tt8 * 512 + 512],
                          s_xln[LB + tt8 * 512: LB + tt8 * 512 + 512, c * 128:(c + 1) * 128],
                          qS, transpose=True)

    wslot_used = [None, None]

    def emit_fm_slices(slices, xsrc, xstride, first_wait):
        w_used = wslot_used
        ev_done = {}
        stg_free = [None] * 4
        kt_free = [None, None]
        kt_ps = [None, None]
        fw = first_wait
        ntt = xstride // 512
        for si, (wc0, rb, kind, dstt) in enumerate(slices):
            sl = si % 2
            wv = wsl[:, sl * 16 * 512:(sl + 1) * 16 * 512]
            if w_used[sl] is not None:
                wt("gpsimd", eT, w_used[sl])
            wl = None
            for c in range(16):
                wl = dma("gpsimd", wv[:, c * 512:(c + 1) * 512],
                         wqT[c * 128:(c + 1) * 128, wc0:wc0 + 512], qG)
            wt("tensor", qG, wl)
            if fw is not None:
                wt("tensor", qS, fw)
                fw = None
            g = 0
            for m in range(4):
                for tt8 in range(ntt):
                    bank = psA[g % 4]
                    if ev_done.get(g - 4) is not None:
                        wt("tensor", eS, ev_done[g - 4])
                    for c in range(16):
                        T(lambda t, b=bank, w=wv, c=c, m=m, x=xsrc, tt=tt8, stx=xstride:
                          t.matmul(b[:, :], w[:, c * 512 + m * 128: c * 512 + m * 128 + 128],
                                   x[:, c * stx + tt * 512: c * stx + tt * 512 + 512],
                                   start=(c == 0), stop=(c == 15)))
                    mmv = tnop_inc()
                    stg = qstg[:, (g % 4) * 512:(g % 4 + 1) * 512]
                    wt("scalar", eT, mmv)
                    if stg_free[g % 4] is not None:
                        wt("scalar", qP, stg_free[g % 4])
                    if kind == "k":
                        if ev_done.get(g - 4) is not None:
                            wt("scalar", eT, kt_last_T[0] if kt_last_T[0] else 0)
                    fn = AF.Silu if kind in ("q", "k") else AF.Copy
                    S(lambda s, o=stg, b=bank, f=fn: s.activation(o, b[:, :], f).then_inc(eS.h, 1))
                    ev_done[g] = eS.bump()
                    wt("scalar", eS, ev_done[g])
                    stg_free[g % 4] = dma("scalar",
                                          dstt[rb + m * 128: rb + m * 128 + 128, tt8 * 512:(tt8 + 1) * 512],
                                          stg, qP)
                    if kind == "k":
                        hh = (rb + m * 128) // 128
                        ksl = g % 2
                        wt("tensor", eS, ev_done[g])
                        if kt_ps[ksl] is not None:
                            wt("tensor", eV, kt_ps[ksl])
                        for i4 in range(4):
                            T(lambda t, i4=i4, stg=stg, ksl=ksl:
                              t.transpose(psKT[:, ksl * 512 + i4 * 128: ksl * 512 + (i4 + 1) * 128],
                                          stg[:, i4 * 128:(i4 + 1) * 128], id_bf[:, :]))
                        trv = tnop_inc()
                        kt_last_T[0] = trv
                        wt("vector", eT, trv)
                        if kt_free[ksl] is not None:
                            wt("vector", qP, kt_free[ksl])
                        for i4 in range(4):
                            t2g = (LB // 128) + tt8 * 4 + i4
                            par = t2g % 2
                            V(lambda v, i4=i4, ksl=ksl, hh=hh, par=par:
                              v.tensor_scalar(ktst[:, ksl * 512 + i4 * 128: ksl * 512 + (i4 + 1) * 128],
                                              psKT[:, ksl * 512 + i4 * 128: ksl * 512 + (i4 + 1) * 128],
                                              kdc[:, hh * 2 + par: hh * 2 + par + 1], None, ALU.mult))
                        kt_ps[ksl] = vinc_last()
                        wt("scalar", eV, kt_ps[ksl])
                        kt_free[ksl] = dma("scalar",
                                           s_ktm[hh, :, (LB // 128) + tt8 * 4: (LB // 128) + tt8 * 4 + 4, :],
                                           ktst[:, ksl * 512:(ksl + 1) * 512].rearrange("p (t e) -> p t e", t=4),
                                           qP)
                    g += 1
            w_used[sl] = eT.v

    kt_last_T = [None]
    fm_slices = []
    for s4 in range(4):
        fm_slices.append((s4 * 512, s4 * 512, "q", s_qfm))
    for s4 in range(4):
        fm_slices.append((D + s4 * 512, s4 * 512, "k", s_kfm))
    for s4 in range(4):
        fm_slices.append((3 * D + s4 * 512, s4 * 512, "u", s_ufm))
    emit_fm_slices(fm_slices, xfm, MAIN, xfm_tgt)

    def emit_roleswap(xsrc, xstride, t2base, vcol0, dst, do_silu_kd, first_wait):
        w_used = wslot_used
        psum_guard = {}
        bfree = [None, None]
        fw = first_wait
        ntm = (LB if xstride == LB else MAIN) // 128
        for vs in range(4):
            sl = vs % 2
            wv = wsl[:, sl * 16 * 512:(sl + 1) * 16 * 512]
            if w_used[sl] is not None:
                wt("gpsimd", eT, w_used[sl])
            wl = None
            for c in range(16):
                wl = dma("gpsimd", wv[:, c * 512:(c + 1) * 512],
                         wqT[c * 128:(c + 1) * 128, vcol0 + vs * 512: vcol0 + vs * 512 + 512], qG)
            wt("tensor", qG, wl)
            if fw is not None:
                wt("tensor", qS, fw)
            g = 0
            for tm in range(ntm):
                bank = psA[g % 4]
                if psum_guard.get(g - 4) is not None:
                    wt("tensor", psum_guard[g - 4][0], psum_guard[g - 4][1])
                for c in range(16):
                    T(lambda t, b=bank, w=wv, c=c, tm=tm, x=xsrc, stx=xstride:
                      t.matmul(b[:, :], x[:, c * stx + tm * 128: c * stx + tm * 128 + 128],
                               w[:, c * 512:(c + 1) * 512], start=(c == 0), stop=(c == 15)))
                mmv = tnop_inc()
                bsl = (tm // 4) % 2
                vbuf = vst2[:, bsl * 2048 + (tm % 4) * 512: bsl * 2048 + (tm % 4) * 512 + 512]
                wt("scalar", eT, mmv)
                if tm % 4 == 0 and bfree[bsl] is not None:
                    wt("scalar", qP, bfree[bsl])
                fn = AF.Silu if do_silu_kd else AF.Copy
                S(lambda s, o=vbuf, b=bank, f=fn: s.activation(o, b[:, :], f).then_inc(eS.h, 1))
                sv = eS.bump()
                if do_silu_kd:
                    wt("vector", eS, sv)
                    par = tm % 2
                    for hh4 in range(4):
                        V(lambda v, o=vbuf, hh=vs * 4 + hh4, par=par, h4=hh4:
                          v.tensor_scalar(o[:, h4 * 128:(h4 + 1) * 128], o[:, h4 * 128:(h4 + 1) * 128],
                                          kdc[:, hh * 2 + par: hh * 2 + par + 1], None, ALU.mult))
                    done = vinc_last()
                    psum_guard[g] = (eV, done)
                else:
                    done = sv
                    psum_guard[g] = (eS, done)
                if tm % 4 == 3:
                    gsem, dval = psum_guard[g]
                    wt("scalar", gsem, dval)
                    last = None
                    for hh4 in range(4):
                        last = dma("scalar", dst[vs * 4 + hh4, :, t2base + tm - 3: t2base + tm + 1, :],
                                   vst2[:, bsl * 2048: bsl * 2048 + 2048]
                                   .rearrange("p (t e) -> p t e", t=4)[:, :, hh4 * 128:(hh4 + 1) * 128], qP)
                    bfree[bsl] = last
                g += 1
            w_used[sl] = eT.v

    emit_roleswap(xfm, MAIN, LB // 128, 2 * D, s_vtm, False, None)
    bar("PA_main")
    if stop == "PA_main":
        finish()
        return nc

    xlb_tgt = None
    for c in range(16):
        for tt4 in range(4):
            xlb_tgt = dma("sync", xfm[:, c * LB + tt4 * 512: c * LB + tt4 * 512 + 512],
                          s_xln[tt4 * 512: tt4 * 512 + 512, c * 128:(c + 1) * 128],
                          qS, transpose=True)
    emit_roleswap(xfm, LB, 0, D, s_ktm, True, xlb_tgt)
    emit_roleswap(xfm, LB, 0, 2 * D, s_vtm, False, None)
    bar("PA_lb")
    if stop == "PA_lb":
        finish()
        return nc

    pa.close()
    pc = ExitStack()
    Ec = pc.enter_context
    qh_b = Ec(nc.sbuf_tensor("pb_qh", [128, 2 * MAIN], BF16))
    kfh_b = Ec(nc.sbuf_tensor("pb_kfh", [128, 2 * MAIN], BF16))
    kth_b = Ec(nc.sbuf_tensor("pb_kth", [128, 2 * 48 * 128], BF16))
    vth_b = Ec(nc.sbuf_tensor("pb_vth", [128, 2 * 48 * 128], BF16))
    oh_b = Ec(nc.sbuf_tensor("pb_oh", [128, 2 * MAIN], F32))
    qd_b = Ec(nc.sbuf_tensor("pb_qd", [128, 2 * 256], F32))
    dd_b = Ec(nc.sbuf_tensor("pb_dd", [128, 2 * 512], BF16))
    tmpd = Ec(nc.sbuf_tensor("pb_tmpd", [128, 512], F32))
    qs_b = Ec(nc.sbuf_tensor("pb_qs", [128, 2 * 256], BF16))
    sc_b = Ec(nc.sbuf_tensor("pb_sc", [128, 2 * 512], BF16))
    S_b = Ec(nc.sbuf_tensor("pb_S", [128, 128], F32))
    Sbf_b = Ec(nc.sbuf_tensor("pb_Sbf", [128, 2 * 128], BF16))
    psSC = [Ec(nc.psum_tensor("pb_psc%d" % i, [128, 512], F32)) for i in range(2)]
    psO = [Ec(nc.psum_tensor("pb_pso%d" % i, [128, 512], F32)) for i in range(2)]
    psAA = [Ec(nc.psum_tensor("pb_psa%d" % i, [128, 512], F32)) for i in range(2)]

    head_T_done = [None] * (HQ + 2)
    head_V_done = [None] * (HQ + 2)
    head_S_done = [None] * (HQ + 2)
    head_spill = [None] * (HQ + 2)
    head_load = [None] * HQ

    def emit_head_load(h):
        hs = h % 2
        if h >= 2:
            wt("sync", eT, head_T_done[h - 2])
            wt("sync", eV, head_V_done[h - 2])
        t1 = dma("sync", qh_b[:, hs * MAIN:(hs + 1) * MAIN], s_qfm[h * 128:(h + 1) * 128, :], qS)
        t1 = dma("sync", kfh_b[:, hs * MAIN:(hs + 1) * MAIN], s_kfm[h * 128:(h + 1) * 128, :], qS)
        t1 = dma("sync", kth_b[:, hs * 48 * 128:(hs + 1) * 48 * 128],
                 s_ktm[h].rearrange("p t e -> p (t e)"), qS)
        t1 = dma("sync", vth_b[:, hs * 48 * 128:(hs + 1) * 48 * 128],
                 s_vtm[h].rearrange("p t e -> p (t e)"), qS)
        head_load[h] = t1

    emit_head_load(0)
    emit_head_load(1)

    cst = Ec(nc.sbuf_tensor("p1_cst", [128, 8 * 2048], BF16))
    qC = SemC(Ec(nc.semaphore("sem_qC")))
    cast_jobs = []

    def add_cast(csrc, cdst, rows, cols):
        step = 2 if cols <= 1024 else 1
        for r0 in range(0, rows, 128 * step):
            nchunk = min(step, (rows - r0) // 128)
            sap = csrc[r0:r0 + nchunk * 128, :].rearrange("(c p) e -> p c e", p=128) \
                if nchunk > 1 else csrc[r0:r0 + 128, :]
            dap = cdst[r0:r0 + nchunk * 128, :].rearrange("(c p) e -> p c e", p=128) \
                if nchunk > 1 else cdst[r0:r0 + 128, :]
            cast_jobs.append((sap, dap, nchunk * cols, nchunk, cols))

    for n in range(NEXP):
        add_cast(wup_in[n], z_up[n], D, E)
        add_cast(wsw_in[n], z_sw[n], E, E)
        for oc in range(2):
            add_cast(wdn_in[n][:, oc * E:(oc + 1) * E], z_dn[n, oc], E, E)
    add_cast(wsu_in, z_su, D, SED)
    add_cast(wsa_in, z_sa, E, E)
    for oc in range(2):
        add_cast(wsd_in[:, oc * E:(oc + 1) * E], z_sd[oc], SED, E)

    NSL = 8
    cload = [None] * len(cast_jobs)
    cstore = [None] * len(cast_jobs)
    def cst_view(sl, nchunk, cols):
        v = cst[:, sl * 2048: sl * 2048 + nchunk * cols]
        return v.rearrange("p (c e) -> p c e", c=nchunk) if nchunk > 1 else v

    for i in range(min(NSL, len(cast_jobs))):
        sl = i % NSL
        cload[i] = dma("gpsimd", cst_view(sl, cast_jobs[i][3], cast_jobs[i][4]), cast_jobs[i][0], qG)
    for i, (csrc, cdst, ncols, nchunk, cols) in enumerate(cast_jobs):
        sl = i % NSL
        wt("gpsimd", qG, cload[i])
        cstore[i] = dma("gpsimd", cdst, cst_view(sl, nchunk, cols), qC)
        j = i + NSL
        if j < len(cast_jobs):
            wt("gpsimd", qC, cstore[i])
            cload[j] = dma("gpsimd", cst_view(j % NSL, cast_jobs[j][3], cast_jobs[j][4]), cast_jobs[j][0], qG)
    cast_done_tgt = ("qC", qC.v)

    if stop == "S4":
        bar("S4")
        finish()
        return nc


    for h in range(HQ):
        hs = h % 2
        qh = qh_b[:, hs * MAIN:(hs + 1) * MAIN]
        kfh = kfh_b[:, hs * MAIN:(hs + 1) * MAIN]
        kth = kth_b[:, hs * 48 * 128:(hs + 1) * 48 * 128]
        vth = vth_b[:, hs * 48 * 128:(hs + 1) * 48 * 128]
        oh = oh_b[:, hs * MAIN:(hs + 1) * MAIN]
        qd = qd_b[:, hs * 256:(hs + 1) * 256]
        dd = dd_b[:, hs * 512:(hs + 1) * 512]

        if h >= 1:
            wt("vector", eT, head_T_done[h - 1])
            wt("vector", eS, head_S_done[h - 1])
            wt("tensor", eV, head_V_done[h - 1])
            wt("tensor", eS, head_S_done[h - 1])
        if h >= 2:
            wt("scalar", qP, head_spill[h - 2])
        V(lambda v, qd=qd, h=h: v.tensor_scalar(qd, ar1_bc[:, :], sigb[:, h:h + 1], None, ALU.mult))
        V(lambda v, h=h: v.tensor_scalar(tmpd[:, :], dc_t[:, :], sigb[:, h:h + 1], None, ALU.mult).then_inc(eV.h, 1))
        tgen = eV.bump()
        wt("scalar", eV, tgen)
        S(lambda s, qd=qd: s.activation(qd, qd, AF.Exp, scale=-1.0))
        S(lambda s: s.activation(tmpd[:, :], tmpd[:, :], AF.Exp, scale=-1.0).then_inc(eS.h, 1))
        tgen2 = eS.bump()
        wt("vector", eS, tgen2)
        V(lambda v, dd=dd: v.tensor_tensor(dd, tmpd[:, :], msk_t[:, :], ALU.mult))
        V(lambda v: v.memset(S_b[:, :], 0.0).then_inc(eV.h, 1))
        dd_done = eV.bump()

        wt("tensor", qS, head_load[h])
        wt("vector", qS, head_load[h])

        sc_done = {}
        mask_done = {}
        a_done = {}
        o_done = {}
        oevac_done = {}
        sbf_done = {}
        qs_done = {}
        upd_hist = {-1: dd_done}

        for t in range(NBE):
            par = t % 2
            lb = t < NBL
            if upd_hist.get(t - 2) is not None:
                wt("tensor", eV, upd_hist[t - 2])
            for i2 in range(2):
                T(lambda t_, par=par, kth=kth, vth=vth, t2=t * 2 + i2, i2=i2:
                  t_.matmul(psAA[par][:, 0:128], kth[:, t2 * 128:(t2 + 1) * 128],
                            vth[:, t2 * 128:(t2 + 1) * 128], start=(i2 == 0), stop=(i2 == 1)))
            a_done[t] = tnop_inc()

            if not lb:
                tm = t - NBL
                wt("vector", eS, tgen2)
                if o_done.get(t - 2) is not None:
                    wt("vector", eT, o_done[t - 2])
                V(lambda v, par=par, qh=qh, tm=tm, qd=qd:
                  v.tensor_tensor(qs_b[:, par * 256:(par + 1) * 256], qd,
                                  qh[:, tm * 256:(tm + 1) * 256], ALU.mult).then_inc(eV.h, 1))
                qs_done[t] = eV.bump()
                if mask_done.get(t - 2) is not None:
                    wt("tensor", eV, mask_done[t - 2])
                for i2 in range(2):
                    T(lambda t_, par=par, kfh=kfh, qh=qh, tm=tm, i2=i2:
                      t_.matmul(psSC[par][:, i2 * 256:(i2 + 1) * 256],
                                kfh[:, tm * 256 + i2 * 128: tm * 256 + i2 * 128 + 128],
                                qh[:, tm * 256:(tm + 1) * 256], start=True, stop=True))
                sc_done[t] = tnop_inc()
                wt("vector", eT, sc_done[t])
                if o_done.get(t - 2) is not None:
                    wt("vector", eT, o_done[t - 2])
                for i2 in range(2):
                    V(lambda v, par=par, i2=i2, dd=dd:
                      v.tensor_tensor(sc_b[:, par * 512 + i2 * 256: par * 512 + (i2 + 1) * 256],
                                      psSC[par][:, i2 * 256:(i2 + 1) * 256],
                                      dd[:, i2 * 256:(i2 + 1) * 256], ALU.mult))
                mask_done[t] = vinc_last()
                wt("vector", eV, upd_hist[t - 1])
                if o_done.get(t - 2) is not None:
                    wt("vector", eT, o_done[t - 2])
                V(lambda v, par=par: v.tensor_copy(Sbf_b[:, par * 128:(par + 1) * 128], S_b[:, :]).then_inc(eV.h, 1))
                sbf_done[t] = eV.bump()
                wt("tensor", eV, mask_done[t])
                wt("tensor", eV, sbf_done[t])
                wt("tensor", eV, qs_done[t])
                if oevac_done.get(t - 2) is not None:
                    wt("tensor", eS, oevac_done[t - 2])
                for i2 in range(2):
                    T(lambda t_, par=par, vth=vth, t2=t * 2 + i2, i2=i2:
                      t_.matmul(psO[par][:, 0:256], vth[:, t2 * 128:(t2 + 1) * 128],
                                sc_b[:, par * 512 + i2 * 256: par * 512 + (i2 + 1) * 256],
                                start=(i2 == 0), stop=False))
                T(lambda t_, par=par: t_.matmul(psO[par][:, 0:256], Sbf_b[:, par * 128:(par + 1) * 128],
                                                qs_b[:, par * 256:(par + 1) * 256], start=False, stop=True))
                o_done[t] = tnop_inc()
                wt("scalar", eT, o_done[t])
                S(lambda s, par=par, oh=oh, tm=tm:
                  s.activation(oh[:, tm * 256:(tm + 1) * 256], psO[par][:, 0:256], AF.Copy).then_inc(eS.h, 1))
                oevac_done[t] = eS.bump()

            wt("vector", eT, a_done[t])
            if not lb:
                wt("vector", eV, sbf_done[t])
            V(lambda v, par=par, h=h:
              v.scalar_tensor_tensor(S_b[:, :], S_b[:, :], lamb[:, h:h + 1], psAA[par][:, 0:128],
                                     ALU.mult, ALU.add).then_inc(eV.h, 1))
            upd_hist[t] = eV.bump()

        head_T_done[h] = eT.v
        head_V_done[h] = eV.v
        head_S_done[h] = eS.v
        wt("scalar", eS, oevac_done[NBE - 1])
        head_spill[h] = dma("scalar", s_ofm[h * 128:(h + 1) * 128, :], oh, qP)
        if h + 2 < HQ:
            emit_head_load(h + 2)
    bar("PB_scan")
    if stop == "PB_scan":
        finish()
        return nc

    pc.close()
    pd = ExitStack()
    Ed = pd.enter_context
    wo_sb = Ed(nc.sbuf_tensor("pc_wo", [128, 16 * D], BF16))
    o_sb = Ed(nc.sbuf_tensor("pc_o", [128, 16 * 512], F32))
    u_sb = Ed(nc.sbuf_tensor("pc_u", [128, 16 * 512], BF16))
    oum = Ed(nc.sbuf_tensor("pc_oum", [128, 16 * 512], BF16))
    sq_c = Ed(nc.sbuf_tensor("pc_sq", [128, 2 * 512], F32))
    xm_c = Ed(nc.sbuf_tensor("pc_xm", [128, 2 * D], F32))
    xmid = Ed(nc.sbuf_tensor("pc_xmid", [128, 2 * D], F32))
    rrow = Ed(nc.sbuf_tensor("pc_rrow", [1, 512], F32))
    l2c = Ed(nc.sbuf_tensor("pc_l2c", [128, 4], F32))
    xl2f = Ed(nc.sbuf_tensor("pc_xl2f", [128, D], F32))
    xl2s = Ed(nc.sbuf_tensor("pc_xl2s", [128, 2 * D], BF16))
    psW = [Ed(nc.psum_tensor("pc_psw%d" % i, [128, 512], F32)) for i in range(4)]
    psQ = Ed(nc.psum_tensor("pc_psq", [128, 512], F32))
    psB2 = Ed(nc.psum_tensor("pc_psb", [128, 512], F32))

    wo_tgt = None
    for c in range(16):
        wo_tgt = dma("gpsimd", wo_sb[:, c * D:(c + 1) * D], woT[c * 128:(c + 1) * 128, :], qG)

    ssqmm = {}
    prev_xm_done = [None]
    oum_used = None
    xm_used = [None, None]
    xmid_used = [None, None]
    stg_used = [None, None]
    for mt in range(NMT):
        wt("vector", eT, eT.v)
        if oum_used is not None:
            wt("sync", eV, oum_used)
        lo = None
        for c in range(16):
            lo = dma("sync", o_sb[:, c * 512:(c + 1) * 512],
                     s_ofm[c * 128:(c + 1) * 128, mt * 512:(mt + 1) * 512], qS)
            lo = dma("sync", u_sb[:, c * 512:(c + 1) * 512],
                     s_ufm[c * 128:(c + 1) * 128, mt * 512:(mt + 1) * 512], qS)
        wt("vector", qS, lo)
        for c in range(16):
            ssl = c % 2
            if ssqmm.get((mt, c - 2)) is not None:
                wt("vector", eT, ssqmm[(mt, c - 2)])
            V(lambda v, c=c, ssl=ssl: v.tensor_tensor(sq_c[:, ssl * 512:(ssl + 1) * 512],
                                                      o_sb[:, c * 512:(c + 1) * 512],
                                                      o_sb[:, c * 512:(c + 1) * 512], ALU.mult).then_inc(eV.h, 1))
            sqd = eV.bump()
            wt("tensor", eV, sqd)
            T(lambda t, c=c, ssl=ssl: t.matmul(psQ[0:1, :], ones_c_f[:, 0:1],
                                               sq_c[:, ssl * 512:(ssl + 1) * 512],
                                               start=(c == 0), stop=(c == 15)))
            ssqmm[(mt, c)] = tnop_inc()
        wt("scalar", eT, ssqmm[(mt, 15)])
        S(lambda s: s.activation(rrow[0:1, :], psQ[0:1, :], AF.Ln, bias=eps6[0:1, 0:1], scale=1.0 / D))
        S(lambda s: s.activation(rrow[0:1, :], rrow[0:1, :], AF.Exp, scale=-0.5).then_inc(eS.h, 1))
        rs_done = eS.bump()
        wt("tensor", eS, rs_done)
        T(lambda t: t.matmul(psB2[:, :], ones_r_f[0:1, :], rrow[0:1, :], start=True, stop=True))
        bc_done = tnop_inc()
        wt("vector", eT, bc_done)
        for c in range(16):
            V(lambda v, c=c: v.tensor_tensor(o_sb[:, c * 512:(c + 1) * 512],
                                             o_sb[:, c * 512:(c + 1) * 512],
                                             u_sb[:, c * 512:(c + 1) * 512], ALU.mult))
        ou1 = vinc_last()
        wt("vector", eV, ou1)
        for c in range(16):
            V(lambda v, c=c: v.tensor_tensor(oum[:, c * 512:(c + 1) * 512],
                                             o_sb[:, c * 512:(c + 1) * 512],
                                             psB2[:, :], ALU.mult))
        oum_done = vinc_last()
        oum_used = oum_done

        for t4 in range(4):
            xsl = t4 % 2
            glob_t = mt * 512 + t4 * 128
            if xm_used[xsl] is not None:
                wt("gpsimd", eV, xm_used[xsl])
            xl = dma("gpsimd", xm_c[:, xsl * D:(xsl + 1) * D],
                     x_ext[LB + glob_t: LB + glob_t + 128, :], qG)
            wt("tensor", eV, oum_done)
            wt("tensor", qG, wo_tgt)
            if prev_xm_done[0] is not None:
                wt("tensor", eV, prev_xm_done[0])
            for isl in range(4):
                for c in range(16):
                    T(lambda t, b=psW[isl], c=c, t4=t4, isl=isl:
                      t.matmul(b[:, :], oum[:, c * 512 + t4 * 128: c * 512 + t4 * 128 + 128],
                               wo_sb[:, c * D + isl * 512: c * D + isl * 512 + 512],
                               start=(c == 0), stop=(c == 15)))
            wmm = tnop_inc()
            wt("vector", eT, wmm)
            wt("vector", qG, xl)
            if xmid_used[xsl] is not None:
                wt("vector", qP, xmid_used[xsl])
            for isl in range(4):
                V(lambda v, isl=isl, xsl=xsl:
                  v.tensor_tensor(xmid[:, xsl * D + isl * 512: xsl * D + isl * 512 + 512],
                                  psW[isl][:, :], xm_c[:, xsl * D + isl * 512: xsl * D + isl * 512 + 512], ALU.add))
            xm_done = vinc_last()
            prev_xm_done[0] = xm_done
            xm_used[xsl] = xm_done
            xv = xmid[:, xsl * D:(xsl + 1) * D]
            wt("vector", eV, xm_done)
            V(lambda v, xv=xv: v.tensor_tensor(xl2f[:, :], xv, xv, ALU.mult).then_inc(eV.h, 1))
            lt1 = eV.bump()
            wt("vector", eV, lt1)
            V(lambda v: v.reduce_sum(l2c[:, 1:2], xl2f[:, :], axis=AX.X))
            V(lambda v, xv=xv: v.reduce_sum(l2c[:, 0:1], xv, axis=AX.X).then_inc(eV.h, 1))
            lt2 = eV.bump()
            wt("vector", eV, lt2)
            V(lambda v: v.tensor_scalar(l2c[:, 2:3], l2c[:, 0:1], 1.0 / D, None, ALU.mult))
            V(lambda v: v.tensor_scalar(l2c[:, 3:4], l2c[:, 1:2], 1.0 / D, None, ALU.mult).then_inc(eV.h, 1))
            lt3 = eV.bump()
            wt("vector", eV, lt3)
            V(lambda v: v.tensor_tensor(l2c[:, 0:1], l2c[:, 2:3], l2c[:, 2:3], ALU.mult).then_inc(eV.h, 1))
            lt4 = eV.bump()
            wt("vector", eV, lt4)
            V(lambda v: v.tensor_tensor(l2c[:, 3:4], l2c[:, 3:4], l2c[:, 0:1], ALU.subtract).then_inc(eV.h, 1))
            lt5 = eV.bump()
            wt("scalar", eV, lt5)
            S(lambda s: s.activation(l2c[:, 3:4], l2c[:, 3:4], AF.Ln, bias=eps5[:, 0:1]))
            S(lambda s: s.activation(l2c[:, 3:4], l2c[:, 3:4], AF.Exp, scale=-0.5).then_inc(eS.h, 1))
            lt6 = eS.bump()
            wt("vector", eS, lt6)
            V(lambda v, xv=xv: v.tensor_scalar(xl2f[:, :], xv, l2c[:, 2:3], l2c[:, 3:4],
                                               ALU.subtract, ALU.mult).then_inc(eV.h, 1))
            lt7 = eV.bump()
            wt("vector", eV, lt7)
            if stg_used[xsl] is not None:
                wt("vector", qP, stg_used[xsl])
            V(lambda v, xv=xv: v.tensor_tensor(xv, xv, xl2f[:, :], ALU.add))
            V(lambda v, xsl=xsl: v.tensor_copy(xl2s[:, xsl * D:(xsl + 1) * D], xl2f[:, :]).then_inc(eV.h, 1))
            lt8 = eV.bump()
            wt("scalar", eV, lt8)
            dma("scalar", s_xr2[glob_t: glob_t + 128, :], xv, qP)
            stg_used[xsl] = dma("scalar", s_xl2[glob_t: glob_t + 128, :], xl2s[:, xsl * D:(xsl + 1) * D], qP)
            xmid_used[xsl] = stg_used[xsl]
    bar("PC_epilogue")
    if stop == "PC_epilogue":
        finish()
        return nc

    pd.close()
    pe = ExitStack()
    Ee = pe.enter_context
    wsl_d = Ee(nc.sbuf_tensor("pd_wsl", [128, 2 * 8192], BF16))
    h2w_d = Ee(nc.sbuf_tensor("pd_h2w", [128, NEXP * 8 * 512], BF16))
    sha_d = Ee(nc.sbuf_tensor("pd_sha", [128, 16 * 512], BF16))
    u1_d = Ee(nc.sbuf_tensor("pd_u1", [128, 8192], BF16))
    xfm2_d = Ee(nc.sbuf_tensor("pd_xfm2", [128, 16 * 512], BF16))
    xt1_d = Ee(nc.sbuf_tensor("pd_xt1", [128, 16 * 512], BF16))
    wbc_d = Ee(nc.sbuf_tensor("pd_wbc", [128, NEXP * 512], BF16))
    g_d = Ee(nc.sbuf_tensor("pd_g", [128, 104], F32))
    cmp_d = Ee(nc.sbuf_tensor("pd_cmp", [128, 128], F32))
    wtm_d = Ee(nc.sbuf_tensor("pd_wtm", [128, 32], F32))
    wfm_d = Ee(nc.sbuf_tensor("pd_wfm", [8, 512], BF16))
    tmp_d = Ee(nc.sbuf_tensor("pd_tmp", [128, 2 * 512], F32))
    o1s_d = Ee(nc.sbuf_tensor("pd_o1s", [128, 2 * 512], BF16))
    o1tm_d = Ee(nc.sbuf_tensor("pd_o1tm", [128, 2 * D], BF16))
    xr2_d = Ee(nc.sbuf_tensor("pd_xr2", [128, 2 * D], F32))
    yst_d = Ee(nc.sbuf_tensor("pd_yst", [128, D], F32))
    psD = [Ee(nc.psum_tensor("pd_ps%d" % i, [128, 512], F32)) for i in range(8)]

    sh_v = u1_d[:, 0:8192]
    h_v = u1_d[:, 0:4096]
    si_v = u1_d[:, 4096:8192]

    wt("sync", qC, cast_done_tgt[1])

    slab_load = {}
    slab_used = {}
    slab_ctr = [0]

    def slab(pieces):
        i = slab_ctr[0]
        slab_ctr[0] += 1
        sl = i % 2
        view = wsl_d[:, sl * 8192:(sl + 1) * 8192]
        if i >= 2:
            wt("sync", eT, slab_used[i - 2])
        tgt = None
        for (c0, c1), src in pieces:
            tgt = dma("sync", view[:, c0:c1], src, qS)
        slab_load[i] = tgt
        return view, i

    def slab_end(i, tgt=None):
        slab_used[i] = tgt if tgt is not None else tnop_inc()

    def fence_T():
        wt("tensor", eS, eS.v)
        wt("tensor", eV, eV.v)

    y_final = None
    o1tm_used = [None, None]
    o1s_used = [None, None]
    evd = {}
    xt1_ready = None

    for mt in range(NMT):
        osl = mt % 2
        wt("scalar", eT, eT.v)
        wt("scalar", eV, eV.v)
        wt("vector", eT, eT.v)
        xf_tgt = None
        for c in range(16):
            xf_tgt = dma("scalar", xfm2_d[:, c * 512:(c + 1) * 512],
                         s_xl2[mt * 512:(mt + 1) * 512, c * 128:(c + 1) * 128], qP, transpose=True)
        xr_tgt = [None] * 4
        for t4 in range(2):
            xr_tgt[t4] = dma("gpsimd", xr2_d[:, t4 * D:(t4 + 1) * D],
                             s_xr2[mt * 512 + t4 * 128: mt * 512 + (t4 + 1) * 128, :], qG)
        for r in range(2):
            xt = xfm2_d if r == 0 else xt1_d
            if r == 0:
                wt("tensor", qP, xf_tgt)
                wt("vector", qP, xf_tgt)
            else:
                wt("tensor", eV, xt1_ready)
            fence_T()
            for t4 in range(4):
                gv = g_d[:, t4 * 8:(t4 + 1) * 8]
                if evd.get(("g", mt, r, t4 - 1)) is not None:
                    wt("tensor", eV, evd[("g", mt, r, t4 - 1)])
                for c in range(16):
                    T(lambda t, c=c, t4=t4, xt=xt, r=r:
                      t.matmul(psD[6][:, t4 * 8:(t4 + 1) * 8],
                               xt[:, c * 512 + t4 * 128: c * 512 + t4 * 128 + 128],
                               wg_sb[:, (r * 16 + c) * NE:(r * 16 + c + 1) * NE],
                               start=(c == 0), stop=(c == 15)))
                gmm = tnop_inc()
                wt("vector", eT, gmm)
                V(lambda v, gv=gv, t4=t4, r=r: v.tensor_tensor(gv, psD[6][:, t4 * 8:(t4 + 1) * 8],
                                                               bg_bc[:, r * NE:(r + 1) * NE], ALU.add).then_inc(eV.h, 1))
                evd[("g", mt, r, t4)] = eV.bump()
                gj = gv.unsqueeze(1).broadcast_to([128, 8, 8])
                ge = gv.unsqueeze(2).broadcast_to([128, 8, 8])
                rk = g_d[:, 32 + t4 * 8: 40 + t4 * 8]
                mk = g_d[:, 64 + t4 * 8: 72 + t4 * 8]
                ex = wtm_d[:, t4 * 8:(t4 + 1) * 8]
                mx = g_d[:, 96 + t4: 97 + t4]
                sm = g_d[:, 100 + t4: 101 + t4]
                wt("vector", eV, evd[("g", mt, r, t4)])
                V(lambda v, gj=gj, ge=ge: v.tensor_tensor(
                    cmp_d[:, 0:64].rearrange("p (a b) -> p a b", a=8), gj, ge, ALU.is_gt))
                V(lambda v, gj=gj, ge=ge: v.tensor_tensor(
                    cmp_d[:, 64:128].rearrange("p (a b) -> p a b", a=8), gj, ge, ALU.is_equal).then_inc(eV.h, 1))
                c1 = eV.bump()
                wt("vector", eV, c1)
                V(lambda v: v.tensor_tensor(cmp_d[:, 64:128], cmp_d[:, 64:128], ltm_t[:, :], ALU.mult).then_inc(eV.h, 1))
                c2 = eV.bump()
                wt("vector", eV, c2)
                V(lambda v: v.tensor_tensor(cmp_d[:, 0:64], cmp_d[:, 0:64], cmp_d[:, 64:128], ALU.add).then_inc(eV.h, 1))
                c3 = eV.bump()
                wt("vector", eV, c3)
                V(lambda v, rk=rk: v.reduce_sum(rk, cmp_d[:, 0:64].rearrange("p (a b) -> p a b", a=8),
                                                axis=AX.X).then_inc(eV.h, 1))
                c4 = eV.bump()
                wt("vector", eV, c4)
                V(lambda v, rk=rk, mk=mk: v.tensor_scalar(mk, rk, 6.0, None, ALU.is_lt).then_inc(eV.h, 1))
                c5 = eV.bump()
                wt("vector", eV, c5)
                V(lambda v, mk=mk, ex=ex: v.tensor_scalar(ex, mk, 1.0, 1e30, ALU.subtract, ALU.mult).then_inc(eV.h, 1))
                c6 = eV.bump()
                wt("vector", eV, c6)
                V(lambda v, gv=gv, ex=ex: v.scalar_tensor_tensor(ex, gv, 1.0, ex, ALU.mult, ALU.add).then_inc(eV.h, 1))
                c7 = eV.bump()
                wt("vector", eV, c7)
                V(lambda v, ex=ex, mx=mx: v.reduce_max(mx, ex, axis=AX.X).then_inc(eV.h, 1))
                c8 = eV.bump()
                wt("vector", eV, c8)
                V(lambda v, ex=ex, mx=mx: v.tensor_scalar(ex, ex, mx[:, 0:1], None, ALU.subtract).then_inc(eV.h, 1))
                c9 = eV.bump()
                wt("scalar", eV, c9)
                S(lambda s, ex=ex: s.activation(ex, ex, AF.Exp).then_inc(eS.h, 1))
                c10 = eS.bump()
                wt("vector", eS, c10)
                V(lambda v, ex=ex, sm=sm: v.reduce_sum(sm, ex, axis=AX.X).then_inc(eV.h, 1))
                c11 = eV.bump()
                wt("vector", eV, c11)
                V(lambda v, sm=sm: v.reciprocal(sm, sm).then_inc(eV.h, 1))
                c12 = eV.bump()
                wt("vector", eV, c12)
                V(lambda v, ex=ex, sm=sm: v.tensor_scalar(ex, ex, sm[:, 0:1], None, ALU.mult).then_inc(eV.h, 1))
                evd[("w", mt, r, t4)] = eV.bump()
            for mg in range(2):
                vwA, siA = slab([((kc * 1024, (kc + 1) * 1024),
                                  z_su[kc * 128:(kc + 1) * 128, mg * 1024:(mg + 1) * 1024]) for kc in range(8)])
                vwB, siB = slab([((kc * 1024, (kc + 1) * 1024),
                                  z_su[1024 + kc * 128: 1024 + (kc + 1) * 128, mg * 1024:(mg + 1) * 1024]) for kc in range(8)])
                wt("tensor", qS, slab_load[siB])
                for m8 in range(8):
                    m16 = mg * 8 + m8
                    bank = psD[m8 % 4]
                    if evd.get(("su", mt, r, m16 - 4)) is not None:
                        wt("tensor", eS, evd[("su", mt, r, m16 - 4)])
                    for kc in range(8):
                        T(lambda t, b=bank, vw=vwA, kc=kc, m8=m8, xt=xt:
                          t.matmul(b[:, :], vw[:, kc * 1024 + m8 * 128: kc * 1024 + m8 * 128 + 128],
                                   xt[:, kc * 512:(kc + 1) * 512], start=(kc == 0), stop=False))
                    for kc in range(8):
                        T(lambda t, b=bank, vw=vwB, kc=kc, m8=m8, xt=xt:
                          t.matmul(b[:, :], vw[:, kc * 1024 + m8 * 128: kc * 1024 + m8 * 128 + 128],
                                   xt[:, (8 + kc) * 512:(8 + kc + 1) * 512], start=False, stop=(kc == 7)))
                    suv = tnop_inc()
                    wt("scalar", eT, suv)
                    S(lambda s, m16=m16, b=bank: s.activation(sh_v[:, m16 * 512:(m16 + 1) * 512], b[:, :],
                                                              AF.Identity, bias=bias_cols[:, BSU0 + m16: BSU0 + m16 + 1]).then_inc(eS.h, 1))
                    evd[("su", mt, r, m16)] = eS.bump()
                slab_end(siA, eT.v)
                slab_end(siB, eT.v)
            for t4 in range(4):
                wt("tensor", eV, evd[("w", mt, r, t4)])
                T(lambda t, t4=t4: t.transpose(psD[5][0:8, t4 * 128:(t4 + 1) * 128],
                                               wtm_d[:, t4 * 8:(t4 + 1) * 8], id_f[:, :]))
            wtr = tnop_inc()
            wt("vector", eT, wtr)
            V(lambda v: v.tensor_copy(wfm_d[:, :], psD[5][0:8, 0:512]).then_inc(eV.h, 1))
            wfm_done = eV.bump()
            wt("tensor", eV, wfm_done)
            for n in range(NEXP):
                T(lambda t, n=n: t.matmul(psD[7][:, :], sel_t[:, n * 128:(n + 1) * 128],
                                          wfm_d[0:8, :], start=True, stop=True))
                bcv = tnop_inc()
                wt("vector", eT, bcv)
                V(lambda v, n=n: v.tensor_copy(wbc_d[:, n * 512:(n + 1) * 512], psD[7][:, :]).then_inc(eV.h, 1))
                bev = eV.bump()
                wt("tensor", eV, bev)
            vw, si_ = slab([((kc * 1024, (kc + 1) * 1024),
                             z_sa[kc * 128:(kc + 1) * 128, :]) for kc in range(8)])
            wt("tensor", qS, slab_load[si_])
            wt("tensor", eS, evd[("su", mt, r, 15)])
            for sslot in range(2):
                for m8 in range(8):
                    bank = psD[4 + (m8 % 2)]
                    if evd.get(("sa", mt, r, sslot, m8 - 2)) is not None:
                        wt("tensor", eV, evd[("sa", mt, r, sslot, m8 - 2)])
                    elif sslot == 1 and m8 < 2:
                        wt("tensor", eV, evd[("sa", mt, r, 0, 6 + m8)])
                    elif sslot == 0 and m8 < 2:
                        wt("tensor", eV, wfm_done)
                    for kc in range(8):
                        T(lambda t, b=bank, vw=vw, kc=kc, m8=m8, sslot=sslot:
                          t.matmul(b[:, :], vw[:, kc * 1024 + m8 * 128: kc * 1024 + m8 * 128 + 128],
                                   sh_v[:, (sslot * 8 + kc) * 512:(sslot * 8 + kc + 1) * 512],
                                   start=(kc == 0), stop=(kc == 7)))
                    sav = tnop_inc()
                    wt("scalar", eT, sav)
                    S(lambda s, m8=m8, b=bank:
                      s.activation(tmp_d[:, (m8 % 2) * 512:(m8 % 2 + 1) * 512], b[:, :], AF.Silu,
                                   bias=bias_cols[:, BSA0 + m8: BSA0 + m8 + 1]).then_inc(eS.h, 1))
                    sev = eS.bump()
                    wt("vector", eS, sev)
                    V(lambda v, m8=m8, sslot=sslot:
                      v.tensor_tensor(sha_d[:, (sslot * 8 + m8) * 512:(sslot * 8 + m8 + 1) * 512],
                                      tmp_d[:, (m8 % 2) * 512:(m8 % 2 + 1) * 512],
                                      sh_v[:, (sslot * 8 + m8) * 512:(sslot * 8 + m8 + 1) * 512],
                                      ALU.mult).then_inc(eV.h, 1))
                    evd[("sa", mt, r, sslot, m8)] = eV.bump()
            slab_end(si_, eT.v)
            sa_S_end = eS.v
            for n in range(NEXP):
                vwA, siA = slab([((kc * 1024, (kc + 1) * 1024),
                                  z_up[n, kc * 128:(kc + 1) * 128, :]) for kc in range(8)])
                vwB, siB = slab([((kc * 1024, (kc + 1) * 1024),
                                  z_up[n, 1024 + kc * 128: 1024 + (kc + 1) * 128, :]) for kc in range(8)])
                wt("tensor", qS, slab_load[siB])
                if n == 0:
                    wt("tensor", eV, evd[("sa", mt, r, 1, 7)])
                for m8 in range(8):
                    bank = psD[m8 % 4]
                    if evd.get(("up", mt, r, n, m8 - 4)) is not None:
                        wt("tensor", eS, evd[("up", mt, r, n, m8 - 4)])
                    elif n == 0 and m8 < 4:
                        wt("tensor", eS, evd[("su", mt, r, 12 + m8)])
                    elif n > 0 and m8 < 4:
                        wt("tensor", eS, evd[("up", mt, r, n - 1, 4 + m8)])
                    for kc in range(8):
                        T(lambda t, b=bank, vw=vwA, kc=kc, m8=m8, xt=xt:
                          t.matmul(b[:, :], vw[:, kc * 1024 + m8 * 128: kc * 1024 + m8 * 128 + 128],
                                   xt[:, kc * 512:(kc + 1) * 512], start=(kc == 0), stop=False))
                    for kc in range(8):
                        T(lambda t, b=bank, vw=vwB, kc=kc, m8=m8, xt=xt:
                          t.matmul(b[:, :], vw[:, kc * 1024 + m8 * 128: kc * 1024 + m8 * 128 + 128],
                                   xt[:, (8 + kc) * 512:(8 + kc + 1) * 512], start=False, stop=(kc == 7)))
                    upv = tnop_inc()
                    wt("scalar", eT, upv)
                    S(lambda s, m8=m8, n=n: s.activation(h_v[:, m8 * 512:(m8 + 1) * 512], psD[m8 % 4][:, :],
                                                         AF.Identity, bias=bias_cols[:, BUP0 + n * 8 + m8: BUP0 + n * 8 + m8 + 1]).then_inc(eS.h, 1))
                    evd[("up", mt, r, n, m8)] = eS.bump()
                slab_end(siA, eT.v)
                slab_end(siB, eT.v)
                vw, si_ = slab([((kc * 1024, (kc + 1) * 1024),
                                 z_sw[n, kc * 128:(kc + 1) * 128, :]) for kc in range(8)])
                wt("tensor", qS, slab_load[si_])
                wt("tensor", eS, evd[("up", mt, r, n, 7)])
                for m8 in range(8):
                    bank = psD[4 + (m8 % 2)]
                    if evd.get(("sw", mt, r, n, m8 - 2)) is not None:
                        wt("tensor", eV, evd[("sw", mt, r, n, m8 - 2)])
                    elif n == 0 and m8 < 2:
                        wt("tensor", eS, sa_S_end)
                    elif n > 0 and m8 < 2:
                        wt("tensor", eV, evd[("sw", mt, r, n - 1, 6 + m8)])
                    for kc in range(8):
                        T(lambda t, b=bank, vw=vw, kc=kc, m8=m8:
                          t.matmul(b[:, :], vw[:, kc * 1024 + m8 * 128: kc * 1024 + m8 * 128 + 128],
                                   h_v[:, kc * 512:(kc + 1) * 512], start=(kc == 0), stop=(kc == 7)))
                    swv = tnop_inc()
                    wt("scalar", eT, swv)
                    S(lambda s, m8=m8, n=n: s.activation(si_v[:, m8 * 512:(m8 + 1) * 512], psD[4 + (m8 % 2)][:, :],
                                                         AF.Silu, bias=bias_cols[:, BSW0 + n * 8 + m8: BSW0 + n * 8 + m8 + 1]).then_inc(eS.h, 1))
                    sev2 = eS.bump()
                    wt("vector", eS, sev2)
                    V(lambda v, m8=m8: v.tensor_tensor(si_v[:, m8 * 512:(m8 + 1) * 512],
                                                       si_v[:, m8 * 512:(m8 + 1) * 512],
                                                       h_v[:, m8 * 512:(m8 + 1) * 512], ALU.mult).then_inc(eV.h, 1))
                    h2a = eV.bump()
                    wt("vector", eV, h2a)
                    V(lambda v, m8=m8, n=n: v.tensor_tensor(h2w_d[:, (n * 8 + m8) * 512:(n * 8 + m8 + 1) * 512],
                                                            si_v[:, m8 * 512:(m8 + 1) * 512],
                                                            wbc_d[:, n * 512:(n + 1) * 512], ALU.mult).then_inc(eV.h, 1))
                    evd[("sw", mt, r, n, m8)] = eV.bump()
                slab_end(si_, eT.v)
            fence_T()
            tmp_used = {}
            for oc in range(2):
                for n in range(NEXP):
                    vw, si_ = slab([((kc * 1024, (kc + 1) * 1024),
                                     z_dn[n, oc, kc * 128:(kc + 1) * 128, :]) for kc in range(8)])
                    wt("tensor", qS, slab_load[si_])
                    if n == 0 and oc == 1:
                        wt("tensor", eS, evd[("dnS", mt, r, 0)])
                    for ht in range(8):
                        for kc in range(8):
                            T(lambda t, b=psD[ht], vw=vw, kc=kc, ht=ht, n=n:
                              t.matmul(b[:, :], vw[:, kc * 1024 + ht * 128: kc * 1024 + ht * 128 + 128],
                                       h2w_d[:, (n * 8 + kc) * 512:(n * 8 + kc + 1) * 512],
                                       start=(n == 0 and kc == 0), stop=False))
                    slab_end(si_)
                for half in range(2):
                    vw, si_ = slab([((kc * 1024, (kc + 1) * 1024),
                                     z_sd[oc, half * 1024 + kc * 128: half * 1024 + (kc) * 128 + 128, :]) for kc in range(8)])
                    wt("tensor", qS, slab_load[si_])
                    for ht in range(8):
                        for kc in range(8):
                            T(lambda t, b=psD[ht], vw=vw, kc=kc, ht=ht, half=half:
                              t.matmul(b[:, :], vw[:, kc * 1024 + ht * 128: kc * 1024 + ht * 128 + 128],
                                       sha_d[:, (half * 8 + kc) * 512:(half * 8 + kc + 1) * 512],
                                       start=False, stop=False))
                    slab_end(si_)
                for ht in range(8):
                    T(lambda t, ht=ht, oc=oc: t.matmul(psD[ht][:, :],
                                                       bdn_bf[0:6, oc * 1024 + ht * 128: oc * 1024 + ht * 128 + 128],
                                                       wfm_d[0:6, :], start=False, stop=True))
                dn_done = tnop_inc()
                for ht in range(8):
                    hglob = oc * 8 + ht
                    tsl = ht % 2
                    wt("scalar", eT, dn_done)
                    if r == 0:
                        if tmp_used.get(tsl) is not None:
                            wt("scalar", eV, tmp_used[tsl])
                        S(lambda s, ht=ht, hg=hglob, tsl=tsl:
                          s.activation(tmp_d[:, tsl * 512:(tsl + 1) * 512], psD[ht][:, :],
                                       AF.Identity, bias=bias_cols[:, BSD0 + hg: BSD0 + hg + 1]).then_inc(eS.h, 1))
                        sev3 = eS.bump()
                        wt("vector", eS, sev3)
                        V(lambda v, hg=hglob, tsl=tsl: v.tensor_tensor(xt1_d[:, hg * 512:(hg + 1) * 512],
                                                                       tmp_d[:, tsl * 512:(tsl + 1) * 512],
                                                                       xfm2_d[:, hg * 512:(hg + 1) * 512],
                                                                       ALU.add).then_inc(eV.h, 1))
                        tmp_used[tsl] = eV.bump()
                    else:
                        if o1s_used[tsl] is not None:
                            wt("scalar", qP, o1s_used[tsl])
                        S(lambda s, ht=ht, hg=hglob, tsl=tsl:
                          s.activation(o1s_d[:, tsl * 512:(tsl + 1) * 512], psD[ht][:, :],
                                       AF.Identity, bias=bias_cols[:, BSD0 + hg: BSD0 + hg + 1]).then_inc(eS.h, 1))
                        sev3 = eS.bump()
                        wt("scalar", eS, sev3)
                        o1s_used[tsl] = dma("scalar", s_o1[osl, hglob * 128:(hglob + 1) * 128, :],
                                            o1s_d[:, tsl * 512:(tsl + 1) * 512], qP)
                evd[("dnS", mt, r, oc)] = eS.v
            if r == 0:
                xt1_ready = eV.v
        o1_store_done = qP.v
        for t4 in range(4):
            tsl = t4 % 2
            wt("scalar", qP, o1_store_done)
            if o1tm_used[tsl] is not None:
                wt("scalar", eV, o1tm_used[tsl])
            o1t_tgt = None
            for hs4 in range(4):
                o1t_tgt = dma("scalar", o1tm_d[:, tsl * D + hs4 * 512: tsl * D + (hs4 + 1) * 512],
                              s_o1[osl, hs4 * 512:(hs4 + 1) * 512, t4 * 128:(t4 + 1) * 128], qP, transpose=True)
            wt("vector", qP, o1t_tgt)
            if xr_tgt[t4] is not None:
                wt("vector", qG, xr_tgt[t4])
            if y_final is not None:
                wt("vector", qY, y_final)
            V(lambda v, tsl=tsl: v.tensor_tensor(yst_d[:, :], xr2_d[:, tsl * D:(tsl + 1) * D],
                                                 o1tm_d[:, tsl * D:(tsl + 1) * D], ALU.add).then_inc(eV.h, 1))
            yv = eV.bump()
            o1tm_used[tsl] = yv
            wt("gpsimd", eV, yv)
            y_final = dma("gpsimd", y_out[mt * 512 + t4 * 128: mt * 512 + (t4 + 1) * 128, :], yst_d[:, :], qY)
            if t4 < 2:
                nt4 = t4 + 2
                xr_tgt[nt4] = dma("gpsimd", xr2_d[:, (nt4 % 2) * D:(nt4 % 2 + 1) * D],
                                  s_xr2[mt * 512 + nt4 * 128: mt * 512 + (nt4 + 1) * 128, :], qG)
    bar("PD_ffn")
    finish()
    return nc


_CACHE = {}


def _get_prog(dbg=()):
    key = tuple(dbg)
    if key not in _CACHE:
        _CACHE[key] = build(key)
    return _CACHE[key]


def make_in_maps(inputs):
    x = np.asarray(inputs["L0_attn_x"], dtype=np.float32)
    base = {
        "slope": np.ascontiguousarray(np.asarray(inputs["L0_attn_slope_rate"], np.float32).reshape(1, HQ)),
        "wqT": np.ascontiguousarray(np.asarray(inputs["L0_attn_W_qkvu"], np.float32).T),
        "woT": np.ascontiguousarray(np.asarray(inputs["L0_attn_W_out"], np.float32).T),
        "wg_in": np.ascontiguousarray(inputs["L0_ffn_Wg"], np.float32),
        "bg_in": np.ascontiguousarray(inputs["L0_ffn_bg"], np.float32),
        "wup_in": np.ascontiguousarray(inputs["L0_ffn_Wup"], np.float32),
        "bup_in": np.ascontiguousarray(inputs["L0_ffn_bup"], np.float32),
        "wsw_in": np.ascontiguousarray(inputs["L0_ffn_Wsw"], np.float32),
        "bsw_in": np.ascontiguousarray(inputs["L0_ffn_bsw"], np.float32),
        "wdn_in": np.ascontiguousarray(inputs["L0_ffn_Wdn"], np.float32),
        "bdn_in": np.ascontiguousarray(inputs["L0_ffn_bdn"], np.float32),
        "wsu_in": np.ascontiguousarray(inputs["L0_ffn_Wsu"], np.float32),
        "bsu_in": np.ascontiguousarray(np.asarray(inputs["L0_ffn_bsu"], np.float32).reshape(1, SED)),
        "wsa_in": np.ascontiguousarray(inputs["L0_ffn_Wsa"], np.float32),
        "bsa_in": np.ascontiguousarray(np.asarray(inputs["L0_ffn_bsa"], np.float32).reshape(1, E)),
        "wsd_in": np.ascontiguousarray(inputs["L0_ffn_Wsd"], np.float32),
        "bsd_in": np.ascontiguousarray(np.asarray(inputs["L0_ffn_bsd"], np.float32).reshape(1, D)),
    }
    in_maps = []
    for c in range(8):
        b, pos = c // 4, c % 4
        main = x[b, pos * MAIN:(pos + 1) * MAIN]
        if pos == 0:
            lbp = np.zeros((LB, D), np.float32)
        else:
            lbp = x[b, pos * MAIN - LB: pos * MAIN]
        m = dict(base)
        m["x_ext"] = np.ascontiguousarray(np.concatenate([lbp, main], axis=0))
        in_maps.append(m)
    return in_maps


def kernel(**inputs):
    nc = _get_prog()
    in_maps = make_in_maps(inputs)
    res = run_bass_kernel_spmd(nc, in_maps, core_ids=list(range(8)))
    out = np.empty((2, 16384, D), np.float32)
    for c in range(8):
        out[c // 4, (c % 4) * MAIN:((c % 4) + 1) * MAIN] = res.results[c]["y"]
    return out


# version 04
# speedup vs baseline: 1.0310x; 1.0310x over previous
import numpy as np
import ml_dtypes
import concourse.bass as bass
import concourse.mybir as mybir
from concourse.bass_utils import run_bass_kernel_spmd
from contextlib import ExitStack

F32 = mybir.dt.float32
BF16 = mybir.dt.bfloat16
AF = mybir.ActivationFunctionType
ALU = mybir.AluOpType
AX = mybir.AxisListType
BF = ml_dtypes.bfloat16

D = 2048
HQ = 16
BLK = 256
MAIN = 4096
LB = 2048
EXT = MAIN + LB
NBM = MAIN // BLK
NBL = LB // BLK
NBE = NBM + NBL
E = 1024
NEXP = 6
NE = 8
SED = 2048
MT = 512
NMT = MAIN // MT

KEYS = ("tensor", "vector", "scalar", "sync", "gpsimd")


class SemC:
    __slots__ = ("h", "v")

    def __init__(self, h):
        self.h = h
        self.v = 0

    def bump(self, n=1):
        self.v += n
        return self.v


def build(dbg=(), stop=None):
    import os as _os0
    nc = bass.Bass("TRN2", target_bir_lowering=False, detect_race_conditions=(_os0.environ.get("NORACE") is None))
    top = ExitStack()
    Et = top.enter_context

    x_ext = nc.dram_tensor("x_ext", [EXT, D], F32, kind="ExternalInput").ap()
    slope = nc.dram_tensor("slope", [1, HQ], F32, kind="ExternalInput").ap()
    wqT = nc.dram_tensor("wqT", [D, 4 * D], F32, kind="ExternalInput").ap()
    woT = nc.dram_tensor("woT", [D, D], F32, kind="ExternalInput").ap()
    wg_in = nc.dram_tensor("wg_in", [2, D, NE], F32, kind="ExternalInput").ap()
    bg_in = nc.dram_tensor("bg_in", [2, NE], F32, kind="ExternalInput").ap()
    wup_in = nc.dram_tensor("wup_in", [NEXP, D, E], F32, kind="ExternalInput").ap()
    bup_in = nc.dram_tensor("bup_in", [NEXP, E], F32, kind="ExternalInput").ap()
    wsw_in = nc.dram_tensor("wsw_in", [NEXP, E, E], F32, kind="ExternalInput").ap()
    bsw_in = nc.dram_tensor("bsw_in", [NEXP, E], F32, kind="ExternalInput").ap()
    wdn_in = nc.dram_tensor("wdn_in", [NEXP, E, D], F32, kind="ExternalInput").ap()
    bdn_in = nc.dram_tensor("bdn_in", [NEXP, D], F32, kind="ExternalInput").ap()
    wsu_in = nc.dram_tensor("wsu_in", [D, SED], F32, kind="ExternalInput").ap()
    bsu_in = nc.dram_tensor("bsu_in", [1, SED], F32, kind="ExternalInput").ap()
    wsa_in = nc.dram_tensor("wsa_in", [E, E], F32, kind="ExternalInput").ap()
    bsa_in = nc.dram_tensor("bsa_in", [1, E], F32, kind="ExternalInput").ap()
    wsd_in = nc.dram_tensor("wsd_in", [SED, D], F32, kind="ExternalInput").ap()
    bsd_in = nc.dram_tensor("bsd_in", [1, D], F32, kind="ExternalInput").ap()
    y_out = nc.dram_tensor("y", [MAIN, D], F32, kind="ExternalOutput").ap()

    s_xln = nc.dram_tensor("s_xln", [EXT, D], BF16).ap()
    s_qfm = nc.dram_tensor("s_qfm", [D, MAIN], BF16).ap()
    s_kfm = nc.dram_tensor("s_kfm", [D, MAIN], BF16).ap()
    s_ktm = nc.dram_tensor("s_ktm", [HQ, 128, EXT // 128, 128], BF16).ap()
    s_vtm = nc.dram_tensor("s_vtm", [HQ, 128, EXT // 128, 128], BF16).ap()
    s_ufm = nc.dram_tensor("s_ufm", [D, MAIN], BF16).ap()
    s_ofm = nc.dram_tensor("s_ofm", [D, MAIN], F32).ap()
    s_xr2 = nc.dram_tensor("s_xr2", [MAIN, D], F32).ap()
    s_xl2 = nc.dram_tensor("s_xl2", [MAIN, D], BF16).ap()
    s_o1 = nc.dram_tensor("s_o1", [2, D, MT], BF16).ap()
    z_up = nc.dram_tensor("z_up", [NEXP, D, E], BF16).ap()
    z_sw = nc.dram_tensor("z_sw", [NEXP, E, E], BF16).ap()
    z_dn = nc.dram_tensor("z_dn", [NEXP, 2, E, E], BF16).ap()
    z_su = nc.dram_tensor("z_su", [D, SED], BF16).ap()
    z_sa = nc.dram_tensor("z_sa", [E, E], BF16).ap()
    z_sd = nc.dram_tensor("z_sd", [2, SED, E], BF16).ap()

    scratch_map = {
        "s_xln": s_xln, "s_qfm": s_qfm, "s_kfm": s_kfm, "s_ktm": s_ktm,
        "s_vtm": s_vtm, "s_ufm": s_ufm, "s_ofm": s_ofm, "s_xr2": s_xr2,
        "s_xl2": s_xl2,
    }
    dbg_outs = {}
    for name in dbg:
        t = scratch_map[name]
        dbg_outs[name] = nc.dram_tensor("dbg_" + name, list(t.shape),
                                        t.dtype, kind="ExternalOutput").ap()

    il_id_bf = nc.inline_tensor(np.eye(128, dtype=BF), "il_id_bf").ap()
    il_id_f = nc.inline_tensor(np.eye(128, dtype=np.float32), "il_id_f").ap()
    il_ones_r_f = nc.inline_tensor(np.ones((1, 128), np.float32), "il_ones_r_f").ap()
    il_ones_c_f = nc.inline_tensor(np.ones((128, 1), np.float32), "il_ones_c_f").ap()
    il_ar1_r = nc.inline_tensor(np.arange(1, 257, dtype=np.float32)[None, :], "il_ar1_r").ap()
    il_ar0_c = nc.inline_tensor(np.arange(0, 128, dtype=np.float32)[:, None], "il_ar0_c").ap()
    ltm_np = np.zeros((1, 64), np.float32)
    for e_ in range(8):
        for j_ in range(8):
            if j_ < e_:
                ltm_np[0, e_ * 8 + j_] = 1.0
    il_ltm = nc.inline_tensor(ltm_np, "il_ltm").ap()
    sel_np = np.zeros((8, NEXP * 128), BF)
    for n_ in range(NEXP):
        sel_np[n_, n_ * 128:(n_ + 1) * 128] = 1.0
    il_sel = nc.inline_tensor(sel_np, "il_sel").ap()

    id_bf = Et(nc.sbuf_tensor("c_id_bf", [128, 128], BF16))
    id_f = Et(nc.sbuf_tensor("c_id_f", [128, 128], F32))
    ones_r_f = Et(nc.sbuf_tensor("c_ones_r_f", [1, 128], F32))
    ones_c_f = Et(nc.sbuf_tensor("c_ones_c_f", [128, 1], F32))
    ar1_r = Et(nc.sbuf_tensor("c_ar1_r", [1, 256], F32))
    ar0_c = Et(nc.sbuf_tensor("c_ar0_c", [128, 1], F32))
    ltm_t = Et(nc.sbuf_tensor("c_ltm", [128, 64], F32))
    sel_t = Et(nc.sbuf_tensor("c_sel", [8, NEXP * 128], BF16))
    slp_r = Et(nc.sbuf_tensor("c_slp_r", [1, HQ], F32))
    sigb = Et(nc.sbuf_tensor("c_sigb", [128, HQ], F32))
    lamb = Et(nc.sbuf_tensor("c_lamb", [128, HQ], F32))
    kdc = Et(nc.sbuf_tensor("c_kdc", [128, 2 * HQ], F32))
    ar1_bc = Et(nc.sbuf_tensor("c_ar1_bc", [128, 256], F32))
    dc_t = Et(nc.sbuf_tensor("c_dc", [128, 512], F32))
    msk_t = Et(nc.sbuf_tensor("c_msk", [128, 512], F32))
    eps5 = Et(nc.sbuf_tensor("c_eps5", [128, 1], F32))
    eps6 = Et(nc.sbuf_tensor("c_eps6", [128, 1], F32))
    bias_cols = Et(nc.sbuf_tensor("c_bias_cols", [128, 232], F32))
    bdn_bf = Et(nc.sbuf_tensor("c_bdn_bf", [NEXP, D], BF16))
    bg_bc = Et(nc.sbuf_tensor("c_bg_bc", [128, 2 * NE], F32))
    wg_sb = Et(nc.sbuf_tensor("c_wg_sb", [128, 2 * 16 * NE], BF16))
    bgrow = Et(nc.sbuf_tensor("c_bgrow", [1, 2 * NE], F32))
    BUP0, BSW0, BDN0, BSU0, BSA0, BSD0 = 0, 48, 96, 192, 208, 216

    qS = SemC(Et(nc.semaphore("sem_qS")))
    qP = SemC(Et(nc.semaphore("sem_qP")))
    qG = SemC(Et(nc.semaphore("sem_qG")))
    qY = SemC(Et(nc.semaphore("sem_qY")))
    eT = SemC(Et(nc.semaphore("sem_eT")))
    eV = SemC(Et(nc.semaphore("sem_eV")))
    eS = SemC(Et(nc.semaphore("sem_eS")))

    rec = {k: [] for k in KEYS}

    def _emit(key, f):
        rec[key].append(f)

    def T(f):
        _emit("tensor", f)

    def V(f):
        _emit("vector", f)

    def S(f):
        _emit("scalar", f)

    EMIT = {k: (lambda f, kk=k: _emit(kk, f)) for k in KEYS}

    def wt(key, sem, tgt):
        if tgt is None or tgt <= 0:
            return
        EMIT[key](lambda e, h=sem.h, v=tgt: e.wait_ge(h, v))

    def dma(key, out_ap, in_ap, sem, **kw):
        tgt = sem.bump(16)
        EMIT[key](lambda e, o=out_ap, i=in_ap, h=sem.h, k=dict(kw):
                  e.dma_start(out=o, in_=i, **k).then_inc(h, 16))
        return tgt

    def tnop_inc():
        f = rec["tensor"].pop()
        rec["tensor"].append(lambda e, f=f: f(e).then_inc(eT.h, 1))
        return eT.bump()

    def vinc_last():
        f = rec["vector"].pop()
        rec["vector"].append(lambda e, f=f: f(e).then_inc(eV.h, 1))
        return eV.bump()

    def flush(tag):
        nonlocal rec
        with nc.Block(tag) as blk:
            for key in KEYS:
                lst = rec[key]
                if lst:
                    getattr(blk, key)(lambda eng, L=tuple(lst): [f(eng) for f in L])
        rec = {k: [] for k in KEYS}

    nc._dbg_sems = {"qS": qS, "qP": qP, "qG": qG, "qY": qY, "eT": eT, "eV": eV, "eS": eS}

    def bar(tag):
        wt("sync", qS, qS.v)
        wt("scalar", qP, qP.v)
        wt("gpsimd", qG, qG.v)
        wt("gpsimd", qY, qY.v)
        try:
            wt("gpsimd", qC, qC.v)
        except NameError:
            pass
        flush(tag)
        nc.all_engine_barrier()

    def finish():
        if dbg_outs:
            for name, out_ap in dbg_outs.items():
                t = scratch_map[name]
                dma("gpsimd", out_ap, t, qY)
            wt("gpsimd", qY, qY.v)
            flush("dbg")
            nc.all_engine_barrier()
        wt("gpsimd", qY, qY.v)
        flush("fin")

    dma("sync", id_bf[:, :], il_id_bf, qS)
    dma("sync", id_f[:, :], il_id_f, qS)
    dma("sync", ones_r_f[:, :], il_ones_r_f, qS)
    dma("sync", ones_c_f[:, :], il_ones_c_f, qS)
    dma("sync", ar1_r[:, :], il_ar1_r, qS)
    dma("sync", ar0_c[:, :], il_ar0_c, qS)
    dma("sync", ltm_t[:, :], il_ltm.broadcast_to([128, 64]), qS)
    dma("sync", sel_t[:, :], il_sel, qS)
    dma("sync", slp_r[:, :], slope, qS)
    dma("sync", bgrow[:, :], bg_in.rearrange("a b -> (a b)").unsqueeze(0), qS)
    dma("gpsimd", bdn_bf[:, :], bdn_in, qG)
    for j in range(2):
        for c in range(16):
            dma("gpsimd", wg_sb[:, (j * 16 + c) * NE:(j * 16 + c + 1) * NE],
                wg_in[j, c * 128:(c + 1) * 128, :], qG)
    ld_all = qS.v
    wt("vector", qS, ld_all)
    wt("vector", qG, qG.v)
    V(lambda v: v.memset(eps5[:, :], 1e-5))
    V(lambda v: v.memset(eps6[:, :], 1e-6).then_inc(eV.h, 1))
    eV.bump()

    p0 = ExitStack()
    E0 = p0.enter_context
    if stop == "S0":
        bar("S0")
        finish()
        return nc
    psP = E0(nc.psum_tensor("psP0", [128, 512], F32))
    psP2 = E0(nc.psum_tensor("psP0b", [128, 512], F32))

    wt("tensor", qS, ld_all)
    T(lambda t: t.matmul(psP[:, 0:HQ], ones_r_f[0:1, :], slp_r[0:1, :], start=True, stop=True))
    for j in range(2):
        T(lambda t, j=j: t.matmul(psP[:, 32 + j * NE:32 + (j + 1) * NE], ones_r_f[0:1, :],
                                  bgrow[0:1, j * NE:(j + 1) * NE], start=True, stop=True))
    T(lambda t: t.matmul(psP2[:, 0:256], ones_r_f[0:1, :], ar1_r[0:1, :], start=True, stop=True))
    t_p0 = tnop_inc()
    wt("vector", eT, t_p0)
    V(lambda v: v.tensor_copy(sigb[:, :], psP[:, 0:HQ]))
    V(lambda v: v.tensor_copy(bg_bc[:, :], psP[:, 32:32 + 2 * NE]))
    V(lambda v: v.tensor_copy(ar1_bc[:, :], psP2[:, 0:256]).then_inc(eV.h, 1))
    v_p0 = eV.bump()
    wt("vector", eV, v_p0)
    V(lambda v: v.tensor_scalar(lamb[:, :], sigb[:, :], -256.0, None, ALU.mult))
    for h in range(HQ):
        for par in range(2):
            col = kdc[:, h * 2 + par: h * 2 + par + 1]
            V(lambda v, c=col, p=par: v.tensor_scalar(c, ar0_c[:, :], -1.0, float(255 - p * 128), ALU.mult, ALU.add))
    v_p1 = vinc_last()
    wt("vector", eV, v_p1)
    for h in range(HQ):
        for par in range(2):
            col = kdc[:, h * 2 + par: h * 2 + par + 1]
            V(lambda v, c=col, hh=h: v.tensor_tensor(c, c, sigb[:, hh:hh + 1], ALU.mult))
    for par in range(2):
        sl = slice(par * 256, (par + 1) * 256)
        V(lambda v, sl=sl, p=par: v.tensor_scalar(dc_t[:, sl], ar1_bc[:, :], ar0_c[:, 0:1],
                                                  float(-1.0 - p * 128), ALU.subtract, ALU.add))
    v_p2 = vinc_last()
    wt("vector", eV, v_p2)
    for par in range(2):
        sl = slice(par * 256, (par + 1) * 256)
        V(lambda v, sl=sl: v.tensor_scalar(msk_t[:, sl], dc_t[:, sl], 0.0, None, ALU.is_ge))
    v_p3 = vinc_last()
    wt("vector", eV, v_p3)
    for par in range(2):
        sl = slice(par * 256, (par + 1) * 256)
        V(lambda v, sl=sl: v.tensor_scalar_max(dc_t[:, sl], dc_t[:, sl], 0.0))
    v_p4 = vinc_last()
    wt("scalar", eV, v_p4)
    S(lambda s: s.activation(lamb[:, :], lamb[:, :], AF.Exp))
    S(lambda s: s.activation(kdc[:, :], kdc[:, :], AF.Exp, scale=-1.0).then_inc(eS.h, 1))
    eS.bump()

    if stop == "S2":
        bar("S2")
        finish()
        return nc
    bstage = E0(nc.sbuf_tensor("c_bstage", [1, 12288], F32))
    bias_jobs = [
        (bup_in.rearrange("n e -> (n e)").unsqueeze(0), 48, BUP0),
        (bsw_in.rearrange("n e -> (n e)").unsqueeze(0), 48, BSW0),
        (bdn_in.rearrange("n d -> (n d)").unsqueeze(0), 96, BDN0),
        (bsu_in, 16, BSU0),
        (bsa_in, 8, BSA0),
        (bsd_in, 16, BSD0),
    ]
    _kb = _os0.environ.get("KBIAS")
    if _kb is not None:
        bias_jobs = bias_jobs[: int(_kb)]
    wt("tensor", eV, v_p0)
    _bm = _os0.environ.get("BIASMODE", "")
    for bsrc, ncols, base in bias_jobs:
        wt("sync", eT, eT.v)
        if _bm != "mmonly":
            tgt = dma("sync", bstage[:, 0:ncols * 128], bsrc, qS)
            wt("tensor", qS, tgt)
        off0 = 0
        k = 0
        while k < ncols:
            if _bm == "dmaonly":
                break
            n4 = min(4, ncols - k)
            for i in range(n4):
                T(lambda t, o=off0 + (k + i) * 128, g=i:
                  t.matmul(psP2[:, g:g + 1], bstage[0:1, o:o + 128], ones_c_f[0:1, 0:1],
                           start=True, stop=True))
            tv = tnop_inc()
            wt("vector", eT, tv)
            V(lambda v, b=base + k, n=n4: v.tensor_copy(
                bias_cols[:, b:b + n], psP2[:, 0:n]).then_inc(eV.h, 1))
            tvv = eV.bump()
            wt("tensor", eV, tvv)
            k += n4

    if stop == "S3":
        bar("S3")
        finish()
        return nc
    xt0 = E0(nc.sbuf_tensor("a0_x", [128, 2 * D], F32))
    xsq = E0(nc.sbuf_tensor("a0_sq", [128, 2 * D], F32))
    lstg = E0(nc.sbuf_tensor("a0_stg", [128, 2 * D], BF16))
    scol = E0(nc.sbuf_tensor("a0_scol", [128, 8], F32))
    x_used = [None, None]
    st_done = [None, None]
    for i in range(EXT // 128):
        sl = i % 2
        xv = xt0[:, sl * D:(sl + 1) * D]
        sq = xsq[:, sl * D:(sl + 1) * D]
        st = lstg[:, sl * D:(sl + 1) * D]
        c_s1 = scol[:, sl * 4 + 0: sl * 4 + 1]
        c_s2 = scol[:, sl * 4 + 1: sl * 4 + 2]
        c_m = scol[:, sl * 4 + 2: sl * 4 + 3]
        c_v = scol[:, sl * 4 + 3: sl * 4 + 4]
        if x_used[sl] is not None:
            wt("sync", eV, x_used[sl])
        xl = dma("sync", xv, x_ext[i * 128:(i + 1) * 128, :], qS)
        wt("vector", qS, xl)
        V(lambda v, a=xv, b=sq: v.tensor_tensor(b, a, a, ALU.mult).then_inc(eV.h, 1))
        t1 = eV.bump()
        wt("vector", eV, t1)
        V(lambda v, b=sq, c=c_s2: v.reduce_sum(c, b, axis=AX.X))
        V(lambda v, a=xv, c=c_s1: v.reduce_sum(c, a, axis=AX.X).then_inc(eV.h, 1))
        t2 = eV.bump()
        wt("vector", eV, t2)
        V(lambda v, c=c_s1, m=c_m: v.tensor_scalar(m, c, 1.0 / D, None, ALU.mult))
        V(lambda v, c=c_s2, vv=c_v: v.tensor_scalar(vv, c, 1.0 / D, None, ALU.mult).then_inc(eV.h, 1))
        t3 = eV.bump()
        wt("vector", eV, t3)
        V(lambda v, m=c_m, tmp=c_s1: v.tensor_tensor(tmp, m, m, ALU.mult).then_inc(eV.h, 1))
        t4 = eV.bump()
        wt("vector", eV, t4)
        V(lambda v, vv=c_v, tmp=c_s1: v.tensor_tensor(vv, vv, tmp, ALU.subtract).then_inc(eV.h, 1))
        t5 = eV.bump()
        wt("scalar", eV, t5)
        S(lambda s, vv=c_v: s.activation(vv, vv, AF.Ln, bias=eps5[:, 0:1]))
        S(lambda s, vv=c_v: s.activation(vv, vv, AF.Exp, scale=-0.5).then_inc(eS.h, 1))
        t6 = eS.bump()
        if st_done[sl] is not None:
            wt("vector", qP, st_done[sl])
        wt("vector", eS, t6)
        V(lambda v, a=xv, o=st, m=c_m, r=c_v:
          v.tensor_scalar(o, a, m[:, 0:1], r[:, 0:1], ALU.subtract, ALU.mult).then_inc(eV.h, 1))
        x_used[sl] = eV.bump()
        wt("scalar", eV, x_used[sl])
        st_done[sl] = dma("scalar", s_xln[i * 128:(i + 1) * 128, :], st, qP)
    bar("P1_ln1_casts")
    if stop == "P1_ln1_casts":
        finish()
        return nc
    p0.close()

    pa = ExitStack()
    Ea = pa.enter_context
    xfm = Ea(nc.sbuf_tensor("pa_xfm", [128, 16 * MAIN], BF16))
    wsl = Ea(nc.sbuf_tensor("pa_wsl", [128, 2 * 16 * 512], BF16))
    qstg = Ea(nc.sbuf_tensor("pa_stg", [128, 4 * 512], BF16))
    ktst = Ea(nc.sbuf_tensor("pa_ktst", [128, 2 * 512], BF16))
    vst2 = Ea(nc.sbuf_tensor("pa_vst2", [128, 2 * 4 * 512], BF16))
    psA = [Ea(nc.psum_tensor("pa_ps%d" % i, [128, 512], F32)) for i in range(4)]
    psKT = Ea(nc.psum_tensor("pa_pskt", [128, 1024], BF16))

    xfm_tgt = None
    for c in range(16):
        for tt8 in range(8):
            xfm_tgt = dma("sync", xfm[:, c * MAIN + tt8 * 512: c * MAIN + tt8 * 512 + 512],
                          s_xln[LB + tt8 * 512: LB + tt8 * 512 + 512, c * 128:(c + 1) * 128],
                          qS, transpose=True)

    wslot_used = [None, None]

    def emit_fm_slices(slices, xsrc, xstride, first_wait):
        w_used = wslot_used
        ev_done = {}
        stg_free = [None] * 4
        kt_free = [None, None]
        kt_ps = [None, None]
        fw = first_wait
        ntt = xstride // 512
        for si, (wc0, rb, kind, dstt) in enumerate(slices):
            sl = si % 2
            wv = wsl[:, sl * 16 * 512:(sl + 1) * 16 * 512]
            if w_used[sl] is not None:
                wt("gpsimd", eT, w_used[sl])
            wl = None
            for c in range(16):
                wl = dma("gpsimd", wv[:, c * 512:(c + 1) * 512],
                         wqT[c * 128:(c + 1) * 128, wc0:wc0 + 512], qG)
            wt("tensor", qG, wl)
            if fw is not None:
                wt("tensor", qS, fw)
                fw = None
            g = 0
            for m in range(4):
                for tt8 in range(ntt):
                    bank = psA[g % 4]
                    if ev_done.get(g - 4) is not None:
                        wt("tensor", eS, ev_done[g - 4])
                    for c in range(16):
                        T(lambda t, b=bank, w=wv, c=c, m=m, x=xsrc, tt=tt8, stx=xstride:
                          t.matmul(b[:, :], w[:, c * 512 + m * 128: c * 512 + m * 128 + 128],
                                   x[:, c * stx + tt * 512: c * stx + tt * 512 + 512],
                                   start=(c == 0), stop=(c == 15)))
                    mmv = tnop_inc()
                    stg = qstg[:, (g % 4) * 512:(g % 4 + 1) * 512]
                    wt("scalar", eT, mmv)
                    if stg_free[g % 4] is not None:
                        wt("scalar", qP, stg_free[g % 4])
                    if kind == "k":
                        if ev_done.get(g - 4) is not None:
                            wt("scalar", eT, kt_last_T[0] if kt_last_T[0] else 0)
                    fn = AF.Silu if kind in ("q", "k") else AF.Copy
                    S(lambda s, o=stg, b=bank, f=fn: s.activation(o, b[:, :], f).then_inc(eS.h, 1))
                    ev_done[g] = eS.bump()
                    wt("scalar", eS, ev_done[g])
                    stg_free[g % 4] = dma("scalar",
                                          dstt[rb + m * 128: rb + m * 128 + 128, tt8 * 512:(tt8 + 1) * 512],
                                          stg, qP)
                    if kind == "k":
                        hh = (rb + m * 128) // 128
                        ksl = g % 2
                        wt("tensor", eS, ev_done[g])
                        if kt_ps[ksl] is not None:
                            wt("tensor", eV, kt_ps[ksl])
                        for i4 in range(4):
                            T(lambda t, i4=i4, stg=stg, ksl=ksl:
                              t.transpose(psKT[:, ksl * 512 + i4 * 128: ksl * 512 + (i4 + 1) * 128],
                                          stg[:, i4 * 128:(i4 + 1) * 128], id_bf[:, :]))
                        trv = tnop_inc()
                        kt_last_T[0] = trv
                        wt("vector", eT, trv)
                        if kt_free[ksl] is not None:
                            wt("vector", qP, kt_free[ksl])
                        for i4 in range(4):
                            t2g = (LB // 128) + tt8 * 4 + i4
                            par = t2g % 2
                            V(lambda v, i4=i4, ksl=ksl, hh=hh, par=par:
                              v.tensor_scalar(ktst[:, ksl * 512 + i4 * 128: ksl * 512 + (i4 + 1) * 128],
                                              psKT[:, ksl * 512 + i4 * 128: ksl * 512 + (i4 + 1) * 128],
                                              kdc[:, hh * 2 + par: hh * 2 + par + 1], None, ALU.mult))
                        kt_ps[ksl] = vinc_last()
                        wt("scalar", eV, kt_ps[ksl])
                        kt_free[ksl] = dma("scalar",
                                           s_ktm[hh, :, (LB // 128) + tt8 * 4: (LB // 128) + tt8 * 4 + 4, :],
                                           ktst[:, ksl * 512:(ksl + 1) * 512].rearrange("p (t e) -> p t e", t=4),
                                           qP)
                    g += 1
            w_used[sl] = eT.v

    kt_last_T = [None]
    fm_slices = []
    for s4 in range(4):
        fm_slices.append((s4 * 512, s4 * 512, "q", s_qfm))
    for s4 in range(4):
        fm_slices.append((D + s4 * 512, s4 * 512, "k", s_kfm))
    for s4 in range(4):
        fm_slices.append((3 * D + s4 * 512, s4 * 512, "u", s_ufm))
    emit_fm_slices(fm_slices, xfm, MAIN, xfm_tgt)

    def emit_roleswap(xsrc, xstride, t2base, vcol0, dst, do_silu_kd, first_wait):
        w_used = wslot_used
        psum_guard = {}
        bfree = [None, None]
        fw = first_wait
        ntm = (LB if xstride == LB else MAIN) // 128
        for vs in range(4):
            sl = vs % 2
            wv = wsl[:, sl * 16 * 512:(sl + 1) * 16 * 512]
            if w_used[sl] is not None:
                wt("gpsimd", eT, w_used[sl])
            wl = None
            for c in range(16):
                wl = dma("gpsimd", wv[:, c * 512:(c + 1) * 512],
                         wqT[c * 128:(c + 1) * 128, vcol0 + vs * 512: vcol0 + vs * 512 + 512], qG)
            wt("tensor", qG, wl)
            if fw is not None:
                wt("tensor", qS, fw)
            g = 0
            for tm in range(ntm):
                bank = psA[g % 4]
                if psum_guard.get(g - 4) is not None:
                    wt("tensor", psum_guard[g - 4][0], psum_guard[g - 4][1])
                for c in range(16):
                    T(lambda t, b=bank, w=wv, c=c, tm=tm, x=xsrc, stx=xstride:
                      t.matmul(b[:, :], x[:, c * stx + tm * 128: c * stx + tm * 128 + 128],
                               w[:, c * 512:(c + 1) * 512], start=(c == 0), stop=(c == 15)))
                mmv = tnop_inc()
                bsl = (tm // 4) % 2
                vbuf = vst2[:, bsl * 2048 + (tm % 4) * 512: bsl * 2048 + (tm % 4) * 512 + 512]
                wt("scalar", eT, mmv)
                if tm % 4 == 0 and bfree[bsl] is not None:
                    wt("scalar", qP, bfree[bsl])
                fn = AF.Silu if do_silu_kd else AF.Copy
                S(lambda s, o=vbuf, b=bank, f=fn: s.activation(o, b[:, :], f).then_inc(eS.h, 1))
                sv = eS.bump()
                if do_silu_kd:
                    wt("vector", eS, sv)
                    par = tm % 2
                    for hh4 in range(4):
                        V(lambda v, o=vbuf, hh=vs * 4 + hh4, par=par, h4=hh4:
                          v.tensor_scalar(o[:, h4 * 128:(h4 + 1) * 128], o[:, h4 * 128:(h4 + 1) * 128],
                                          kdc[:, hh * 2 + par: hh * 2 + par + 1], None, ALU.mult))
                    done = vinc_last()
                    psum_guard[g] = (eV, done)
                else:
                    done = sv
                    psum_guard[g] = (eS, done)
                if tm % 4 == 3:
                    gsem, dval = psum_guard[g]
                    wt("scalar", gsem, dval)
                    last = None
                    for hh4 in range(4):
                        last = dma("scalar", dst[vs * 4 + hh4, :, t2base + tm - 3: t2base + tm + 1, :],
                                   vst2[:, bsl * 2048: bsl * 2048 + 2048]
                                   .rearrange("p (t e) -> p t e", t=4)[:, :, hh4 * 128:(hh4 + 1) * 128], qP)
                    bfree[bsl] = last
                g += 1
            w_used[sl] = eT.v

    emit_roleswap(xfm, MAIN, LB // 128, 2 * D, s_vtm, False, None)
    bar("PA_main")
    if stop == "PA_main":
        finish()
        return nc

    xlb_tgt = None
    for c in range(16):
        for tt4 in range(4):
            xlb_tgt = dma("sync", xfm[:, c * LB + tt4 * 512: c * LB + tt4 * 512 + 512],
                          s_xln[tt4 * 512: tt4 * 512 + 512, c * 128:(c + 1) * 128],
                          qS, transpose=True)
    emit_roleswap(xfm, LB, 0, D, s_ktm, True, xlb_tgt)
    emit_roleswap(xfm, LB, 0, 2 * D, s_vtm, False, None)
    bar("PA_lb")
    if stop == "PA_lb":
        finish()
        return nc

    pa.close()
    pc = ExitStack()
    Ec = pc.enter_context
    qh_b = Ec(nc.sbuf_tensor("pb_qh", [128, 2 * MAIN], BF16))
    kfh_b = Ec(nc.sbuf_tensor("pb_kfh", [128, 2 * MAIN], BF16))
    kth_b = Ec(nc.sbuf_tensor("pb_kth", [128, 2 * 48 * 128], BF16))
    vth_b = Ec(nc.sbuf_tensor("pb_vth", [128, 2 * 48 * 128], BF16))
    oh_b = Ec(nc.sbuf_tensor("pb_oh", [128, 2 * MAIN], F32))
    qd_b = Ec(nc.sbuf_tensor("pb_qd", [128, 2 * 256], F32))
    dd_b = Ec(nc.sbuf_tensor("pb_dd", [128, 2 * 512], BF16))
    tmpd = Ec(nc.sbuf_tensor("pb_tmpd", [128, 512], F32))
    qs_b = Ec(nc.sbuf_tensor("pb_qs", [128, 2 * 256], BF16))
    sc_b = Ec(nc.sbuf_tensor("pb_sc", [128, 2 * 512], BF16))
    S_b = Ec(nc.sbuf_tensor("pb_S", [128, 128], F32))
    Sbf_b = Ec(nc.sbuf_tensor("pb_Sbf", [128, 2 * 128], BF16))
    psSC = [Ec(nc.psum_tensor("pb_psc%d" % i, [128, 512], F32)) for i in range(2)]
    psO = [Ec(nc.psum_tensor("pb_pso%d" % i, [128, 512], F32)) for i in range(2)]
    psAA = [Ec(nc.psum_tensor("pb_psa%d" % i, [128, 512], F32)) for i in range(2)]

    head_T_done = [None] * (HQ + 2)
    head_V_done = [None] * (HQ + 2)
    head_S_done = [None] * (HQ + 2)
    head_spill = [None] * (HQ + 2)
    head_load = [None] * HQ

    def emit_head_load(h):
        hs = h % 2
        if h >= 2:
            wt("sync", eT, head_T_done[h - 2])
            wt("sync", eV, head_V_done[h - 2])
        t1 = dma("sync", qh_b[:, hs * MAIN:(hs + 1) * MAIN], s_qfm[h * 128:(h + 1) * 128, :], qS)
        t1 = dma("sync", kfh_b[:, hs * MAIN:(hs + 1) * MAIN], s_kfm[h * 128:(h + 1) * 128, :], qS)
        t1 = dma("sync", kth_b[:, hs * 48 * 128:(hs + 1) * 48 * 128],
                 s_ktm[h].rearrange("p t e -> p (t e)"), qS)
        t1 = dma("sync", vth_b[:, hs * 48 * 128:(hs + 1) * 48 * 128],
                 s_vtm[h].rearrange("p t e -> p (t e)"), qS)
        head_load[h] = t1

    emit_head_load(0)
    emit_head_load(1)

    cst = Ec(nc.sbuf_tensor("p1_cst", [128, 8 * 2048], BF16))
    qC = SemC(Ec(nc.semaphore("sem_qC")))
    cast_jobs = []

    def add_cast(csrc, cdst, rows, cols):
        step = 2 if cols <= 1024 else 1
        for r0 in range(0, rows, 128 * step):
            nchunk = min(step, (rows - r0) // 128)
            sap = csrc[r0:r0 + nchunk * 128, :].rearrange("(c p) e -> p c e", p=128) \
                if nchunk > 1 else csrc[r0:r0 + 128, :]
            dap = cdst[r0:r0 + nchunk * 128, :].rearrange("(c p) e -> p c e", p=128) \
                if nchunk > 1 else cdst[r0:r0 + 128, :]
            cast_jobs.append((sap, dap, nchunk * cols, nchunk, cols))

    for n in range(NEXP):
        add_cast(wup_in[n], z_up[n], D, E)
        add_cast(wsw_in[n], z_sw[n], E, E)
        for oc in range(2):
            add_cast(wdn_in[n][:, oc * E:(oc + 1) * E], z_dn[n, oc], E, E)
    add_cast(wsu_in, z_su, D, SED)
    add_cast(wsa_in, z_sa, E, E)
    for oc in range(2):
        add_cast(wsd_in[:, oc * E:(oc + 1) * E], z_sd[oc], SED, E)

    NSL = 8
    cload = [None] * len(cast_jobs)
    cstore = [None] * len(cast_jobs)
    def cst_view(sl, nchunk, cols):
        v = cst[:, sl * 2048: sl * 2048 + nchunk * cols]
        return v.rearrange("p (c e) -> p c e", c=nchunk) if nchunk > 1 else v

    for i in range(min(NSL, len(cast_jobs))):
        sl = i % NSL
        cload[i] = dma("gpsimd", cst_view(sl, cast_jobs[i][3], cast_jobs[i][4]), cast_jobs[i][0], qG)
    for i, (csrc, cdst, ncols, nchunk, cols) in enumerate(cast_jobs):
        sl = i % NSL
        wt("gpsimd", qG, cload[i])
        cstore[i] = dma("gpsimd", cdst, cst_view(sl, nchunk, cols), qC)
        j = i + NSL
        if j < len(cast_jobs):
            wt("gpsimd", qC, cstore[i])
            cload[j] = dma("gpsimd", cst_view(j % NSL, cast_jobs[j][3], cast_jobs[j][4]), cast_jobs[j][0], qG)
    cast_done_tgt = ("qC", qC.v)

    if stop == "S4":
        bar("S4")
        finish()
        return nc


    for h in range(HQ):
        hs = h % 2
        qh = qh_b[:, hs * MAIN:(hs + 1) * MAIN]
        kfh = kfh_b[:, hs * MAIN:(hs + 1) * MAIN]
        kth = kth_b[:, hs * 48 * 128:(hs + 1) * 48 * 128]
        vth = vth_b[:, hs * 48 * 128:(hs + 1) * 48 * 128]
        oh = oh_b[:, hs * MAIN:(hs + 1) * MAIN]
        qd = qd_b[:, hs * 256:(hs + 1) * 256]
        dd = dd_b[:, hs * 512:(hs + 1) * 512]

        if h >= 1:
            wt("vector", eT, head_T_done[h - 1])
            wt("vector", eS, head_S_done[h - 1])
            wt("tensor", eV, head_V_done[h - 1])
            wt("tensor", eS, head_S_done[h - 1])
        if h >= 2:
            wt("scalar", qP, head_spill[h - 2])
        V(lambda v, qd=qd, h=h: v.tensor_scalar(qd, ar1_bc[:, :], sigb[:, h:h + 1], None, ALU.mult))
        V(lambda v, h=h: v.tensor_scalar(tmpd[:, :], dc_t[:, :], sigb[:, h:h + 1], None, ALU.mult).then_inc(eV.h, 1))
        tgen = eV.bump()
        wt("scalar", eV, tgen)
        S(lambda s, qd=qd: s.activation(qd, qd, AF.Exp, scale=-1.0))
        S(lambda s: s.activation(tmpd[:, :], tmpd[:, :], AF.Exp, scale=-1.0).then_inc(eS.h, 1))
        tgen2 = eS.bump()
        wt("vector", eS, tgen2)
        V(lambda v, dd=dd: v.tensor_tensor(dd, tmpd[:, :], msk_t[:, :], ALU.mult))
        V(lambda v: v.memset(S_b[:, :], 0.0).then_inc(eV.h, 1))
        dd_done = eV.bump()

        wt("tensor", qS, head_load[h])
        wt("vector", qS, head_load[h])

        sc_done = {}
        mask_done = {}
        a_done = {}
        o_done = {}
        oevac_done = {}
        sbf_done = {}
        qs_done = {}
        upd_hist = {-1: dd_done}

        for t in range(NBE):
            par = t % 2
            lb = t < NBL
            if upd_hist.get(t - 2) is not None:
                wt("tensor", eV, upd_hist[t - 2])
            for i2 in range(2):
                T(lambda t_, par=par, kth=kth, vth=vth, t2=t * 2 + i2, i2=i2:
                  t_.matmul(psAA[par][:, 0:128], kth[:, t2 * 128:(t2 + 1) * 128],
                            vth[:, t2 * 128:(t2 + 1) * 128], start=(i2 == 0), stop=(i2 == 1)))
            a_done[t] = tnop_inc()

            if not lb:
                tm = t - NBL
                wt("vector", eS, tgen2)
                if o_done.get(t - 2) is not None:
                    wt("vector", eT, o_done[t - 2])
                V(lambda v, par=par, qh=qh, tm=tm, qd=qd:
                  v.tensor_tensor(qs_b[:, par * 256:(par + 1) * 256], qd,
                                  qh[:, tm * 256:(tm + 1) * 256], ALU.mult).then_inc(eV.h, 1))
                qs_done[t] = eV.bump()
                if mask_done.get(t - 2) is not None:
                    wt("tensor", eV, mask_done[t - 2])
                for i2 in range(2):
                    T(lambda t_, par=par, kfh=kfh, qh=qh, tm=tm, i2=i2:
                      t_.matmul(psSC[par][:, i2 * 256:(i2 + 1) * 256],
                                kfh[:, tm * 256 + i2 * 128: tm * 256 + i2 * 128 + 128],
                                qh[:, tm * 256:(tm + 1) * 256], start=True, stop=True))
                sc_done[t] = tnop_inc()
                wt("vector", eT, sc_done[t])
                if o_done.get(t - 2) is not None:
                    wt("vector", eT, o_done[t - 2])
                for i2 in range(2):
                    V(lambda v, par=par, i2=i2, dd=dd:
                      v.tensor_tensor(sc_b[:, par * 512 + i2 * 256: par * 512 + (i2 + 1) * 256],
                                      psSC[par][:, i2 * 256:(i2 + 1) * 256],
                                      dd[:, i2 * 256:(i2 + 1) * 256], ALU.mult))
                mask_done[t] = vinc_last()
                wt("vector", eV, upd_hist[t - 1])
                if o_done.get(t - 2) is not None:
                    wt("vector", eT, o_done[t - 2])
                V(lambda v, par=par: v.tensor_copy(Sbf_b[:, par * 128:(par + 1) * 128], S_b[:, :]).then_inc(eV.h, 1))
                sbf_done[t] = eV.bump()
                wt("tensor", eV, mask_done[t])
                wt("tensor", eV, sbf_done[t])
                wt("tensor", eV, qs_done[t])
                if oevac_done.get(t - 2) is not None:
                    wt("tensor", eS, oevac_done[t - 2])
                for i2 in range(2):
                    T(lambda t_, par=par, vth=vth, t2=t * 2 + i2, i2=i2:
                      t_.matmul(psO[par][:, 0:256], vth[:, t2 * 128:(t2 + 1) * 128],
                                sc_b[:, par * 512 + i2 * 256: par * 512 + (i2 + 1) * 256],
                                start=(i2 == 0), stop=False))
                T(lambda t_, par=par: t_.matmul(psO[par][:, 0:256], Sbf_b[:, par * 128:(par + 1) * 128],
                                                qs_b[:, par * 256:(par + 1) * 256], start=False, stop=True))
                o_done[t] = tnop_inc()
                wt("scalar", eT, o_done[t])
                S(lambda s, par=par, oh=oh, tm=tm:
                  s.activation(oh[:, tm * 256:(tm + 1) * 256], psO[par][:, 0:256], AF.Copy).then_inc(eS.h, 1))
                oevac_done[t] = eS.bump()

            wt("vector", eT, a_done[t])
            if not lb:
                wt("vector", eV, sbf_done[t])
            V(lambda v, par=par, h=h:
              v.scalar_tensor_tensor(S_b[:, :], S_b[:, :], lamb[:, h:h + 1], psAA[par][:, 0:128],
                                     ALU.mult, ALU.add).then_inc(eV.h, 1))
            upd_hist[t] = eV.bump()

        head_T_done[h] = eT.v
        head_V_done[h] = eV.v
        head_S_done[h] = eS.v
        wt("scalar", eS, oevac_done[NBE - 1])
        head_spill[h] = dma("scalar", s_ofm[h * 128:(h + 1) * 128, :], oh, qP)
        if h + 2 < HQ:
            emit_head_load(h + 2)
    bar("PB_scan")
    if stop == "PB_scan":
        finish()
        return nc

    pc.close()
    pd = ExitStack()
    Ed = pd.enter_context
    wo_sb = Ed(nc.sbuf_tensor("pc_wo", [128, 16 * D], BF16))
    o_sb = Ed(nc.sbuf_tensor("pc_o", [128, 16 * 512], F32))
    u_sb = Ed(nc.sbuf_tensor("pc_u", [128, 16 * 512], BF16))
    oum = Ed(nc.sbuf_tensor("pc_oum", [128, 16 * 512], BF16))
    sq_c = Ed(nc.sbuf_tensor("pc_sq", [128, 2 * 512], F32))
    xm_c = Ed(nc.sbuf_tensor("pc_xm", [128, 2 * D], F32))
    xmid = Ed(nc.sbuf_tensor("pc_xmid", [128, 2 * D], F32))
    rrow = Ed(nc.sbuf_tensor("pc_rrow", [1, 512], F32))
    l2c = Ed(nc.sbuf_tensor("pc_l2c", [128, 4], F32))
    xl2f = Ed(nc.sbuf_tensor("pc_xl2f", [128, D], F32))
    xl2s = Ed(nc.sbuf_tensor("pc_xl2s", [128, 2 * D], BF16))
    psW = [Ed(nc.psum_tensor("pc_psw%d" % i, [128, 512], F32)) for i in range(4)]
    psQ = Ed(nc.psum_tensor("pc_psq", [128, 512], F32))
    psB2 = Ed(nc.psum_tensor("pc_psb", [128, 512], F32))

    wo_tgt = None
    for c in range(16):
        wo_tgt = dma("gpsimd", wo_sb[:, c * D:(c + 1) * D], woT[c * 128:(c + 1) * 128, :], qG)

    ssqmm = {}
    prev_xm_done = [None]
    oum_used = None
    xm_used = [None, None]
    xmid_used = [None, None]
    stg_used = [None, None]
    for mt in range(NMT):
        wt("vector", eT, eT.v)
        if oum_used is not None:
            wt("sync", eV, oum_used)
        lo = None
        for c in range(16):
            lo = dma("sync", o_sb[:, c * 512:(c + 1) * 512],
                     s_ofm[c * 128:(c + 1) * 128, mt * 512:(mt + 1) * 512], qS)
            lo = dma("sync", u_sb[:, c * 512:(c + 1) * 512],
                     s_ufm[c * 128:(c + 1) * 128, mt * 512:(mt + 1) * 512], qS)
        wt("vector", qS, lo)
        for c in range(16):
            ssl = c % 2
            if ssqmm.get((mt, c - 2)) is not None:
                wt("vector", eT, ssqmm[(mt, c - 2)])
            V(lambda v, c=c, ssl=ssl: v.tensor_tensor(sq_c[:, ssl * 512:(ssl + 1) * 512],
                                                      o_sb[:, c * 512:(c + 1) * 512],
                                                      o_sb[:, c * 512:(c + 1) * 512], ALU.mult).then_inc(eV.h, 1))
            sqd = eV.bump()
            wt("tensor", eV, sqd)
            T(lambda t, c=c, ssl=ssl: t.matmul(psQ[0:1, :], ones_c_f[:, 0:1],
                                               sq_c[:, ssl * 512:(ssl + 1) * 512],
                                               start=(c == 0), stop=(c == 15)))
            ssqmm[(mt, c)] = tnop_inc()
        wt("scalar", eT, ssqmm[(mt, 15)])
        S(lambda s: s.activation(rrow[0:1, :], psQ[0:1, :], AF.Ln, bias=eps6[0:1, 0:1], scale=1.0 / D))
        S(lambda s: s.activation(rrow[0:1, :], rrow[0:1, :], AF.Exp, scale=-0.5).then_inc(eS.h, 1))
        rs_done = eS.bump()
        wt("tensor", eS, rs_done)
        T(lambda t: t.matmul(psB2[:, :], ones_r_f[0:1, :], rrow[0:1, :], start=True, stop=True))
        bc_done = tnop_inc()
        wt("vector", eT, bc_done)
        for c in range(16):
            V(lambda v, c=c: v.tensor_tensor(o_sb[:, c * 512:(c + 1) * 512],
                                             o_sb[:, c * 512:(c + 1) * 512],
                                             u_sb[:, c * 512:(c + 1) * 512], ALU.mult))
        ou1 = vinc_last()
        wt("vector", eV, ou1)
        for c in range(16):
            V(lambda v, c=c: v.tensor_tensor(oum[:, c * 512:(c + 1) * 512],
                                             o_sb[:, c * 512:(c + 1) * 512],
                                             psB2[:, :], ALU.mult))
        oum_done = vinc_last()
        oum_used = oum_done

        for t4 in range(4):
            xsl = t4 % 2
            glob_t = mt * 512 + t4 * 128
            if xm_used[xsl] is not None:
                wt("gpsimd", eV, xm_used[xsl])
            xl = dma("gpsimd", xm_c[:, xsl * D:(xsl + 1) * D],
                     x_ext[LB + glob_t: LB + glob_t + 128, :], qG)
            wt("tensor", eV, oum_done)
            wt("tensor", qG, wo_tgt)
            if prev_xm_done[0] is not None:
                wt("tensor", eV, prev_xm_done[0])
            for isl in range(4):
                for c in range(16):
                    T(lambda t, b=psW[isl], c=c, t4=t4, isl=isl:
                      t.matmul(b[:, :], oum[:, c * 512 + t4 * 128: c * 512 + t4 * 128 + 128],
                               wo_sb[:, c * D + isl * 512: c * D + isl * 512 + 512],
                               start=(c == 0), stop=(c == 15)))
            wmm = tnop_inc()
            wt("vector", eT, wmm)
            wt("vector", qG, xl)
            if xmid_used[xsl] is not None:
                wt("vector", qP, xmid_used[xsl])
            for isl in range(4):
                V(lambda v, isl=isl, xsl=xsl:
                  v.tensor_tensor(xmid[:, xsl * D + isl * 512: xsl * D + isl * 512 + 512],
                                  psW[isl][:, :], xm_c[:, xsl * D + isl * 512: xsl * D + isl * 512 + 512], ALU.add))
            xm_done = vinc_last()
            prev_xm_done[0] = xm_done
            xm_used[xsl] = xm_done
            xv = xmid[:, xsl * D:(xsl + 1) * D]
            wt("vector", eV, xm_done)
            V(lambda v, xv=xv: v.tensor_tensor(xl2f[:, :], xv, xv, ALU.mult).then_inc(eV.h, 1))
            lt1 = eV.bump()
            wt("vector", eV, lt1)
            V(lambda v: v.reduce_sum(l2c[:, 1:2], xl2f[:, :], axis=AX.X))
            V(lambda v, xv=xv: v.reduce_sum(l2c[:, 0:1], xv, axis=AX.X).then_inc(eV.h, 1))
            lt2 = eV.bump()
            wt("vector", eV, lt2)
            V(lambda v: v.tensor_scalar(l2c[:, 2:3], l2c[:, 0:1], 1.0 / D, None, ALU.mult))
            V(lambda v: v.tensor_scalar(l2c[:, 3:4], l2c[:, 1:2], 1.0 / D, None, ALU.mult).then_inc(eV.h, 1))
            lt3 = eV.bump()
            wt("vector", eV, lt3)
            V(lambda v: v.tensor_tensor(l2c[:, 0:1], l2c[:, 2:3], l2c[:, 2:3], ALU.mult).then_inc(eV.h, 1))
            lt4 = eV.bump()
            wt("vector", eV, lt4)
            V(lambda v: v.tensor_tensor(l2c[:, 3:4], l2c[:, 3:4], l2c[:, 0:1], ALU.subtract).then_inc(eV.h, 1))
            lt5 = eV.bump()
            wt("scalar", eV, lt5)
            S(lambda s: s.activation(l2c[:, 3:4], l2c[:, 3:4], AF.Ln, bias=eps5[:, 0:1]))
            S(lambda s: s.activation(l2c[:, 3:4], l2c[:, 3:4], AF.Exp, scale=-0.5).then_inc(eS.h, 1))
            lt6 = eS.bump()
            wt("vector", eS, lt6)
            V(lambda v, xv=xv: v.tensor_scalar(xl2f[:, :], xv, l2c[:, 2:3], l2c[:, 3:4],
                                               ALU.subtract, ALU.mult).then_inc(eV.h, 1))
            lt7 = eV.bump()
            wt("vector", eV, lt7)
            if stg_used[xsl] is not None:
                wt("vector", qP, stg_used[xsl])
            V(lambda v, xv=xv: v.tensor_tensor(xv, xv, xl2f[:, :], ALU.add))
            V(lambda v, xsl=xsl: v.tensor_copy(xl2s[:, xsl * D:(xsl + 1) * D], xl2f[:, :]).then_inc(eV.h, 1))
            lt8 = eV.bump()
            wt("scalar", eV, lt8)
            dma("scalar", s_xr2[glob_t: glob_t + 128, :], xv, qP)
            stg_used[xsl] = dma("scalar", s_xl2[glob_t: glob_t + 128, :], xl2s[:, xsl * D:(xsl + 1) * D], qP)
            xmid_used[xsl] = stg_used[xsl]
    bar("PC_epilogue")
    if stop == "PC_epilogue":
        finish()
        return nc

    pd.close()
    pe = ExitStack()
    Ee = pe.enter_context
    wsl_d = Ee(nc.sbuf_tensor("pd_wsl", [128, 2 * 8192], BF16))
    h2w_d = Ee(nc.sbuf_tensor("pd_h2w", [128, NEXP * 8 * 512], BF16))
    sha_d = Ee(nc.sbuf_tensor("pd_sha", [128, 16 * 512], BF16))
    u1_d = Ee(nc.sbuf_tensor("pd_u1", [128, 8192], BF16))
    xfm2_d = Ee(nc.sbuf_tensor("pd_xfm2", [128, 16 * 512], BF16))
    xt1_d = Ee(nc.sbuf_tensor("pd_xt1", [128, 16 * 512], BF16))
    wbc_d = Ee(nc.sbuf_tensor("pd_wbc", [128, NEXP * 512], BF16))
    g_d = Ee(nc.sbuf_tensor("pd_g", [128, 104], F32))
    cmp_d = Ee(nc.sbuf_tensor("pd_cmp", [128, 128], F32))
    wtm_d = Ee(nc.sbuf_tensor("pd_wtm", [128, 32], F32))
    wfm_d = Ee(nc.sbuf_tensor("pd_wfm", [8, 512], BF16))
    tmp_d = Ee(nc.sbuf_tensor("pd_tmp", [128, 2 * 512], F32))
    o1s_d = Ee(nc.sbuf_tensor("pd_o1s", [128, 2 * 512], BF16))
    o1tm_d = Ee(nc.sbuf_tensor("pd_o1tm", [128, 2 * D], BF16))
    xr2_d = Ee(nc.sbuf_tensor("pd_xr2", [128, 2 * D], F32))
    yst_d = Ee(nc.sbuf_tensor("pd_yst", [128, D], F32))
    psD = [Ee(nc.psum_tensor("pd_ps%d" % i, [128, 512], F32)) for i in range(8)]

    sh_v = u1_d[:, 0:8192]
    h_v = u1_d[:, 0:4096]
    si_v = u1_d[:, 4096:8192]

    wt("sync", qC, cast_done_tgt[1])

    slab_load = {}
    slab_used = {}
    slab_ctr = [0]

    def slab(pieces):
        i = slab_ctr[0]
        slab_ctr[0] += 1
        sl = i % 2
        view = wsl_d[:, sl * 8192:(sl + 1) * 8192]
        if i >= 2:
            wt("sync", eT, slab_used[i - 2])
        tgt = None
        for (c0, c1), src in pieces:
            tgt = dma("sync", view[:, c0:c1], src, qS)
        slab_load[i] = tgt
        return view, i

    def slab_end(i, tgt=None):
        slab_used[i] = tgt if tgt is not None else tnop_inc()

    def fence_T():
        wt("tensor", eS, eS.v)
        wt("tensor", eV, eV.v)

    y_final = [None]
    y_final_hist = [None, None]
    o1tm_used = [None, None]
    o1s_used = [None, None]
    evd = {}
    xt1_ready = None

    o1_store_done = {}

    def emit_final(fmt):
        fosl = fmt % 2
        for t4 in range(4):
            tsl = t4 % 2
            wt("scalar", qP, o1_store_done[fmt])
            if o1tm_used[tsl] is not None:
                wt("scalar", eV, o1tm_used[tsl])
            o1t_tgt = None
            for hs4 in range(4):
                o1t_tgt = dma("scalar", o1tm_d[:, tsl * D + hs4 * 512: tsl * D + (hs4 + 1) * 512],
                              s_o1[fosl, hs4 * 512:(hs4 + 1) * 512, t4 * 128:(t4 + 1) * 128], qP, transpose=True)
            if y_final_hist[tsl] is not None:
                wt("gpsimd", eV, y_final_hist[tsl])
            xr_t = dma("gpsimd", xr2_d[:, tsl * D:(tsl + 1) * D],
                       s_xr2[fmt * 512 + t4 * 128: fmt * 512 + (t4 + 1) * 128, :], qG)
            wt("vector", qP, o1t_tgt)
            wt("vector", qG, xr_t)
            if y_final[0] is not None:
                wt("vector", qY, y_final[0])
            V(lambda v, tsl=tsl: v.tensor_tensor(yst_d[:, :], xr2_d[:, tsl * D:(tsl + 1) * D],
                                                 o1tm_d[:, tsl * D:(tsl + 1) * D], ALU.add).then_inc(eV.h, 1))
            yv = eV.bump()
            o1tm_used[tsl] = yv
            y_final_hist[tsl] = yv
            wt("gpsimd", eV, yv)
            y_final[0] = dma("gpsimd", y_out[fmt * 512 + t4 * 128: fmt * 512 + (t4 + 1) * 128, :], yst_d[:, :], qY)

    for mt in range(NMT):
        osl = mt % 2
        wt("scalar", eT, eT.v)
        wt("scalar", eV, eV.v)
        wt("vector", eT, eT.v)
        xf_tgt = None
        for c in range(16):
            xf_tgt = dma("scalar", xfm2_d[:, c * 512:(c + 1) * 512],
                         s_xl2[mt * 512:(mt + 1) * 512, c * 128:(c + 1) * 128], qP, transpose=True)
        for r in range(2):
            xt = xfm2_d if r == 0 else xt1_d
            if r == 0:
                wt("tensor", qP, xf_tgt)
                wt("vector", qP, xf_tgt)
            else:
                wt("tensor", eV, xt1_ready)
            fence_T()
            for t4 in range(4):
                gb = 6 + (t4 % 2)
                gv = g_d[:, t4 * 8:(t4 + 1) * 8]
                if evd.get(("g", mt, r, t4 - 2)) is not None:
                    wt("tensor", eV, evd[("g", mt, r, t4 - 2)])
                for c in range(16):
                    T(lambda t, c=c, t4=t4, xt=xt, r=r, gb=gb:
                      t.matmul(psD[gb][:, t4 * 8:(t4 + 1) * 8],
                               xt[:, c * 512 + t4 * 128: c * 512 + t4 * 128 + 128],
                               wg_sb[:, (r * 16 + c) * NE:(r * 16 + c + 1) * NE],
                               start=(c == 0), stop=(c == 15)))
                gmm = tnop_inc()
                wt("vector", eT, gmm)
                V(lambda v, gv=gv, t4=t4, r=r, gb=gb: v.tensor_tensor(gv, psD[gb][:, t4 * 8:(t4 + 1) * 8],
                                                               bg_bc[:, r * NE:(r + 1) * NE], ALU.add).then_inc(eV.h, 1))
                evd[("g", mt, r, t4)] = eV.bump()
                gj = gv.unsqueeze(1).broadcast_to([128, 8, 8])
                ge = gv.unsqueeze(2).broadcast_to([128, 8, 8])
                rk = g_d[:, 32 + t4 * 8: 40 + t4 * 8]
                mk = g_d[:, 64 + t4 * 8: 72 + t4 * 8]
                ex = wtm_d[:, t4 * 8:(t4 + 1) * 8]
                mx = g_d[:, 96 + t4: 97 + t4]
                wt("vector", eV, evd[("g", mt, r, t4)])
                V(lambda v, gj=gj, ge=ge: v.tensor_tensor(
                    cmp_d[:, 0:64].rearrange("p (a b) -> p a b", a=8), gj, ge, ALU.is_gt))
                V(lambda v, gj=gj, ge=ge: v.tensor_tensor(
                    cmp_d[:, 64:128].rearrange("p (a b) -> p a b", a=8), gj, ge, ALU.is_equal).then_inc(eV.h, 1))
                c1 = eV.bump()
                wt("vector", eV, c1)
                V(lambda v: v.tensor_tensor(cmp_d[:, 64:128], cmp_d[:, 64:128], ltm_t[:, :], ALU.mult).then_inc(eV.h, 1))
                c2 = eV.bump()
                wt("vector", eV, c2)
                V(lambda v: v.tensor_tensor(cmp_d[:, 0:64], cmp_d[:, 0:64], cmp_d[:, 64:128], ALU.add).then_inc(eV.h, 1))
                c3 = eV.bump()
                wt("vector", eV, c3)
                V(lambda v, rk=rk: v.reduce_sum(rk, cmp_d[:, 0:64].rearrange("p (a b) -> p a b", a=8),
                                                axis=AX.X).then_inc(eV.h, 1))
                c4 = eV.bump()
                wt("vector", eV, c4)
                V(lambda v, rk=rk, mk=mk: v.tensor_scalar(mk, rk, 6.0, None, ALU.is_lt).then_inc(eV.h, 1))
                c5 = eV.bump()
                wt("vector", eV, c5)
                V(lambda v, mk=mk, ex=ex: v.tensor_scalar(ex, mk, 1.0, 1e30, ALU.subtract, ALU.mult).then_inc(eV.h, 1))
                c6 = eV.bump()
                wt("vector", eV, c6)
                V(lambda v, gv=gv, ex=ex: v.scalar_tensor_tensor(ex, gv, 1.0, ex, ALU.mult, ALU.add).then_inc(eV.h, 1))
                c7 = eV.bump()
                wt("vector", eV, c7)
                V(lambda v, ex=ex, mx=mx: v.reduce_max(mx, ex, axis=AX.X).then_inc(eV.h, 1))
                c8 = eV.bump()
                wt("vector", eV, c8)
                V(lambda v, ex=ex, mx=mx: v.tensor_scalar(ex, ex, mx[:, 0:1], None, ALU.subtract).then_inc(eV.h, 1))
                evd[("gp1", mt, r, t4)] = eV.bump()
            for mg in range(2):
                vwA, siA = slab([((kc * 1024, (kc + 1) * 1024),
                                  z_su[kc * 128:(kc + 1) * 128, mg * 1024:(mg + 1) * 1024]) for kc in range(8)])
                vwB, siB = slab([((kc * 1024, (kc + 1) * 1024),
                                  z_su[1024 + kc * 128: 1024 + (kc + 1) * 128, mg * 1024:(mg + 1) * 1024]) for kc in range(8)])
                wt("tensor", qS, slab_load[siA])
                for m8 in range(8):
                    m16 = mg * 8 + m8
                    bank = psD[m8 % 4]
                    if evd.get(("su", mt, r, m16 - 4)) is not None:
                        wt("tensor", eS, evd[("su", mt, r, m16 - 4)])
                    for kc in range(8):
                        T(lambda t, b=bank, vw=vwA, kc=kc, m8=m8, xt=xt:
                          t.matmul(b[:, :], vw[:, kc * 1024 + m8 * 128: kc * 1024 + m8 * 128 + 128],
                                   xt[:, kc * 512:(kc + 1) * 512], start=(kc == 0), stop=False))
                    if m8 == 0:
                        wt("tensor", qS, slab_load[siB])
                    for kc in range(8):
                        T(lambda t, b=bank, vw=vwB, kc=kc, m8=m8, xt=xt:
                          t.matmul(b[:, :], vw[:, kc * 1024 + m8 * 128: kc * 1024 + m8 * 128 + 128],
                                   xt[:, (8 + kc) * 512:(8 + kc + 1) * 512], start=False, stop=(kc == 7)))
                    suv = tnop_inc()
                    wt("scalar", eT, suv)
                    S(lambda s, m16=m16, b=bank: s.activation(sh_v[:, m16 * 512:(m16 + 1) * 512], b[:, :],
                                                              AF.Identity, bias=bias_cols[:, BSU0 + m16: BSU0 + m16 + 1]).then_inc(eS.h, 1))
                    evd[("su", mt, r, m16)] = eS.bump()
                slab_end(siA, eT.v)
                slab_end(siB, eT.v)
            if r == 0 and mt > 0:
                emit_final(mt - 1)
            for t4 in range(4):
                ex = wtm_d[:, t4 * 8:(t4 + 1) * 8]
                sm = g_d[:, 100 + t4: 101 + t4]
                wt("scalar", eV, evd[("gp1", mt, r, t4)])
                S(lambda s, ex=ex: s.activation(ex, ex, AF.Exp).then_inc(eS.h, 1))
                c10 = eS.bump()
                wt("vector", eS, c10)
                V(lambda v, ex=ex, sm=sm: v.reduce_sum(sm, ex, axis=AX.X).then_inc(eV.h, 1))
                c11 = eV.bump()
                wt("vector", eV, c11)
                V(lambda v, sm=sm: v.reciprocal(sm, sm).then_inc(eV.h, 1))
                c12 = eV.bump()
                wt("vector", eV, c12)
                V(lambda v, ex=ex, sm=sm: v.tensor_scalar(ex, ex, sm[:, 0:1], None, ALU.mult).then_inc(eV.h, 1))
                evd[("w", mt, r, t4)] = eV.bump()
            vw, si_ = slab([((kc * 1024, (kc + 1) * 1024),
                             z_sa[kc * 128:(kc + 1) * 128, :]) for kc in range(8)])
            wt("tensor", qS, slab_load[si_])
            wt("tensor", eS, evd[("su", mt, r, 15)])
            for sslot in range(2):
                for m8 in range(8):
                    bank = psD[4 + (m8 % 2)]
                    if evd.get(("sa", mt, r, sslot, m8 - 2)) is not None:
                        wt("tensor", eV, evd[("sa", mt, r, sslot, m8 - 2)])
                    elif sslot == 1 and m8 < 2:
                        wt("tensor", eV, evd[("sa", mt, r, 0, 6 + m8)])
                    for kc in range(8):
                        T(lambda t, b=bank, vw=vw, kc=kc, m8=m8, sslot=sslot:
                          t.matmul(b[:, :], vw[:, kc * 1024 + m8 * 128: kc * 1024 + m8 * 128 + 128],
                                   sh_v[:, (sslot * 8 + kc) * 512:(sslot * 8 + kc + 1) * 512],
                                   start=(kc == 0), stop=(kc == 7)))
                    sav = tnop_inc()
                    wt("scalar", eT, sav)
                    S(lambda s, m8=m8, b=bank:
                      s.activation(tmp_d[:, (m8 % 2) * 512:(m8 % 2 + 1) * 512], b[:, :], AF.Silu,
                                   bias=bias_cols[:, BSA0 + m8: BSA0 + m8 + 1]).then_inc(eS.h, 1))
                    sev = eS.bump()
                    wt("vector", eS, sev)
                    V(lambda v, m8=m8, sslot=sslot:
                      v.tensor_tensor(sha_d[:, (sslot * 8 + m8) * 512:(sslot * 8 + m8 + 1) * 512],
                                      tmp_d[:, (m8 % 2) * 512:(m8 % 2 + 1) * 512],
                                      sh_v[:, (sslot * 8 + m8) * 512:(sslot * 8 + m8 + 1) * 512],
                                      ALU.mult).then_inc(eV.h, 1))
                    evd[("sa", mt, r, sslot, m8)] = eV.bump()
            slab_end(si_, eT.v)
            sa_S_end = eS.v
            wt("tensor", eS, sa_S_end)
            wt("tensor", eV, evd[("g", mt, r, 3)])
            for t4 in range(4):
                wt("tensor", eV, evd[("w", mt, r, t4)])
                T(lambda t, t4=t4: t.transpose(psD[5][0:8, t4 * 128:(t4 + 1) * 128],
                                               wtm_d[:, t4 * 8:(t4 + 1) * 8], id_f[:, :]))
            wtr = tnop_inc()
            wt("vector", eT, wtr)
            V(lambda v: v.tensor_copy(wfm_d[:, :], psD[5][0:8, 0:512]).then_inc(eV.h, 1))
            wfm_done = eV.bump()
            wt("tensor", eV, wfm_done)
            for n in range(NEXP):
                T(lambda t, n=n: t.matmul(psD[7][:, :], sel_t[:, n * 128:(n + 1) * 128],
                                          wfm_d[0:8, :], start=True, stop=True))
                bcv = tnop_inc()
                wt("vector", eT, bcv)
                V(lambda v, n=n: v.tensor_copy(wbc_d[:, n * 512:(n + 1) * 512], psD[7][:, :]).then_inc(eV.h, 1))
                bev = eV.bump()
                wt("tensor", eV, bev)
            for n in range(NEXP):
                vwA, siA = slab([((kc * 1024, (kc + 1) * 1024),
                                  z_up[n, kc * 128:(kc + 1) * 128, :]) for kc in range(8)])
                vwB, siB = slab([((kc * 1024, (kc + 1) * 1024),
                                  z_up[n, 1024 + kc * 128: 1024 + (kc + 1) * 128, :]) for kc in range(8)])
                wt("tensor", qS, slab_load[siA])
                if n == 0:
                    wt("tensor", eV, evd[("sa", mt, r, 1, 7)])
                for m8 in range(8):
                    bank = psD[m8 % 4]
                    if evd.get(("up", mt, r, n, m8 - 4)) is not None:
                        wt("tensor", eS, evd[("up", mt, r, n, m8 - 4)])
                    elif n == 0 and m8 < 4:
                        wt("tensor", eS, evd[("su", mt, r, 12 + m8)])
                    elif n > 0 and m8 < 4:
                        wt("tensor", eS, evd[("up", mt, r, n - 1, 4 + m8)])
                    for kc in range(8):
                        T(lambda t, b=bank, vw=vwA, kc=kc, m8=m8, xt=xt:
                          t.matmul(b[:, :], vw[:, kc * 1024 + m8 * 128: kc * 1024 + m8 * 128 + 128],
                                   xt[:, kc * 512:(kc + 1) * 512], start=(kc == 0), stop=False))
                    if m8 == 0:
                        wt("tensor", qS, slab_load[siB])
                    for kc in range(8):
                        T(lambda t, b=bank, vw=vwB, kc=kc, m8=m8, xt=xt:
                          t.matmul(b[:, :], vw[:, kc * 1024 + m8 * 128: kc * 1024 + m8 * 128 + 128],
                                   xt[:, (8 + kc) * 512:(8 + kc + 1) * 512], start=False, stop=(kc == 7)))
                    upv = tnop_inc()
                    wt("scalar", eT, upv)
                    S(lambda s, m8=m8, n=n: s.activation(h_v[:, m8 * 512:(m8 + 1) * 512], psD[m8 % 4][:, :],
                                                         AF.Identity, bias=bias_cols[:, BUP0 + n * 8 + m8: BUP0 + n * 8 + m8 + 1]).then_inc(eS.h, 1))
                    evd[("up", mt, r, n, m8)] = eS.bump()
                slab_end(siA, eT.v)
                slab_end(siB, eT.v)
                vw, si_ = slab([((kc * 1024, (kc + 1) * 1024),
                                 z_sw[n, kc * 128:(kc + 1) * 128, :]) for kc in range(8)])
                wt("tensor", qS, slab_load[si_])
                wt("tensor", eS, evd[("up", mt, r, n, 7)])
                for m8 in range(8):
                    bank = psD[4 + (m8 % 2)]
                    if evd.get(("sw", mt, r, n, m8 - 2)) is not None:
                        wt("tensor", eV, evd[("sw", mt, r, n, m8 - 2)])
                    elif n == 0 and m8 < 2:
                        wt("tensor", eS, sa_S_end)
                    elif n > 0 and m8 < 2:
                        wt("tensor", eV, evd[("sw", mt, r, n - 1, 6 + m8)])
                    for kc in range(8):
                        T(lambda t, b=bank, vw=vw, kc=kc, m8=m8:
                          t.matmul(b[:, :], vw[:, kc * 1024 + m8 * 128: kc * 1024 + m8 * 128 + 128],
                                   h_v[:, kc * 512:(kc + 1) * 512], start=(kc == 0), stop=(kc == 7)))
                    swv = tnop_inc()
                    wt("scalar", eT, swv)
                    S(lambda s, m8=m8, n=n: s.activation(si_v[:, m8 * 512:(m8 + 1) * 512], psD[4 + (m8 % 2)][:, :],
                                                         AF.Silu, bias=bias_cols[:, BSW0 + n * 8 + m8: BSW0 + n * 8 + m8 + 1]).then_inc(eS.h, 1))
                    sev2 = eS.bump()
                    wt("vector", eS, sev2)
                    V(lambda v, m8=m8: v.tensor_tensor(si_v[:, m8 * 512:(m8 + 1) * 512],
                                                       si_v[:, m8 * 512:(m8 + 1) * 512],
                                                       h_v[:, m8 * 512:(m8 + 1) * 512], ALU.mult).then_inc(eV.h, 1))
                    h2a = eV.bump()
                    wt("vector", eV, h2a)
                    V(lambda v, m8=m8, n=n: v.tensor_tensor(h2w_d[:, (n * 8 + m8) * 512:(n * 8 + m8 + 1) * 512],
                                                            si_v[:, m8 * 512:(m8 + 1) * 512],
                                                            wbc_d[:, n * 512:(n + 1) * 512], ALU.mult).then_inc(eV.h, 1))
                    evd[("sw", mt, r, n, m8)] = eV.bump()
                slab_end(si_, eT.v)
            fence_T()
            tmp_used = {}
            for oc in range(2):
                for n in range(NEXP):
                    vw, si_ = slab([((kc * 1024, (kc + 1) * 1024),
                                     z_dn[n, oc, kc * 128:(kc + 1) * 128, :]) for kc in range(8)])
                    wt("tensor", qS, slab_load[si_])
                    if n == 0 and oc == 1:
                        wt("tensor", eS, evd[("dnS", mt, r, 0)])
                    for ht in range(8):
                        for kc in range(8):
                            T(lambda t, b=psD[ht], vw=vw, kc=kc, ht=ht, n=n:
                              t.matmul(b[:, :], vw[:, kc * 1024 + ht * 128: kc * 1024 + ht * 128 + 128],
                                       h2w_d[:, (n * 8 + kc) * 512:(n * 8 + kc + 1) * 512],
                                       start=(n == 0 and kc == 0), stop=False))
                    slab_end(si_)
                for half in range(2):
                    vw, si_ = slab([((kc * 1024, (kc + 1) * 1024),
                                     z_sd[oc, half * 1024 + kc * 128: half * 1024 + (kc) * 128 + 128, :]) for kc in range(8)])
                    wt("tensor", qS, slab_load[si_])
                    for ht in range(8):
                        for kc in range(8):
                            T(lambda t, b=psD[ht], vw=vw, kc=kc, ht=ht, half=half:
                              t.matmul(b[:, :], vw[:, kc * 1024 + ht * 128: kc * 1024 + ht * 128 + 128],
                                       sha_d[:, (half * 8 + kc) * 512:(half * 8 + kc + 1) * 512],
                                       start=False, stop=False))
                    slab_end(si_)
                for ht in range(8):
                    T(lambda t, ht=ht, oc=oc: t.matmul(psD[ht][:, :],
                                                       bdn_bf[0:6, oc * 1024 + ht * 128: oc * 1024 + ht * 128 + 128],
                                                       wfm_d[0:6, :], start=False, stop=True))
                dn_done = tnop_inc()
                for ht in range(8):
                    hglob = oc * 8 + ht
                    tsl = ht % 2
                    wt("scalar", eT, dn_done)
                    if r == 0:
                        if tmp_used.get(tsl) is not None:
                            wt("scalar", eV, tmp_used[tsl])
                        S(lambda s, ht=ht, hg=hglob, tsl=tsl:
                          s.activation(tmp_d[:, tsl * 512:(tsl + 1) * 512], psD[ht][:, :],
                                       AF.Identity, bias=bias_cols[:, BSD0 + hg: BSD0 + hg + 1]).then_inc(eS.h, 1))
                        sev3 = eS.bump()
                        wt("vector", eS, sev3)
                        V(lambda v, hg=hglob, tsl=tsl: v.tensor_tensor(xt1_d[:, hg * 512:(hg + 1) * 512],
                                                                       tmp_d[:, tsl * 512:(tsl + 1) * 512],
                                                                       xfm2_d[:, hg * 512:(hg + 1) * 512],
                                                                       ALU.add).then_inc(eV.h, 1))
                        tmp_used[tsl] = eV.bump()
                    else:
                        if o1s_used[tsl] is not None:
                            wt("scalar", qP, o1s_used[tsl])
                        S(lambda s, ht=ht, hg=hglob, tsl=tsl:
                          s.activation(o1s_d[:, tsl * 512:(tsl + 1) * 512], psD[ht][:, :],
                                       AF.Identity, bias=bias_cols[:, BSD0 + hg: BSD0 + hg + 1]).then_inc(eS.h, 1))
                        sev3 = eS.bump()
                        wt("scalar", eS, sev3)
                        o1s_used[tsl] = dma("scalar", s_o1[osl, hglob * 128:(hglob + 1) * 128, :],
                                            o1s_d[:, tsl * 512:(tsl + 1) * 512], qP)
                evd[("dnS", mt, r, oc)] = eS.v
            if r == 0:
                xt1_ready = eV.v
            else:
                o1_store_done[mt] = qP.v
    emit_final(NMT - 1)
    bar("PD_ffn")
    finish()
    return nc


_CACHE = {}


def _get_prog(dbg=()):
    key = tuple(dbg)
    if key not in _CACHE:
        _CACHE[key] = build(key)
    return _CACHE[key]


def make_in_maps(inputs):
    x = np.asarray(inputs["L0_attn_x"], dtype=np.float32)
    base = {
        "slope": np.ascontiguousarray(np.asarray(inputs["L0_attn_slope_rate"], np.float32).reshape(1, HQ)),
        "wqT": np.ascontiguousarray(np.asarray(inputs["L0_attn_W_qkvu"], np.float32).T),
        "woT": np.ascontiguousarray(np.asarray(inputs["L0_attn_W_out"], np.float32).T),
        "wg_in": np.ascontiguousarray(inputs["L0_ffn_Wg"], np.float32),
        "bg_in": np.ascontiguousarray(inputs["L0_ffn_bg"], np.float32),
        "wup_in": np.ascontiguousarray(inputs["L0_ffn_Wup"], np.float32),
        "bup_in": np.ascontiguousarray(inputs["L0_ffn_bup"], np.float32),
        "wsw_in": np.ascontiguousarray(inputs["L0_ffn_Wsw"], np.float32),
        "bsw_in": np.ascontiguousarray(inputs["L0_ffn_bsw"], np.float32),
        "wdn_in": np.ascontiguousarray(inputs["L0_ffn_Wdn"], np.float32),
        "bdn_in": np.ascontiguousarray(inputs["L0_ffn_bdn"], np.float32),
        "wsu_in": np.ascontiguousarray(inputs["L0_ffn_Wsu"], np.float32),
        "bsu_in": np.ascontiguousarray(np.asarray(inputs["L0_ffn_bsu"], np.float32).reshape(1, SED)),
        "wsa_in": np.ascontiguousarray(inputs["L0_ffn_Wsa"], np.float32),
        "bsa_in": np.ascontiguousarray(np.asarray(inputs["L0_ffn_bsa"], np.float32).reshape(1, E)),
        "wsd_in": np.ascontiguousarray(inputs["L0_ffn_Wsd"], np.float32),
        "bsd_in": np.ascontiguousarray(np.asarray(inputs["L0_ffn_bsd"], np.float32).reshape(1, D)),
    }
    in_maps = []
    for c in range(8):
        b, pos = c // 4, c % 4
        main = x[b, pos * MAIN:(pos + 1) * MAIN]
        if pos == 0:
            lbp = np.zeros((LB, D), np.float32)
        else:
            lbp = x[b, pos * MAIN - LB: pos * MAIN]
        m = dict(base)
        m["x_ext"] = np.ascontiguousarray(np.concatenate([lbp, main], axis=0))
        in_maps.append(m)
    return in_maps


def kernel(**inputs):
    nc = _get_prog()
    in_maps = make_in_maps(inputs)
    res = run_bass_kernel_spmd(nc, in_maps, core_ids=list(range(8)))
    out = np.empty((2, 16384, D), np.float32)
    for c in range(8):
        out[c // 4, (c % 4) * MAIN:((c % 4) + 1) * MAIN] = res.results[c]["y"]
    return out


# version 08
# speedup vs baseline: 1.1297x; 1.0957x over previous
import numpy as np
import ml_dtypes
import concourse.bass as bass
import concourse.mybir as mybir
from concourse.bass_utils import run_bass_kernel_spmd
from contextlib import ExitStack

F32 = mybir.dt.float32
BF16 = mybir.dt.bfloat16
AF = mybir.ActivationFunctionType
ALU = mybir.AluOpType
AX = mybir.AxisListType
BF = ml_dtypes.bfloat16

D = 2048
HQ = 16
BLK = 256
MAIN = 4096
LB = 2048
EXT = MAIN + LB
NBM = MAIN // BLK
NBL = LB // BLK
NBE = NBM + NBL
E = 1024
NEXP = 6
NE = 8
SED = 2048
MT = 512
NMT = MAIN // MT

KEYS = ("tensor", "vector", "scalar", "sync", "gpsimd")


class SemC:
    __slots__ = ("h", "v")

    def __init__(self, h):
        self.h = h
        self.v = 0

    def bump(self, n=1):
        self.v += n
        return self.v


def build(dbg=(), stop=None):
    import os as _os0
    nc = bass.Bass("TRN2", target_bir_lowering=False, detect_race_conditions=(_os0.environ.get("NORACE") is None))
    top = ExitStack()
    Et = top.enter_context

    x_ext = nc.dram_tensor("x_ext", [EXT, D], F32, kind="ExternalInput").ap()
    slope = nc.dram_tensor("slope", [1, HQ], F32, kind="ExternalInput").ap()
    wqT = nc.dram_tensor("wqT", [D, 4 * D], F32, kind="ExternalInput").ap()
    woT = nc.dram_tensor("woT", [D, D], F32, kind="ExternalInput").ap()
    wg_in = nc.dram_tensor("wg_in", [2, D, NE], F32, kind="ExternalInput").ap()
    bg_in = nc.dram_tensor("bg_in", [2, NE], F32, kind="ExternalInput").ap()
    wup_in = nc.dram_tensor("wup_in", [NEXP, D, E], F32, kind="ExternalInput").ap()
    bup_in = nc.dram_tensor("bup_in", [NEXP, E], F32, kind="ExternalInput").ap()
    wsw_in = nc.dram_tensor("wsw_in", [NEXP, E, E], F32, kind="ExternalInput").ap()
    bsw_in = nc.dram_tensor("bsw_in", [NEXP, E], F32, kind="ExternalInput").ap()
    wdn_in = nc.dram_tensor("wdn_in", [NEXP, E, D], F32, kind="ExternalInput").ap()
    bdn_in = nc.dram_tensor("bdn_in", [NEXP, D], F32, kind="ExternalInput").ap()
    wsu_in = nc.dram_tensor("wsu_in", [D, SED], F32, kind="ExternalInput").ap()
    bsu_in = nc.dram_tensor("bsu_in", [1, SED], F32, kind="ExternalInput").ap()
    wsa_in = nc.dram_tensor("wsa_in", [E, E], F32, kind="ExternalInput").ap()
    bsa_in = nc.dram_tensor("bsa_in", [1, E], F32, kind="ExternalInput").ap()
    wsd_in = nc.dram_tensor("wsd_in", [SED, D], F32, kind="ExternalInput").ap()
    bsd_in = nc.dram_tensor("bsd_in", [1, D], F32, kind="ExternalInput").ap()
    y_out = nc.dram_tensor("y", [MAIN, D], F32, kind="ExternalOutput").ap()

    s_xln = nc.dram_tensor("s_xln", [EXT, D], BF16).ap()
    s_qfm = nc.dram_tensor("s_qfm", [D, MAIN], BF16).ap()
    s_kfm = nc.dram_tensor("s_kfm", [D, MAIN], BF16).ap()
    s_ktm = nc.dram_tensor("s_ktm", [HQ, 128, EXT // 128, 128], BF16).ap()
    s_vtm = nc.dram_tensor("s_vtm", [HQ, 128, EXT // 128, 128], BF16).ap()
    s_ufm = nc.dram_tensor("s_ufm", [D, MAIN], BF16).ap()
    s_ofm = nc.dram_tensor("s_ofm", [D, MAIN], F32).ap()
    s_xr2 = nc.dram_tensor("s_xr2", [MAIN, D], F32).ap()
    s_xl2 = nc.dram_tensor("s_xl2", [MAIN, D], BF16).ap()
    s_o1 = nc.dram_tensor("s_o1", [2, D, MT], BF16).ap()
    z_up = nc.dram_tensor("z_up", [NEXP, D, E], BF16).ap()
    z_sw = nc.dram_tensor("z_sw", [NEXP, E, E], BF16).ap()
    z_dn = nc.dram_tensor("z_dn", [NEXP, 2, E, E], BF16).ap()
    z_su = nc.dram_tensor("z_su", [D, SED], BF16).ap()
    z_sa = nc.dram_tensor("z_sa", [E, E], BF16).ap()
    z_sd = nc.dram_tensor("z_sd", [2, SED, E], BF16).ap()

    scratch_map = {
        "s_xln": s_xln, "s_qfm": s_qfm, "s_kfm": s_kfm, "s_ktm": s_ktm,
        "s_vtm": s_vtm, "s_ufm": s_ufm, "s_ofm": s_ofm, "s_xr2": s_xr2,
        "s_xl2": s_xl2,
    }
    dbg_outs = {}
    for name in dbg:
        t = scratch_map[name]
        dbg_outs[name] = nc.dram_tensor("dbg_" + name, list(t.shape),
                                        t.dtype, kind="ExternalOutput").ap()

    il_id_bf = nc.inline_tensor(np.eye(128, dtype=BF), "il_id_bf").ap()
    il_id_f = nc.inline_tensor(np.eye(128, dtype=np.float32), "il_id_f").ap()
    il_ones_r_f = nc.inline_tensor(np.ones((1, 128), np.float32), "il_ones_r_f").ap()
    il_ones_c_f = nc.inline_tensor(np.ones((128, 1), np.float32), "il_ones_c_f").ap()
    il_ar1_r = nc.inline_tensor(np.arange(1, 257, dtype=np.float32)[None, :], "il_ar1_r").ap()
    il_ar0_c = nc.inline_tensor(np.arange(0, 128, dtype=np.float32)[:, None], "il_ar0_c").ap()
    ltm_np = np.zeros((1, 64), np.float32)
    for e_ in range(8):
        for j_ in range(8):
            if j_ < e_:
                ltm_np[0, e_ * 8 + j_] = 1.0
    il_ltm = nc.inline_tensor(ltm_np, "il_ltm").ap()
    sel_np = np.zeros((8, NEXP * 128), BF)
    for n_ in range(NEXP):
        sel_np[n_, n_ * 128:(n_ + 1) * 128] = 1.0
    il_sel = nc.inline_tensor(sel_np, "il_sel").ap()

    id_bf = Et(nc.sbuf_tensor("c_id_bf", [128, 128], BF16))
    id_f = Et(nc.sbuf_tensor("c_id_f", [128, 128], F32))
    ones_r_f = Et(nc.sbuf_tensor("c_ones_r_f", [1, 128], F32))
    ones_c_f = Et(nc.sbuf_tensor("c_ones_c_f", [128, 1], F32))
    ar0_c = Et(nc.sbuf_tensor("c_ar0_c", [128, 1], F32))
    ltm_t = Et(nc.sbuf_tensor("c_ltm", [128, 64], F32))
    sel_t = Et(nc.sbuf_tensor("c_sel", [8, NEXP * 128], BF16))
    sigb = Et(nc.sbuf_tensor("c_sigb", [128, HQ], F32))
    lamb = Et(nc.sbuf_tensor("c_lamb", [128, HQ], F32))
    kdc = Et(nc.sbuf_tensor("c_kdc", [128, 2 * HQ], F32))
    ar1_bc = Et(nc.sbuf_tensor("c_ar1_bc", [128, 256], F32))
    dc_t = Et(nc.sbuf_tensor("c_dc", [128, 512], F32))
    msk_t = Et(nc.sbuf_tensor("c_msk", [128, 512], F32))
    eps5 = Et(nc.sbuf_tensor("c_eps5", [128, 1], F32))
    eps6 = Et(nc.sbuf_tensor("c_eps6", [128, 1], F32))
    bias_cols = Et(nc.sbuf_tensor("c_bias_cols", [128, 232], F32))
    bdn_bf = Et(nc.sbuf_tensor("c_bdn_bf", [NEXP, D], BF16))
    bg_bc = Et(nc.sbuf_tensor("c_bg_bc", [128, 2 * NE], F32))
    wg_sb = Et(nc.sbuf_tensor("c_wg_sb", [128, 2 * 16 * NE], BF16))
    BUP0, BSW0, BDN0, BSU0, BSA0, BSD0 = 0, 48, 96, 192, 208, 216

    qS = SemC(Et(nc.semaphore("sem_qS")))
    qP = SemC(Et(nc.semaphore("sem_qP")))
    qG = SemC(Et(nc.semaphore("sem_qG")))
    qY = SemC(Et(nc.semaphore("sem_qY")))
    eT = SemC(Et(nc.semaphore("sem_eT")))
    eV = SemC(Et(nc.semaphore("sem_eV")))
    eS = SemC(Et(nc.semaphore("sem_eS")))

    rec = {k: [] for k in KEYS}

    def _emit(key, f):
        rec[key].append(f)

    def T(f):
        _emit("tensor", f)

    def V(f):
        _emit("vector", f)

    def S(f):
        _emit("scalar", f)

    EMIT = {k: (lambda f, kk=k: _emit(kk, f)) for k in KEYS}

    def wt(key, sem, tgt):
        if tgt is None or tgt <= 0:
            return
        EMIT[key](lambda e, h=sem.h, v=tgt: e.wait_ge(h, v))

    def dma(key, out_ap, in_ap, sem, **kw):
        tgt = sem.bump(16)
        EMIT[key](lambda e, o=out_ap, i=in_ap, h=sem.h, k=dict(kw):
                  e.dma_start(out=o, in_=i, **k).then_inc(h, 16))
        return tgt

    def tnop_inc():
        f = rec["tensor"].pop()
        rec["tensor"].append(lambda e, f=f: f(e).then_inc(eT.h, 1))
        return eT.bump()

    def vinc_last():
        f = rec["vector"].pop()
        rec["vector"].append(lambda e, f=f: f(e).then_inc(eV.h, 1))
        return eV.bump()

    def flush(tag):
        nonlocal rec
        with nc.Block(tag) as blk:
            for key in KEYS:
                lst = rec[key]
                if lst:
                    getattr(blk, key)(lambda eng, L=tuple(lst): [f(eng) for f in L])
        rec = {k: [] for k in KEYS}

    nc._dbg_sems = {"qS": qS, "qP": qP, "qG": qG, "qY": qY, "eT": eT, "eV": eV, "eS": eS}

    def bar(tag, wait_qC=True):
        wt("sync", qS, qS.v)
        wt("scalar", qP, qP.v)
        wt("gpsimd", qG, qG.v)
        wt("gpsimd", qY, qY.v)
        if wait_qC:
            try:
                wt("gpsimd", qC, qC.v)
            except NameError:
                pass
        flush(tag)
        nc.all_engine_barrier()

    def finish():
        if dbg_outs:
            for name, out_ap in dbg_outs.items():
                t = scratch_map[name]
                dma("gpsimd", out_ap, t, qY)
            wt("gpsimd", qY, qY.v)
            flush("dbg")
            nc.all_engine_barrier()
        wt("gpsimd", qY, qY.v)
        flush("fin")

    p0 = ExitStack()
    E0 = p0.enter_context
    ar1_r = E0(nc.sbuf_tensor("c_ar1_r", [1, 256], F32))
    slp_r = E0(nc.sbuf_tensor("c_slp_r", [1, HQ], F32))
    bgrow = E0(nc.sbuf_tensor("c_bgrow", [1, 2 * NE], F32))
    dma("sync", id_bf[:, :], il_id_bf, qS)
    dma("sync", id_f[:, :], il_id_f, qS)
    dma("sync", ones_r_f[:, :], il_ones_r_f, qS)
    dma("sync", ones_c_f[:, :], il_ones_c_f, qS)
    dma("sync", ar1_r[:, :], il_ar1_r, qS)
    dma("sync", ar0_c[:, :], il_ar0_c, qS)
    dma("sync", ltm_t[:, :], il_ltm.broadcast_to([128, 64]), qS)
    dma("sync", sel_t[:, :], il_sel, qS)
    dma("sync", slp_r[:, :], slope, qS)
    dma("sync", bgrow[:, :], bg_in.rearrange("a b -> (a b)").unsqueeze(0), qS)
    dma("gpsimd", bdn_bf[:, :], bdn_in, qG)
    for j in range(2):
        for c in range(16):
            dma("gpsimd", wg_sb[:, (j * 16 + c) * NE:(j * 16 + c + 1) * NE],
                wg_in[j, c * 128:(c + 1) * 128, :], qG)
    ld_all = qS.v
    wt("vector", qS, ld_all)
    wt("vector", qG, qG.v)
    V(lambda v: v.memset(eps5[:, :], 1e-5))
    V(lambda v: v.memset(eps6[:, :], 1e-6).then_inc(eV.h, 1))
    eV.bump()

    if stop == "S0":
        bar("S0")
        finish()
        return nc
    psP = E0(nc.psum_tensor("psP0", [128, 512], F32))
    psP2 = E0(nc.psum_tensor("psP0b", [128, 512], F32))

    wt("tensor", qS, ld_all)
    T(lambda t: t.matmul(psP[:, 0:HQ], ones_r_f[0:1, :], slp_r[0:1, :], start=True, stop=True))
    for j in range(2):
        T(lambda t, j=j: t.matmul(psP[:, 32 + j * NE:32 + (j + 1) * NE], ones_r_f[0:1, :],
                                  bgrow[0:1, j * NE:(j + 1) * NE], start=True, stop=True))
    T(lambda t: t.matmul(psP2[:, 0:256], ones_r_f[0:1, :], ar1_r[0:1, :], start=True, stop=True))
    t_p0 = tnop_inc()
    wt("vector", eT, t_p0)
    V(lambda v: v.tensor_copy(sigb[:, :], psP[:, 0:HQ]))
    V(lambda v: v.tensor_copy(bg_bc[:, :], psP[:, 32:32 + 2 * NE]))
    V(lambda v: v.tensor_copy(ar1_bc[:, :], psP2[:, 0:256]).then_inc(eV.h, 1))
    v_p0 = eV.bump()
    wt("vector", eV, v_p0)
    V(lambda v: v.tensor_scalar(lamb[:, :], sigb[:, :], -256.0, None, ALU.mult))
    for h in range(HQ):
        for par in range(2):
            col = kdc[:, h * 2 + par: h * 2 + par + 1]
            V(lambda v, c=col, p=par: v.tensor_scalar(c, ar0_c[:, :], -1.0, float(255 - p * 128), ALU.mult, ALU.add))
    v_p1 = vinc_last()
    wt("vector", eV, v_p1)
    for h in range(HQ):
        for par in range(2):
            col = kdc[:, h * 2 + par: h * 2 + par + 1]
            V(lambda v, c=col, hh=h: v.tensor_tensor(c, c, sigb[:, hh:hh + 1], ALU.mult))
    for par in range(2):
        sl = slice(par * 256, (par + 1) * 256)
        V(lambda v, sl=sl, p=par: v.tensor_scalar(dc_t[:, sl], ar1_bc[:, :], ar0_c[:, 0:1],
                                                  float(-1.0 - p * 128), ALU.subtract, ALU.add))
    v_p2 = vinc_last()
    wt("vector", eV, v_p2)
    for par in range(2):
        sl = slice(par * 256, (par + 1) * 256)
        V(lambda v, sl=sl: v.tensor_scalar(msk_t[:, sl], dc_t[:, sl], 0.0, None, ALU.is_ge))
    v_p3 = vinc_last()
    wt("vector", eV, v_p3)
    for par in range(2):
        sl = slice(par * 256, (par + 1) * 256)
        V(lambda v, sl=sl: v.tensor_scalar_max(dc_t[:, sl], dc_t[:, sl], 0.0))
    v_p4 = vinc_last()
    wt("scalar", eV, v_p4)
    S(lambda s: s.activation(lamb[:, :], lamb[:, :], AF.Exp))
    S(lambda s: s.activation(kdc[:, :], kdc[:, :], AF.Exp, scale=-1.0).then_inc(eS.h, 1))
    eS.bump()

    if stop == "S2":
        bar("S2")
        finish()
        return nc
    bstage = E0(nc.sbuf_tensor("c_bstage", [1, 12288], F32))
    bias_jobs = [
        (bup_in.rearrange("n e -> (n e)").unsqueeze(0), 48, BUP0),
        (bsw_in.rearrange("n e -> (n e)").unsqueeze(0), 48, BSW0),
        (bdn_in.rearrange("n d -> (n d)").unsqueeze(0), 96, BDN0),
        (bsu_in, 16, BSU0),
        (bsa_in, 8, BSA0),
        (bsd_in, 16, BSD0),
    ]
    _kb = _os0.environ.get("KBIAS")
    if _kb is not None:
        bias_jobs = bias_jobs[: int(_kb)]
    wt("tensor", eV, v_p0)
    _bm = _os0.environ.get("BIASMODE", "")
    for bsrc, ncols, base in bias_jobs:
        wt("sync", eT, eT.v)
        if _bm != "mmonly":
            tgt = dma("sync", bstage[:, 0:ncols * 128], bsrc, qS)
            wt("tensor", qS, tgt)
        off0 = 0
        k = 0
        while k < ncols:
            if _bm == "dmaonly":
                break
            n4 = min(4, ncols - k)
            for i in range(n4):
                T(lambda t, o=off0 + (k + i) * 128, g=i:
                  t.matmul(psP2[:, g:g + 1], bstage[0:1, o:o + 128], ones_c_f[0:1, 0:1],
                           start=True, stop=True))
            tv = tnop_inc()
            wt("vector", eT, tv)
            V(lambda v, b=base + k, n=n4: v.tensor_copy(
                bias_cols[:, b:b + n], psP2[:, 0:n]).then_inc(eV.h, 1))
            tvv = eV.bump()
            wt("tensor", eV, tvv)
            k += n4

    if stop == "S3":
        bar("S3")
        finish()
        return nc
    xt0 = E0(nc.sbuf_tensor("a0_x", [128, 2 * D], F32))
    xsq = E0(nc.sbuf_tensor("a0_sq", [128, 2 * D], F32))
    lstg = E0(nc.sbuf_tensor("a0_stg", [128, 2 * D], BF16))
    scol = E0(nc.sbuf_tensor("a0_scol", [128, 8], F32))
    x_used = [None, None]
    st_done = [None, None]
    for i in range(EXT // 128):
        sl = i % 2
        xv = xt0[:, sl * D:(sl + 1) * D]
        sq = xsq[:, sl * D:(sl + 1) * D]
        st = lstg[:, sl * D:(sl + 1) * D]
        c_s1 = scol[:, sl * 4 + 0: sl * 4 + 1]
        c_s2 = scol[:, sl * 4 + 1: sl * 4 + 2]
        c_m = scol[:, sl * 4 + 2: sl * 4 + 3]
        c_v = scol[:, sl * 4 + 3: sl * 4 + 4]
        if x_used[sl] is not None:
            wt("sync", eV, x_used[sl])
        xl = dma("sync", xv, x_ext[i * 128:(i + 1) * 128, :], qS)
        wt("vector", qS, xl)
        V(lambda v, a=xv, b=sq: v.tensor_tensor(b, a, a, ALU.mult).then_inc(eV.h, 1))
        t1 = eV.bump()
        wt("vector", eV, t1)
        V(lambda v, b=sq, c=c_s2: v.reduce_sum(c, b, axis=AX.X))
        V(lambda v, a=xv, c=c_s1: v.reduce_sum(c, a, axis=AX.X).then_inc(eV.h, 1))
        t2 = eV.bump()
        wt("vector", eV, t2)
        V(lambda v, c=c_s1, m=c_m: v.tensor_scalar(m, c, 1.0 / D, None, ALU.mult))
        V(lambda v, c=c_s2, vv=c_v: v.tensor_scalar(vv, c, 1.0 / D, None, ALU.mult).then_inc(eV.h, 1))
        t3 = eV.bump()
        wt("vector", eV, t3)
        V(lambda v, m=c_m, tmp=c_s1: v.tensor_tensor(tmp, m, m, ALU.mult).then_inc(eV.h, 1))
        t4 = eV.bump()
        wt("vector", eV, t4)
        V(lambda v, vv=c_v, tmp=c_s1: v.tensor_tensor(vv, vv, tmp, ALU.subtract).then_inc(eV.h, 1))
        t5 = eV.bump()
        wt("scalar", eV, t5)
        S(lambda s, vv=c_v: s.activation(vv, vv, AF.Ln, bias=eps5[:, 0:1]))
        S(lambda s, vv=c_v: s.activation(vv, vv, AF.Exp, scale=-0.5).then_inc(eS.h, 1))
        t6 = eS.bump()
        if st_done[sl] is not None:
            wt("vector", qP, st_done[sl])
        wt("vector", eS, t6)
        V(lambda v, a=xv, o=st, m=c_m, r=c_v:
          v.tensor_scalar(o, a, m[:, 0:1], r[:, 0:1], ALU.subtract, ALU.mult).then_inc(eV.h, 1))
        x_used[sl] = eV.bump()
        wt("scalar", eV, x_used[sl])
        st_done[sl] = dma("scalar", s_xln[i * 128:(i + 1) * 128, :], st, qP)
    bar("P1_ln1_casts")
    if stop == "P1_ln1_casts":
        finish()
        return nc
    p0.close()

    pa = ExitStack()
    Ea = pa.enter_context
    xfm = Ea(nc.sbuf_tensor("pa_xfm", [128, 16 * MAIN], BF16))
    wsl = Ea(nc.sbuf_tensor("pa_wsl", [128, 2 * 16 * 512], BF16))
    qstg = Ea(nc.sbuf_tensor("pa_stg", [128, 4 * 512], BF16))
    ktst = Ea(nc.sbuf_tensor("pa_ktst", [128, 2 * 512], BF16))
    vst2 = Ea(nc.sbuf_tensor("pa_vst2", [128, 2 * 4 * 512], BF16))
    psA = [Ea(nc.psum_tensor("pa_ps%d" % i, [128, 512], F32)) for i in range(4)]
    psKT = Ea(nc.psum_tensor("pa_pskt", [128, 1024], BF16))

    xfm_tgt = None
    for c in range(16):
        for tt8 in range(8):
            xfm_tgt = dma("sync", xfm[:, c * MAIN + tt8 * 512: c * MAIN + tt8 * 512 + 512],
                          s_xln[LB + tt8 * 512: LB + tt8 * 512 + 512, c * 128:(c + 1) * 128],
                          qS, transpose=True)

    wslot_used = [None, None]

    def emit_fm_slices(slices, xsrc, xstride, first_wait):
        w_used = wslot_used
        ev_done = {}
        stg_free = [None] * 4
        kt_free = [None, None]
        kt_ps = [None, None]
        fw = first_wait
        ntt = xstride // 512
        for si, (wc0, rb, kind, dstt) in enumerate(slices):
            sl = si % 2
            wv = wsl[:, sl * 16 * 512:(sl + 1) * 16 * 512]
            if w_used[sl] is not None:
                wt("gpsimd", eT, w_used[sl])
            wl = None
            for c in range(16):
                wl = dma("gpsimd", wv[:, c * 512:(c + 1) * 512],
                         wqT[c * 128:(c + 1) * 128, wc0:wc0 + 512], qG)
            wt("tensor", qG, wl)
            if fw is not None:
                wt("tensor", qS, fw)
                fw = None
            g = 0
            for m in range(4):
                for tt8 in range(ntt):
                    bank = psA[g % 4]
                    if ev_done.get(g - 4) is not None:
                        wt("tensor", eS, ev_done[g - 4])
                    for c in range(16):
                        T(lambda t, b=bank, w=wv, c=c, m=m, x=xsrc, tt=tt8, stx=xstride:
                          t.matmul(b[:, :], w[:, c * 512 + m * 128: c * 512 + m * 128 + 128],
                                   x[:, c * stx + tt * 512: c * stx + tt * 512 + 512],
                                   start=(c == 0), stop=(c == 15)))
                    mmv = tnop_inc()
                    stg = qstg[:, (g % 4) * 512:(g % 4 + 1) * 512]
                    wt("scalar", eT, mmv)
                    if stg_free[g % 4] is not None:
                        wt("scalar", qP, stg_free[g % 4])
                    if kind == "k":
                        if ev_done.get(g - 4) is not None:
                            wt("scalar", eT, kt_last_T[0] if kt_last_T[0] else 0)
                    fn = AF.Silu if kind in ("q", "k") else AF.Copy
                    S(lambda s, o=stg, b=bank, f=fn: s.activation(o, b[:, :], f).then_inc(eS.h, 1))
                    ev_done[g] = eS.bump()
                    wt("scalar", eS, ev_done[g])
                    stg_free[g % 4] = dma("scalar",
                                          dstt[rb + m * 128: rb + m * 128 + 128, tt8 * 512:(tt8 + 1) * 512],
                                          stg, qP)
                    if kind == "k":
                        hh = (rb + m * 128) // 128
                        ksl = g % 2
                        wt("tensor", eS, ev_done[g])
                        if kt_ps[ksl] is not None:
                            wt("tensor", eV, kt_ps[ksl])
                        for i4 in range(4):
                            T(lambda t, i4=i4, stg=stg, ksl=ksl:
                              t.transpose(psKT[:, ksl * 512 + i4 * 128: ksl * 512 + (i4 + 1) * 128],
                                          stg[:, i4 * 128:(i4 + 1) * 128], id_bf[:, :]))
                        trv = tnop_inc()
                        kt_last_T[0] = trv
                        wt("vector", eT, trv)
                        if kt_free[ksl] is not None:
                            wt("vector", qP, kt_free[ksl])
                        for i4 in range(4):
                            t2g = (LB // 128) + tt8 * 4 + i4
                            par = t2g % 2
                            V(lambda v, i4=i4, ksl=ksl, hh=hh, par=par:
                              v.tensor_scalar(ktst[:, ksl * 512 + i4 * 128: ksl * 512 + (i4 + 1) * 128],
                                              psKT[:, ksl * 512 + i4 * 128: ksl * 512 + (i4 + 1) * 128],
                                              kdc[:, hh * 2 + par: hh * 2 + par + 1], None, ALU.mult))
                        kt_ps[ksl] = vinc_last()
                        wt("scalar", eV, kt_ps[ksl])
                        kt_free[ksl] = dma("scalar",
                                           s_ktm[hh, :, (LB // 128) + tt8 * 4: (LB // 128) + tt8 * 4 + 4, :],
                                           ktst[:, ksl * 512:(ksl + 1) * 512].rearrange("p (t e) -> p t e", t=4),
                                           qP)
                    g += 1
            w_used[sl] = eT.v

    kt_last_T = [None]
    fm_slices = []
    for s4 in range(4):
        fm_slices.append((s4 * 512, s4 * 512, "q", s_qfm))
    for s4 in range(4):
        fm_slices.append((D + s4 * 512, s4 * 512, "k", s_kfm))
    for s4 in range(4):
        fm_slices.append((3 * D + s4 * 512, s4 * 512, "u", s_ufm))
    emit_fm_slices(fm_slices, xfm, MAIN, xfm_tgt)

    def emit_roleswap(xsrc, xstride, t2base, vcol0, dst, do_silu_kd, first_wait):
        w_used = wslot_used
        psum_guard = {}
        bfree = [None, None]
        fw = first_wait
        ntm = (LB if xstride == LB else MAIN) // 128
        for vs in range(4):
            sl = vs % 2
            wv = wsl[:, sl * 16 * 512:(sl + 1) * 16 * 512]
            if w_used[sl] is not None:
                wt("gpsimd", eT, w_used[sl])
            wl = None
            for c in range(16):
                wl = dma("gpsimd", wv[:, c * 512:(c + 1) * 512],
                         wqT[c * 128:(c + 1) * 128, vcol0 + vs * 512: vcol0 + vs * 512 + 512], qG)
            wt("tensor", qG, wl)
            if fw is not None:
                wt("tensor", qS, fw)
            g = 0
            for tm in range(ntm):
                bank = psA[g % 4]
                if psum_guard.get(g - 4) is not None:
                    wt("tensor", psum_guard[g - 4][0], psum_guard[g - 4][1])
                for c in range(16):
                    T(lambda t, b=bank, w=wv, c=c, tm=tm, x=xsrc, stx=xstride:
                      t.matmul(b[:, :], x[:, c * stx + tm * 128: c * stx + tm * 128 + 128],
                               w[:, c * 512:(c + 1) * 512], start=(c == 0), stop=(c == 15)))
                mmv = tnop_inc()
                bsl = (tm // 4) % 2
                vbuf = vst2[:, bsl * 2048 + (tm % 4) * 512: bsl * 2048 + (tm % 4) * 512 + 512]
                wt("scalar", eT, mmv)
                if tm % 4 == 0 and bfree[bsl] is not None:
                    wt("scalar", qP, bfree[bsl])
                fn = AF.Silu if do_silu_kd else AF.Copy
                S(lambda s, o=vbuf, b=bank, f=fn: s.activation(o, b[:, :], f).then_inc(eS.h, 1))
                sv = eS.bump()
                if do_silu_kd:
                    wt("vector", eS, sv)
                    par = tm % 2
                    for hh4 in range(4):
                        V(lambda v, o=vbuf, hh=vs * 4 + hh4, par=par, h4=hh4:
                          v.tensor_scalar(o[:, h4 * 128:(h4 + 1) * 128], o[:, h4 * 128:(h4 + 1) * 128],
                                          kdc[:, hh * 2 + par: hh * 2 + par + 1], None, ALU.mult))
                    done = vinc_last()
                    psum_guard[g] = (eV, done)
                else:
                    done = sv
                    psum_guard[g] = (eS, done)
                if tm % 4 == 3:
                    gsem, dval = psum_guard[g]
                    wt("scalar", gsem, dval)
                    last = None
                    for hh4 in range(4):
                        last = dma("scalar", dst[vs * 4 + hh4, :, t2base + tm - 3: t2base + tm + 1, :],
                                   vst2[:, bsl * 2048: bsl * 2048 + 2048]
                                   .rearrange("p (t e) -> p t e", t=4)[:, :, hh4 * 128:(hh4 + 1) * 128], qP)
                    bfree[bsl] = last
                g += 1
            w_used[sl] = eT.v

    emit_roleswap(xfm, MAIN, LB // 128, 2 * D, s_vtm, False, None)
    bar("PA_main")
    if stop == "PA_main":
        finish()
        return nc

    xlb_tgt = None
    for c in range(16):
        for tt4 in range(4):
            xlb_tgt = dma("sync", xfm[:, c * LB + tt4 * 512: c * LB + tt4 * 512 + 512],
                          s_xln[tt4 * 512: tt4 * 512 + 512, c * 128:(c + 1) * 128],
                          qS, transpose=True)
    emit_roleswap(xfm, LB, 0, D, s_ktm, True, xlb_tgt)
    emit_roleswap(xfm, LB, 0, 2 * D, s_vtm, False, None)
    bar("PA_lb")
    if stop == "PA_lb":
        finish()
        return nc

    pa.close()
    cstk = ExitStack()
    cst = cstk.enter_context(nc.sbuf_tensor("p1_cst", [128, 2 * 2048], BF16))
    wo_sb = cstk.enter_context(nc.sbuf_tensor("pc_wo", [128, 16 * D], BF16))
    qC = SemC(Et(nc.semaphore("sem_qC")))
    qL = SemC(Et(nc.semaphore("sem_qL")))
    pc = ExitStack()
    Ec = pc.enter_context
    qh_b = Ec(nc.sbuf_tensor("pb_qh", [128, 2 * MAIN], BF16))
    kfh_b = Ec(nc.sbuf_tensor("pb_kfh", [128, 2 * MAIN], BF16))
    kth_b = Ec(nc.sbuf_tensor("pb_kth", [128, 2 * 48 * 128], BF16))
    vth_b = Ec(nc.sbuf_tensor("pb_vth", [128, 2 * 48 * 128], BF16))
    oh_b = Ec(nc.sbuf_tensor("pb_oh", [128, 2 * MAIN], F32))
    qd_b = Ec(nc.sbuf_tensor("pb_qd", [128, 2 * 256], F32))
    dd_b = Ec(nc.sbuf_tensor("pb_dd", [128, 2 * 512], BF16))
    tmpd = Ec(nc.sbuf_tensor("pb_tmpd", [128, 512], F32))
    qs_b = Ec(nc.sbuf_tensor("pb_qs", [128, 2 * 256], BF16))
    sc_b = Ec(nc.sbuf_tensor("pb_sc", [128, 2 * 512], BF16))
    S_b = Ec(nc.sbuf_tensor("pb_S", [128, 128], F32))
    Sbf_b = Ec(nc.sbuf_tensor("pb_Sbf", [128, 2 * 128], BF16))
    psSC = [Ec(nc.psum_tensor("pb_psc%d" % i, [128, 512], F32)) for i in range(2)]
    psO = [Ec(nc.psum_tensor("pb_pso%d" % i, [128, 512], F32)) for i in range(2)]
    psAA = [Ec(nc.psum_tensor("pb_psa%d" % i, [128, 512], F32)) for i in range(2)]

    head_T_done = [None] * (HQ + 2)
    head_V_done = [None] * (HQ + 2)
    head_S_done = [None] * (HQ + 2)
    head_spill = [None] * (HQ + 2)
    head_load = [None] * HQ

    def emit_head_load(h):
        hs = h % 2
        if h >= 2:
            wt("sync", eT, head_T_done[h - 2])
            wt("sync", eV, head_V_done[h - 2])
        t1 = dma("sync", qh_b[:, hs * MAIN:(hs + 1) * MAIN], s_qfm[h * 128:(h + 1) * 128, :], qS)
        t1 = dma("sync", kfh_b[:, hs * MAIN:(hs + 1) * MAIN], s_kfm[h * 128:(h + 1) * 128, :], qS)
        t1 = dma("sync", kth_b[:, hs * 48 * 128:(hs + 1) * 48 * 128],
                 s_ktm[h].rearrange("p t e -> p (t e)"), qS)
        t1 = dma("sync", vth_b[:, hs * 48 * 128:(hs + 1) * 48 * 128],
                 s_vtm[h].rearrange("p t e -> p (t e)"), qS)
        head_load[h] = t1

    emit_head_load(0)
    emit_head_load(1)

    cast_jobs = []

    def add_cast(csrc, cdst, rows, cols):
        step = 2 if cols <= 1024 else 1
        for r0 in range(0, rows, 128 * step):
            nchunk = min(step, (rows - r0) // 128)
            sap = csrc[r0:r0 + nchunk * 128, :].rearrange("(c p) e -> p c e", p=128) \
                if nchunk > 1 else csrc[r0:r0 + 128, :]
            dap = cdst[r0:r0 + nchunk * 128, :].rearrange("(c p) e -> p c e", p=128) \
                if nchunk > 1 else cdst[r0:r0 + 128, :]
            cast_jobs.append((sap, dap, nchunk * cols, nchunk, cols))

    for n in range(NEXP):
        add_cast(wup_in[n], z_up[n], D, E)
        add_cast(wsw_in[n], z_sw[n], E, E)
        for oc in range(2):
            add_cast(wdn_in[n][:, oc * E:(oc + 1) * E], z_dn[n, oc], E, E)
    add_cast(wsu_in, z_su, D, SED)
    add_cast(wsa_in, z_sa, E, E)
    for oc in range(2):
        add_cast(wsd_in[:, oc * E:(oc + 1) * E], z_sd[oc], SED, E)

    wo_tgt = None
    for c in range(16):
        wo_tgt = dma("gpsimd", wo_sb[:, c * D:(c + 1) * D], woT[c * 128:(c + 1) * 128, :], qG)

    NSL = 2
    cload = [None] * len(cast_jobs)
    cstore = [None] * len(cast_jobs)
    def cst_view(sl, nchunk, cols):
        v = cst[:, sl * 2048: sl * 2048 + nchunk * cols]
        return v.rearrange("p (c e) -> p c e", c=nchunk) if nchunk > 1 else v

    for i in range(min(NSL, len(cast_jobs))):
        sl = i % NSL
        cload[i] = dma("gpsimd", cst_view(sl, cast_jobs[i][3], cast_jobs[i][4]), cast_jobs[i][0], qL)
    for i, (csrc, cdst, ncols, nchunk, cols) in enumerate(cast_jobs):
        sl = i % NSL
        wt("gpsimd", qL, cload[i])
        cstore[i] = dma("gpsimd", cdst, cst_view(sl, nchunk, cols), qC)
        j = i + NSL
        if j < len(cast_jobs):
            wt("gpsimd", qC, cstore[i])
            cload[j] = dma("gpsimd", cst_view(j % NSL, cast_jobs[j][3], cast_jobs[j][4]), cast_jobs[j][0], qL)
    cast_done_tgt = ("qC", qC.v)

    if stop == "S4":
        bar("S4")
        finish()
        return nc


    for h in range(HQ):
        hs = h % 2
        qh = qh_b[:, hs * MAIN:(hs + 1) * MAIN]
        kfh = kfh_b[:, hs * MAIN:(hs + 1) * MAIN]
        kth = kth_b[:, hs * 48 * 128:(hs + 1) * 48 * 128]
        vth = vth_b[:, hs * 48 * 128:(hs + 1) * 48 * 128]
        oh = oh_b[:, hs * MAIN:(hs + 1) * MAIN]
        qd = qd_b[:, hs * 256:(hs + 1) * 256]
        dd = dd_b[:, hs * 512:(hs + 1) * 512]

        if h >= 1:
            wt("vector", eT, head_T_done[h - 1])
            wt("vector", eS, head_S_done[h - 1])
            wt("tensor", eV, head_V_done[h - 1])
            wt("tensor", eS, head_S_done[h - 1])
        if h >= 2:
            wt("scalar", qP, head_spill[h - 2])
        V(lambda v, qd=qd, h=h: v.tensor_scalar(qd, ar1_bc[:, :], sigb[:, h:h + 1], None, ALU.mult))
        V(lambda v, h=h: v.tensor_scalar(tmpd[:, :], dc_t[:, :], sigb[:, h:h + 1], None, ALU.mult).then_inc(eV.h, 1))
        tgen = eV.bump()
        wt("scalar", eV, tgen)
        S(lambda s, qd=qd: s.activation(qd, qd, AF.Exp, scale=-1.0))
        S(lambda s: s.activation(tmpd[:, :], tmpd[:, :], AF.Exp, scale=-1.0).then_inc(eS.h, 1))
        tgen2 = eS.bump()
        wt("vector", eS, tgen2)
        V(lambda v, dd=dd: v.tensor_tensor(dd, tmpd[:, :], msk_t[:, :], ALU.mult))
        V(lambda v: v.memset(S_b[:, :], 0.0).then_inc(eV.h, 1))
        dd_done = eV.bump()

        wt("tensor", qS, head_load[h])
        wt("vector", qS, head_load[h])

        sc_done = {}
        mask_done = {}
        a_done = {}
        o_done = {}
        oevac_done = {}
        sbf_done = {}
        qs_done = {}
        upd_hist = {-1: dd_done}

        for t in range(NBE):
            par = t % 2
            lb = t < NBL
            if upd_hist.get(t - 2) is not None:
                wt("tensor", eV, upd_hist[t - 2])
            for i2 in range(2):
                T(lambda t_, par=par, kth=kth, vth=vth, t2=t * 2 + i2, i2=i2:
                  t_.matmul(psAA[par][:, 0:128], kth[:, t2 * 128:(t2 + 1) * 128],
                            vth[:, t2 * 128:(t2 + 1) * 128], start=(i2 == 0), stop=(i2 == 1)))
            a_done[t] = tnop_inc()

            if not lb:
                tm = t - NBL
                wt("vector", eS, tgen2)
                if o_done.get(t - 2) is not None:
                    wt("vector", eT, o_done[t - 2])
                V(lambda v, par=par, qh=qh, tm=tm, qd=qd:
                  v.tensor_tensor(qs_b[:, par * 256:(par + 1) * 256], qd,
                                  qh[:, tm * 256:(tm + 1) * 256], ALU.mult).then_inc(eV.h, 1))
                qs_done[t] = eV.bump()
                if mask_done.get(t - 2) is not None:
                    wt("tensor", eV, mask_done[t - 2])
                for i2 in range(2):
                    T(lambda t_, par=par, kfh=kfh, qh=qh, tm=tm, i2=i2:
                      t_.matmul(psSC[par][:, i2 * 256:(i2 + 1) * 256],
                                kfh[:, tm * 256 + i2 * 128: tm * 256 + i2 * 128 + 128],
                                qh[:, tm * 256:(tm + 1) * 256], start=True, stop=True))
                sc_done[t] = tnop_inc()
                wt("vector", eT, sc_done[t])
                if o_done.get(t - 2) is not None:
                    wt("vector", eT, o_done[t - 2])
                for i2 in range(2):
                    V(lambda v, par=par, i2=i2, dd=dd:
                      v.tensor_tensor(sc_b[:, par * 512 + i2 * 256: par * 512 + (i2 + 1) * 256],
                                      psSC[par][:, i2 * 256:(i2 + 1) * 256],
                                      dd[:, i2 * 256:(i2 + 1) * 256], ALU.mult))
                mask_done[t] = vinc_last()
                wt("vector", eV, upd_hist[t - 1])
                if o_done.get(t - 2) is not None:
                    wt("vector", eT, o_done[t - 2])
                V(lambda v, par=par: v.tensor_copy(Sbf_b[:, par * 128:(par + 1) * 128], S_b[:, :]).then_inc(eV.h, 1))
                sbf_done[t] = eV.bump()
                wt("tensor", eV, mask_done[t])
                wt("tensor", eV, sbf_done[t])
                wt("tensor", eV, qs_done[t])
                if oevac_done.get(t - 2) is not None:
                    wt("tensor", eS, oevac_done[t - 2])
                for i2 in range(2):
                    T(lambda t_, par=par, vth=vth, t2=t * 2 + i2, i2=i2:
                      t_.matmul(psO[par][:, 0:256], vth[:, t2 * 128:(t2 + 1) * 128],
                                sc_b[:, par * 512 + i2 * 256: par * 512 + (i2 + 1) * 256],
                                start=(i2 == 0), stop=False))
                T(lambda t_, par=par: t_.matmul(psO[par][:, 0:256], Sbf_b[:, par * 128:(par + 1) * 128],
                                                qs_b[:, par * 256:(par + 1) * 256], start=False, stop=True))
                o_done[t] = tnop_inc()
                wt("scalar", eT, o_done[t])
                S(lambda s, par=par, oh=oh, tm=tm:
                  s.activation(oh[:, tm * 256:(tm + 1) * 256], psO[par][:, 0:256], AF.Copy).then_inc(eS.h, 1))
                oevac_done[t] = eS.bump()

            wt("vector", eT, a_done[t])
            if not lb:
                wt("vector", eV, sbf_done[t])
            V(lambda v, par=par, h=h:
              v.scalar_tensor_tensor(S_b[:, :], S_b[:, :], lamb[:, h:h + 1], psAA[par][:, 0:128],
                                     ALU.mult, ALU.add).then_inc(eV.h, 1))
            upd_hist[t] = eV.bump()

        head_T_done[h] = eT.v
        head_V_done[h] = eV.v
        head_S_done[h] = eS.v
        wt("scalar", eS, oevac_done[NBE - 1])
        head_spill[h] = dma("scalar", s_ofm[h * 128:(h + 1) * 128, :], oh, qP)
        if h + 2 < HQ:
            emit_head_load(h + 2)
    bar("PB_scan", wait_qC=False)
    if stop == "PB_scan":
        finish()
        return nc

    pc.close()
    pd = ExitStack()
    Ed = pd.enter_context
    o_sb = Ed(nc.sbuf_tensor("pc_o", [128, 16 * 512], F32))
    u_sb = Ed(nc.sbuf_tensor("pc_u", [128, 16 * 512], BF16))
    oum = Ed(nc.sbuf_tensor("pc_oum", [128, 16 * 512], BF16))
    sq_c = Ed(nc.sbuf_tensor("pc_sq", [128, 2 * 512], F32))
    xm_c = Ed(nc.sbuf_tensor("pc_xm", [128, 2 * D], F32))
    xmid = Ed(nc.sbuf_tensor("pc_xmid", [128, 2 * D], F32))
    rrow = Ed(nc.sbuf_tensor("pc_rrow", [1, 512], F32))
    l2c = Ed(nc.sbuf_tensor("pc_l2c", [128, 4], F32))
    xl2f = Ed(nc.sbuf_tensor("pc_xl2f", [128, D], F32))
    xl2s = Ed(nc.sbuf_tensor("pc_xl2s", [128, 2 * D], BF16))
    psW = [Ed(nc.psum_tensor("pc_psw%d" % i, [128, 512], F32)) for i in range(4)]
    psQ = Ed(nc.psum_tensor("pc_psq", [128, 512], F32))
    psB2 = Ed(nc.psum_tensor("pc_psb", [128, 512], F32))

    ssqmm = {}
    prev_xm_done = [None]
    oum_used = None
    xm_used = [None, None]
    xmid_used = [None, None]
    stg_used = [None, None]
    for mt in range(NMT):
        wt("vector", eT, eT.v)
        if oum_used is not None:
            wt("sync", eV, oum_used)
        lo = None
        for c in range(16):
            lo = dma("sync", o_sb[:, c * 512:(c + 1) * 512],
                     s_ofm[c * 128:(c + 1) * 128, mt * 512:(mt + 1) * 512], qS)
            lo = dma("sync", u_sb[:, c * 512:(c + 1) * 512],
                     s_ufm[c * 128:(c + 1) * 128, mt * 512:(mt + 1) * 512], qS)
        wt("vector", qS, lo)
        for c in range(16):
            ssl = c % 2
            if ssqmm.get((mt, c - 2)) is not None:
                wt("vector", eT, ssqmm[(mt, c - 2)])
            V(lambda v, c=c, ssl=ssl: v.tensor_tensor(sq_c[:, ssl * 512:(ssl + 1) * 512],
                                                      o_sb[:, c * 512:(c + 1) * 512],
                                                      o_sb[:, c * 512:(c + 1) * 512], ALU.mult).then_inc(eV.h, 1))
            sqd = eV.bump()
            wt("tensor", eV, sqd)
            T(lambda t, c=c, ssl=ssl: t.matmul(psQ[0:1, :], ones_c_f[:, 0:1],
                                               sq_c[:, ssl * 512:(ssl + 1) * 512],
                                               start=(c == 0), stop=(c == 15)))
            ssqmm[(mt, c)] = tnop_inc()
        wt("scalar", eT, ssqmm[(mt, 15)])
        S(lambda s: s.activation(rrow[0:1, :], psQ[0:1, :], AF.Ln, bias=eps6[0:1, 0:1], scale=1.0 / D))
        S(lambda s: s.activation(rrow[0:1, :], rrow[0:1, :], AF.Exp, scale=-0.5).then_inc(eS.h, 1))
        rs_done = eS.bump()
        wt("tensor", eS, rs_done)
        T(lambda t: t.matmul(psB2[:, :], ones_r_f[0:1, :], rrow[0:1, :], start=True, stop=True))
        bc_done = tnop_inc()
        wt("vector", eT, bc_done)
        for c in range(16):
            V(lambda v, c=c: v.tensor_tensor(o_sb[:, c * 512:(c + 1) * 512],
                                             o_sb[:, c * 512:(c + 1) * 512],
                                             u_sb[:, c * 512:(c + 1) * 512], ALU.mult))
        ou1 = vinc_last()
        wt("vector", eV, ou1)
        for c in range(16):
            V(lambda v, c=c: v.tensor_tensor(oum[:, c * 512:(c + 1) * 512],
                                             o_sb[:, c * 512:(c + 1) * 512],
                                             psB2[:, :], ALU.mult))
        oum_done = vinc_last()
        oum_used = oum_done

        for t4 in range(4):
            xsl = t4 % 2
            glob_t = mt * 512 + t4 * 128
            if xm_used[xsl] is not None:
                wt("sync", eV, xm_used[xsl])
            xl = dma("sync", xm_c[:, xsl * D:(xsl + 1) * D],
                     x_ext[LB + glob_t: LB + glob_t + 128, :], qS)
            wt("tensor", eV, oum_done)
            wt("tensor", qG, wo_tgt)
            if prev_xm_done[0] is not None:
                wt("tensor", eV, prev_xm_done[0])
            for isl in range(4):
                for c in range(16):
                    T(lambda t, b=psW[isl], c=c, t4=t4, isl=isl:
                      t.matmul(b[:, :], oum[:, c * 512 + t4 * 128: c * 512 + t4 * 128 + 128],
                               wo_sb[:, c * D + isl * 512: c * D + isl * 512 + 512],
                               start=(c == 0), stop=(c == 15)))
            wmm = tnop_inc()
            wt("vector", eT, wmm)
            wt("vector", qS, xl)
            if xmid_used[xsl] is not None:
                wt("vector", qP, xmid_used[xsl])
            for isl in range(4):
                V(lambda v, isl=isl, xsl=xsl:
                  v.tensor_tensor(xmid[:, xsl * D + isl * 512: xsl * D + isl * 512 + 512],
                                  psW[isl][:, :], xm_c[:, xsl * D + isl * 512: xsl * D + isl * 512 + 512], ALU.add))
            xm_done = vinc_last()
            prev_xm_done[0] = xm_done
            xm_used[xsl] = xm_done
            xv = xmid[:, xsl * D:(xsl + 1) * D]
            wt("vector", eV, xm_done)
            V(lambda v, xv=xv: v.tensor_tensor(xl2f[:, :], xv, xv, ALU.mult).then_inc(eV.h, 1))
            lt1 = eV.bump()
            wt("vector", eV, lt1)
            V(lambda v: v.reduce_sum(l2c[:, 1:2], xl2f[:, :], axis=AX.X))
            V(lambda v, xv=xv: v.reduce_sum(l2c[:, 0:1], xv, axis=AX.X).then_inc(eV.h, 1))
            lt2 = eV.bump()
            wt("vector", eV, lt2)
            V(lambda v: v.tensor_scalar(l2c[:, 2:3], l2c[:, 0:1], 1.0 / D, None, ALU.mult))
            V(lambda v: v.tensor_scalar(l2c[:, 3:4], l2c[:, 1:2], 1.0 / D, None, ALU.mult).then_inc(eV.h, 1))
            lt3 = eV.bump()
            wt("vector", eV, lt3)
            V(lambda v: v.tensor_tensor(l2c[:, 0:1], l2c[:, 2:3], l2c[:, 2:3], ALU.mult).then_inc(eV.h, 1))
            lt4 = eV.bump()
            wt("vector", eV, lt4)
            V(lambda v: v.tensor_tensor(l2c[:, 3:4], l2c[:, 3:4], l2c[:, 0:1], ALU.subtract).then_inc(eV.h, 1))
            lt5 = eV.bump()
            wt("scalar", eV, lt5)
            S(lambda s: s.activation(l2c[:, 3:4], l2c[:, 3:4], AF.Ln, bias=eps5[:, 0:1]))
            S(lambda s: s.activation(l2c[:, 3:4], l2c[:, 3:4], AF.Exp, scale=-0.5).then_inc(eS.h, 1))
            lt6 = eS.bump()
            wt("vector", eS, lt6)
            V(lambda v, xv=xv: v.tensor_scalar(xl2f[:, :], xv, l2c[:, 2:3], l2c[:, 3:4],
                                               ALU.subtract, ALU.mult).then_inc(eV.h, 1))
            lt7 = eV.bump()
            wt("vector", eV, lt7)
            if stg_used[xsl] is not None:
                wt("vector", qP, stg_used[xsl])
            V(lambda v, xv=xv: v.tensor_tensor(xv, xv, xl2f[:, :], ALU.add))
            V(lambda v, xsl=xsl: v.tensor_copy(xl2s[:, xsl * D:(xsl + 1) * D], xl2f[:, :]).then_inc(eV.h, 1))
            lt8 = eV.bump()
            wt("scalar", eV, lt8)
            dma("scalar", s_xr2[glob_t: glob_t + 128, :], xv, qP)
            stg_used[xsl] = dma("scalar", s_xl2[glob_t: glob_t + 128, :], xl2s[:, xsl * D:(xsl + 1) * D], qP)
            xmid_used[xsl] = stg_used[xsl]
    bar("PC_epilogue")
    if stop == "PC_epilogue":
        finish()
        return nc

    pd.close()
    cstk.close()
    pe = ExitStack()
    Ee = pe.enter_context
    wsl_d = Ee(nc.sbuf_tensor("pd_wsl", [128, 3 * 8192], BF16))
    h2w_d = Ee(nc.sbuf_tensor("pd_h2w", [128, NEXP * 8 * 512], BF16))
    sha_d = Ee(nc.sbuf_tensor("pd_sha", [128, 16 * 512], BF16))
    u1_d = Ee(nc.sbuf_tensor("pd_u1", [128, 8192], BF16))
    xfm2_d = Ee(nc.sbuf_tensor("pd_xfm2", [128, 16 * 512], BF16))
    xt1_d = Ee(nc.sbuf_tensor("pd_xt1", [128, 16 * 512], BF16))
    wbc_d = Ee(nc.sbuf_tensor("pd_wbc", [128, NEXP * 512], BF16))
    g_d = Ee(nc.sbuf_tensor("pd_g", [128, 104], F32))
    cmp_d = Ee(nc.sbuf_tensor("pd_cmp", [128, 128], F32))
    wtm_d = Ee(nc.sbuf_tensor("pd_wtm", [128, 32], F32))
    wfm_d = Ee(nc.sbuf_tensor("pd_wfm", [8, 512], BF16))
    tmp_d = Ee(nc.sbuf_tensor("pd_tmp", [128, 2 * 512], F32))
    o1s_d = Ee(nc.sbuf_tensor("pd_o1s", [128, 2 * 512], BF16))
    o1tm_d = Ee(nc.sbuf_tensor("pd_o1tm", [128, D], BF16))
    xr2_d = Ee(nc.sbuf_tensor("pd_xr2", [128, D], F32))
    psD = [Ee(nc.psum_tensor("pd_ps%d" % i, [128, 512], F32)) for i in range(8)]

    sh_v = u1_d[:, 0:8192]
    h_v = u1_d[:, 0:4096]
    si_v = u1_d[:, 4096:8192]

    wt("sync", qC, cast_done_tgt[1])

    slab_load = {}
    slab_used = {}
    slab_ctr = [0]

    def slab(pieces):
        i = slab_ctr[0]
        slab_ctr[0] += 1
        sl = i % 3
        view = wsl_d[:, sl * 8192:(sl + 1) * 8192]
        if i >= 3:
            wt("sync", eT, slab_used[i - 3])
        tgt = None
        for (c0, c1), src in pieces:
            tgt = dma("sync", view[:, c0:c1], src, qS)
        slab_load[i] = tgt
        return view, i

    def slab_end(i, tgt=None):
        slab_used[i] = tgt if tgt is not None else tnop_inc()

    def fence_T():
        wt("tensor", eS, eS.v)
        wt("tensor", eV, eV.v)

    y_final = [None]
    o1tm_used = [None]
    o1s_used = [None, None]
    evd = {}
    xt1_ready = None

    o1_store_done = {}

    def emit_final(fmt):
        fosl = fmt % 2
        for t4 in range(4):
            wt("scalar", qP, o1_store_done[fmt])
            if o1tm_used[0] is not None:
                wt("scalar", eV, o1tm_used[0])
            o1t_tgt = None
            for hs4 in range(4):
                o1t_tgt = dma("scalar", o1tm_d[:, hs4 * 512:(hs4 + 1) * 512],
                              s_o1[fosl, hs4 * 512:(hs4 + 1) * 512, t4 * 128:(t4 + 1) * 128], qP, transpose=True)
            if y_final[0] is not None:
                wt("gpsimd", qY, y_final[0])
            xr_t = dma("gpsimd", xr2_d[:, :],
                       s_xr2[fmt * 512 + t4 * 128: fmt * 512 + (t4 + 1) * 128, :], qG)
            wt("vector", qP, o1t_tgt)
            wt("vector", qG, xr_t)
            V(lambda v: v.tensor_tensor(xr2_d[:, :], xr2_d[:, :],
                                        o1tm_d[:, :], ALU.add).then_inc(eV.h, 1))
            yv = eV.bump()
            o1tm_used[0] = yv
            wt("gpsimd", eV, yv)
            y_final[0] = dma("gpsimd", y_out[fmt * 512 + t4 * 128: fmt * 512 + (t4 + 1) * 128, :], xr2_d[:, :], qY)

    xf_next = [None]
    r0_end_T = [None]
    for mt in range(NMT):
        osl = mt % 2
        wt("scalar", eT, eT.v)
        wt("scalar", eV, eV.v)
        wt("vector", eT, eT.v)
        if mt == 0:
            xf_tgt = None
            for c in range(16):
                xf_tgt = dma("scalar", xfm2_d[:, c * 512:(c + 1) * 512],
                             s_xl2[mt * 512:(mt + 1) * 512, c * 128:(c + 1) * 128], qP, transpose=True)
        else:
            xf_tgt = xf_next[0]
        for r in range(2):
            xt = xfm2_d if r == 0 else xt1_d
            if r == 0:
                wt("tensor", qP, xf_tgt)
                wt("vector", qP, xf_tgt)
            else:
                wt("tensor", eV, xt1_ready)
            fence_T()
            for t4 in range(4):
                gb = 6 + (t4 % 2)
                gv = g_d[:, t4 * 8:(t4 + 1) * 8]
                if evd.get(("g", mt, r, t4 - 2)) is not None:
                    wt("tensor", eV, evd[("g", mt, r, t4 - 2)])
                for c in range(16):
                    T(lambda t, c=c, t4=t4, xt=xt, r=r, gb=gb:
                      t.matmul(psD[gb][:, t4 * 8:(t4 + 1) * 8],
                               xt[:, c * 512 + t4 * 128: c * 512 + t4 * 128 + 128],
                               wg_sb[:, (r * 16 + c) * NE:(r * 16 + c + 1) * NE],
                               start=(c == 0), stop=(c == 15)))
                gmm = tnop_inc()
                wt("vector", eT, gmm)
                V(lambda v, gv=gv, t4=t4, r=r, gb=gb: v.tensor_tensor(gv, psD[gb][:, t4 * 8:(t4 + 1) * 8],
                                                               bg_bc[:, r * NE:(r + 1) * NE], ALU.add).then_inc(eV.h, 1))
                evd[("g", mt, r, t4)] = eV.bump()
                gj = gv.unsqueeze(1).broadcast_to([128, 8, 8])
                ge = gv.unsqueeze(2).broadcast_to([128, 8, 8])
                rk = g_d[:, 32 + t4 * 8: 40 + t4 * 8]
                mk = g_d[:, 64 + t4 * 8: 72 + t4 * 8]
                ex = wtm_d[:, t4 * 8:(t4 + 1) * 8]
                mx = g_d[:, 96 + t4: 97 + t4]
                wt("vector", eV, evd[("g", mt, r, t4)])
                V(lambda v, gj=gj, ge=ge: v.tensor_tensor(
                    cmp_d[:, 0:64].rearrange("p (a b) -> p a b", a=8), gj, ge, ALU.is_gt))
                V(lambda v, gj=gj, ge=ge: v.tensor_tensor(
                    cmp_d[:, 64:128].rearrange("p (a b) -> p a b", a=8), gj, ge, ALU.is_equal).then_inc(eV.h, 1))
                c1 = eV.bump()
                wt("vector", eV, c1)
                V(lambda v: v.tensor_tensor(cmp_d[:, 64:128], cmp_d[:, 64:128], ltm_t[:, :], ALU.mult).then_inc(eV.h, 1))
                c2 = eV.bump()
                wt("vector", eV, c2)
                V(lambda v: v.tensor_tensor(cmp_d[:, 0:64], cmp_d[:, 0:64], cmp_d[:, 64:128], ALU.add).then_inc(eV.h, 1))
                c3 = eV.bump()
                wt("vector", eV, c3)
                V(lambda v, rk=rk: v.reduce_sum(rk, cmp_d[:, 0:64].rearrange("p (a b) -> p a b", a=8),
                                                axis=AX.X).then_inc(eV.h, 1))
                c4 = eV.bump()
                wt("vector", eV, c4)
                V(lambda v, rk=rk, mk=mk: v.tensor_scalar(mk, rk, 6.0, None, ALU.is_lt).then_inc(eV.h, 1))
                c5 = eV.bump()
                wt("vector", eV, c5)
                V(lambda v, mk=mk, ex=ex: v.tensor_scalar(ex, mk, 1.0, 1e30, ALU.subtract, ALU.mult).then_inc(eV.h, 1))
                c6 = eV.bump()
                wt("vector", eV, c6)
                V(lambda v, gv=gv, ex=ex: v.scalar_tensor_tensor(ex, gv, 1.0, ex, ALU.mult, ALU.add).then_inc(eV.h, 1))
                c7 = eV.bump()
                wt("vector", eV, c7)
                V(lambda v, ex=ex, mx=mx: v.reduce_max(mx, ex, axis=AX.X).then_inc(eV.h, 1))
                c8 = eV.bump()
                wt("vector", eV, c8)
                V(lambda v, ex=ex, mx=mx: v.tensor_scalar(ex, ex, mx[:, 0:1], None, ALU.subtract).then_inc(eV.h, 1))
                evd[("gp1", mt, r, t4)] = eV.bump()
            if r == 1 and mt + 1 < NMT:
                wt("scalar", eT, r0_end_T[0])
                wt("scalar", eV, xt1_ready)
                nxt = None
                for c in range(16):
                    nxt = dma("scalar", xfm2_d[:, c * 512:(c + 1) * 512],
                              s_xl2[(mt + 1) * 512:(mt + 2) * 512, c * 128:(c + 1) * 128], qP, transpose=True)
                xf_next[0] = nxt
            for mg in range(2):
                vwA, siA = slab([((kc * 1024, (kc + 1) * 1024),
                                  z_su[kc * 128:(kc + 1) * 128, mg * 1024:(mg + 1) * 1024]) for kc in range(8)])
                vwB, siB = slab([((kc * 1024, (kc + 1) * 1024),
                                  z_su[1024 + kc * 128: 1024 + (kc + 1) * 128, mg * 1024:(mg + 1) * 1024]) for kc in range(8)])
                wt("tensor", qS, slab_load[siA])
                for m8 in range(8):
                    m16 = mg * 8 + m8
                    bank = psD[m8 % 4]
                    if evd.get(("su", mt, r, m16 - 4)) is not None:
                        wt("tensor", eS, evd[("su", mt, r, m16 - 4)])
                    for kc in range(8):
                        T(lambda t, b=bank, vw=vwA, kc=kc, m8=m8, xt=xt:
                          t.matmul(b[:, :], vw[:, kc * 1024 + m8 * 128: kc * 1024 + m8 * 128 + 128],
                                   xt[:, kc * 512:(kc + 1) * 512], start=(kc == 0), stop=False))
                    if m8 == 0:
                        wt("tensor", qS, slab_load[siB])
                    for kc in range(8):
                        T(lambda t, b=bank, vw=vwB, kc=kc, m8=m8, xt=xt:
                          t.matmul(b[:, :], vw[:, kc * 1024 + m8 * 128: kc * 1024 + m8 * 128 + 128],
                                   xt[:, (8 + kc) * 512:(8 + kc + 1) * 512], start=False, stop=(kc == 7)))
                    suv = tnop_inc()
                    wt("scalar", eT, suv)
                    S(lambda s, m16=m16, b=bank: s.activation(sh_v[:, m16 * 512:(m16 + 1) * 512], b[:, :],
                                                              AF.Identity, bias=bias_cols[:, BSU0 + m16: BSU0 + m16 + 1]).then_inc(eS.h, 1))
                    evd[("su", mt, r, m16)] = eS.bump()
                slab_end(siA, eT.v)
                slab_end(siB, eT.v)
            if r == 0 and mt > 0:
                emit_final(mt - 1)
            for t4 in range(4):
                ex = wtm_d[:, t4 * 8:(t4 + 1) * 8]
                sm = g_d[:, 100 + t4: 101 + t4]
                wt("scalar", eV, evd[("gp1", mt, r, t4)])
                S(lambda s, ex=ex: s.activation(ex, ex, AF.Exp).then_inc(eS.h, 1))
                c10 = eS.bump()
                wt("vector", eS, c10)
                V(lambda v, ex=ex, sm=sm: v.reduce_sum(sm, ex, axis=AX.X).then_inc(eV.h, 1))
                c11 = eV.bump()
                wt("vector", eV, c11)
                V(lambda v, sm=sm: v.reciprocal(sm, sm).then_inc(eV.h, 1))
                c12 = eV.bump()
                wt("vector", eV, c12)
                V(lambda v, ex=ex, sm=sm: v.tensor_scalar(ex, ex, sm[:, 0:1], None, ALU.mult).then_inc(eV.h, 1))
                evd[("w", mt, r, t4)] = eV.bump()
            vw, si_ = slab([((kc * 1024, (kc + 1) * 1024),
                             z_sa[kc * 128:(kc + 1) * 128, :]) for kc in range(8)])
            wt("tensor", qS, slab_load[si_])
            wt("tensor", eS, evd[("su", mt, r, 15)])
            for sslot in range(2):
                for m8 in range(8):
                    bank = psD[4 + (m8 % 2)]
                    if evd.get(("sa", mt, r, sslot, m8 - 2)) is not None:
                        wt("tensor", eV, evd[("sa", mt, r, sslot, m8 - 2)])
                    elif sslot == 1 and m8 < 2:
                        wt("tensor", eV, evd[("sa", mt, r, 0, 6 + m8)])
                    for kc in range(8):
                        T(lambda t, b=bank, vw=vw, kc=kc, m8=m8, sslot=sslot:
                          t.matmul(b[:, :], vw[:, kc * 1024 + m8 * 128: kc * 1024 + m8 * 128 + 128],
                                   sh_v[:, (sslot * 8 + kc) * 512:(sslot * 8 + kc + 1) * 512],
                                   start=(kc == 0), stop=(kc == 7)))
                    sav = tnop_inc()
                    wt("scalar", eT, sav)
                    S(lambda s, m8=m8, b=bank:
                      s.activation(tmp_d[:, (m8 % 2) * 512:(m8 % 2 + 1) * 512], b[:, :], AF.Silu,
                                   bias=bias_cols[:, BSA0 + m8: BSA0 + m8 + 1]).then_inc(eS.h, 1))
                    sev = eS.bump()
                    wt("vector", eS, sev)
                    V(lambda v, m8=m8, sslot=sslot:
                      v.tensor_tensor(sha_d[:, (sslot * 8 + m8) * 512:(sslot * 8 + m8 + 1) * 512],
                                      tmp_d[:, (m8 % 2) * 512:(m8 % 2 + 1) * 512],
                                      sh_v[:, (sslot * 8 + m8) * 512:(sslot * 8 + m8 + 1) * 512],
                                      ALU.mult).then_inc(eV.h, 1))
                    evd[("sa", mt, r, sslot, m8)] = eV.bump()
            slab_end(si_, eT.v)
            sa_S_end = eS.v
            wt("tensor", eS, sa_S_end)
            wt("tensor", eV, evd[("g", mt, r, 3)])
            for t4 in range(4):
                wt("tensor", eV, evd[("w", mt, r, t4)])
                T(lambda t, t4=t4: t.transpose(psD[5][0:8, t4 * 128:(t4 + 1) * 128],
                                               wtm_d[:, t4 * 8:(t4 + 1) * 8], id_f[:, :]))
            wtr = tnop_inc()
            wt("vector", eT, wtr)
            V(lambda v: v.tensor_copy(wfm_d[:, :], psD[5][0:8, 0:512]).then_inc(eV.h, 1))
            wfm_done = eV.bump()
            wt("tensor", eV, wfm_done)
            for n in range(NEXP):
                T(lambda t, n=n: t.matmul(psD[7][:, :], sel_t[:, n * 128:(n + 1) * 128],
                                          wfm_d[0:8, :], start=True, stop=True))
                bcv = tnop_inc()
                wt("vector", eT, bcv)
                V(lambda v, n=n: v.tensor_copy(wbc_d[:, n * 512:(n + 1) * 512], psD[7][:, :]).then_inc(eV.h, 1))
                bev = eV.bump()
                wt("tensor", eV, bev)
            for n in range(NEXP):
                vwA, siA = slab([((kc * 1024, (kc + 1) * 1024),
                                  z_up[n, kc * 128:(kc + 1) * 128, :]) for kc in range(8)])
                vwB, siB = slab([((kc * 1024, (kc + 1) * 1024),
                                  z_up[n, 1024 + kc * 128: 1024 + (kc + 1) * 128, :]) for kc in range(8)])
                wt("tensor", qS, slab_load[siA])
                if n == 0:
                    wt("tensor", eV, evd[("sa", mt, r, 1, 7)])
                for m8 in range(8):
                    bank = psD[m8 % 4]
                    if evd.get(("up", mt, r, n, m8 - 4)) is not None:
                        wt("tensor", eS, evd[("up", mt, r, n, m8 - 4)])
                    elif n == 0 and m8 < 4:
                        wt("tensor", eS, evd[("su", mt, r, 12 + m8)])
                    elif n > 0 and m8 < 4:
                        wt("tensor", eS, evd[("up", mt, r, n - 1, 4 + m8)])
                    for kc in range(8):
                        T(lambda t, b=bank, vw=vwA, kc=kc, m8=m8, xt=xt:
                          t.matmul(b[:, :], vw[:, kc * 1024 + m8 * 128: kc * 1024 + m8 * 128 + 128],
                                   xt[:, kc * 512:(kc + 1) * 512], start=(kc == 0), stop=False))
                    if m8 == 0:
                        wt("tensor", qS, slab_load[siB])
                    for kc in range(8):
                        T(lambda t, b=bank, vw=vwB, kc=kc, m8=m8, xt=xt:
                          t.matmul(b[:, :], vw[:, kc * 1024 + m8 * 128: kc * 1024 + m8 * 128 + 128],
                                   xt[:, (8 + kc) * 512:(8 + kc + 1) * 512], start=False, stop=(kc == 7)))
                    upv = tnop_inc()
                    wt("scalar", eT, upv)
                    S(lambda s, m8=m8, n=n: s.activation(h_v[:, m8 * 512:(m8 + 1) * 512], psD[m8 % 4][:, :],
                                                         AF.Identity, bias=bias_cols[:, BUP0 + n * 8 + m8: BUP0 + n * 8 + m8 + 1]).then_inc(eS.h, 1))
                    evd[("up", mt, r, n, m8)] = eS.bump()
                slab_end(siA, eT.v)
                slab_end(siB, eT.v)
                vw, si_ = slab([((kc * 1024, (kc + 1) * 1024),
                                 z_sw[n, kc * 128:(kc + 1) * 128, :]) for kc in range(8)])
                wt("tensor", qS, slab_load[si_])
                wt("tensor", eS, evd[("up", mt, r, n, 7)])
                for m8 in range(8):
                    bank = psD[4 + (m8 % 2)]
                    if evd.get(("sw", mt, r, n, m8 - 2)) is not None:
                        wt("tensor", eV, evd[("sw", mt, r, n, m8 - 2)])
                    elif n == 0 and m8 < 2:
                        wt("tensor", eS, sa_S_end)
                    elif n > 0 and m8 < 2:
                        wt("tensor", eV, evd[("sw", mt, r, n - 1, 6 + m8)])
                    for kc in range(8):
                        T(lambda t, b=bank, vw=vw, kc=kc, m8=m8:
                          t.matmul(b[:, :], vw[:, kc * 1024 + m8 * 128: kc * 1024 + m8 * 128 + 128],
                                   h_v[:, kc * 512:(kc + 1) * 512], start=(kc == 0), stop=(kc == 7)))
                    swv = tnop_inc()
                    wt("scalar", eT, swv)
                    S(lambda s, m8=m8, n=n: s.activation(si_v[:, m8 * 512:(m8 + 1) * 512], psD[4 + (m8 % 2)][:, :],
                                                         AF.Silu, bias=bias_cols[:, BSW0 + n * 8 + m8: BSW0 + n * 8 + m8 + 1]).then_inc(eS.h, 1))
                    sev2 = eS.bump()
                    wt("vector", eS, sev2)
                    V(lambda v, m8=m8: v.tensor_tensor(si_v[:, m8 * 512:(m8 + 1) * 512],
                                                       si_v[:, m8 * 512:(m8 + 1) * 512],
                                                       h_v[:, m8 * 512:(m8 + 1) * 512], ALU.mult).then_inc(eV.h, 1))
                    h2a = eV.bump()
                    wt("vector", eV, h2a)
                    V(lambda v, m8=m8, n=n: v.tensor_tensor(h2w_d[:, (n * 8 + m8) * 512:(n * 8 + m8 + 1) * 512],
                                                            si_v[:, m8 * 512:(m8 + 1) * 512],
                                                            wbc_d[:, n * 512:(n + 1) * 512], ALU.mult).then_inc(eV.h, 1))
                    evd[("sw", mt, r, n, m8)] = eV.bump()
                slab_end(si_, eT.v)
            fence_T()
            tmp_used = {}
            for oc in range(2):
                for n in range(NEXP):
                    vw, si_ = slab([((kc * 1024, (kc + 1) * 1024),
                                     z_dn[n, oc, kc * 128:(kc + 1) * 128, :]) for kc in range(8)])
                    wt("tensor", qS, slab_load[si_])
                    if n == 0 and oc == 1:
                        wt("tensor", eS, evd[("dnS", mt, r, 0)])
                    for ht in range(8):
                        for kc in range(8):
                            T(lambda t, b=psD[ht], vw=vw, kc=kc, ht=ht, n=n:
                              t.matmul(b[:, :], vw[:, kc * 1024 + ht * 128: kc * 1024 + ht * 128 + 128],
                                       h2w_d[:, (n * 8 + kc) * 512:(n * 8 + kc + 1) * 512],
                                       start=(n == 0 and kc == 0), stop=False))
                    slab_end(si_)
                for half in range(2):
                    vw, si_ = slab([((kc * 1024, (kc + 1) * 1024),
                                     z_sd[oc, half * 1024 + kc * 128: half * 1024 + (kc) * 128 + 128, :]) for kc in range(8)])
                    wt("tensor", qS, slab_load[si_])
                    for ht in range(8):
                        for kc in range(8):
                            T(lambda t, b=psD[ht], vw=vw, kc=kc, ht=ht, half=half:
                              t.matmul(b[:, :], vw[:, kc * 1024 + ht * 128: kc * 1024 + ht * 128 + 128],
                                       sha_d[:, (half * 8 + kc) * 512:(half * 8 + kc + 1) * 512],
                                       start=False, stop=False))
                    slab_end(si_)
                for ht in range(8):
                    T(lambda t, ht=ht, oc=oc: t.matmul(psD[ht][:, :],
                                                       bdn_bf[0:6, oc * 1024 + ht * 128: oc * 1024 + ht * 128 + 128],
                                                       wfm_d[0:6, :], start=False, stop=True))
                dn_done = tnop_inc()
                for ht in range(8):
                    hglob = oc * 8 + ht
                    tsl = ht % 2
                    wt("scalar", eT, dn_done)
                    if r == 0:
                        if tmp_used.get(tsl) is not None:
                            wt("scalar", eV, tmp_used[tsl])
                        S(lambda s, ht=ht, hg=hglob, tsl=tsl:
                          s.activation(tmp_d[:, tsl * 512:(tsl + 1) * 512], psD[ht][:, :],
                                       AF.Identity, bias=bias_cols[:, BSD0 + hg: BSD0 + hg + 1]).then_inc(eS.h, 1))
                        sev3 = eS.bump()
                        wt("vector", eS, sev3)
                        V(lambda v, hg=hglob, tsl=tsl: v.tensor_tensor(xt1_d[:, hg * 512:(hg + 1) * 512],
                                                                       tmp_d[:, tsl * 512:(tsl + 1) * 512],
                                                                       xfm2_d[:, hg * 512:(hg + 1) * 512],
                                                                       ALU.add).then_inc(eV.h, 1))
                        tmp_used[tsl] = eV.bump()
                    else:
                        if o1s_used[tsl] is not None:
                            wt("scalar", qP, o1s_used[tsl])
                        S(lambda s, ht=ht, hg=hglob, tsl=tsl:
                          s.activation(o1s_d[:, tsl * 512:(tsl + 1) * 512], psD[ht][:, :],
                                       AF.Identity, bias=bias_cols[:, BSD0 + hg: BSD0 + hg + 1]).then_inc(eS.h, 1))
                        sev3 = eS.bump()
                        wt("scalar", eS, sev3)
                        o1s_used[tsl] = dma("scalar", s_o1[osl, hglob * 128:(hglob + 1) * 128, :],
                                            o1s_d[:, tsl * 512:(tsl + 1) * 512], qP)
                evd[("dnS", mt, r, oc)] = eS.v
            if r == 0:
                xt1_ready = eV.v
                r0_end_T[0] = eT.v
            else:
                o1_store_done[mt] = qP.v
    emit_final(NMT - 1)
    bar("PD_ffn")
    finish()
    return nc


_CACHE = {}


def _get_prog(dbg=()):
    key = tuple(dbg)
    if key not in _CACHE:
        _CACHE[key] = build(key)
    return _CACHE[key]


def make_in_maps(inputs):
    x = np.asarray(inputs["L0_attn_x"], dtype=np.float32)
    base = {
        "slope": np.ascontiguousarray(np.asarray(inputs["L0_attn_slope_rate"], np.float32).reshape(1, HQ)),
        "wqT": np.ascontiguousarray(np.asarray(inputs["L0_attn_W_qkvu"], np.float32).T),
        "woT": np.ascontiguousarray(np.asarray(inputs["L0_attn_W_out"], np.float32).T),
        "wg_in": np.ascontiguousarray(inputs["L0_ffn_Wg"], np.float32),
        "bg_in": np.ascontiguousarray(inputs["L0_ffn_bg"], np.float32),
        "wup_in": np.ascontiguousarray(inputs["L0_ffn_Wup"], np.float32),
        "bup_in": np.ascontiguousarray(inputs["L0_ffn_bup"], np.float32),
        "wsw_in": np.ascontiguousarray(inputs["L0_ffn_Wsw"], np.float32),
        "bsw_in": np.ascontiguousarray(inputs["L0_ffn_bsw"], np.float32),
        "wdn_in": np.ascontiguousarray(inputs["L0_ffn_Wdn"], np.float32),
        "bdn_in": np.ascontiguousarray(inputs["L0_ffn_bdn"], np.float32),
        "wsu_in": np.ascontiguousarray(inputs["L0_ffn_Wsu"], np.float32),
        "bsu_in": np.ascontiguousarray(np.asarray(inputs["L0_ffn_bsu"], np.float32).reshape(1, SED)),
        "wsa_in": np.ascontiguousarray(inputs["L0_ffn_Wsa"], np.float32),
        "bsa_in": np.ascontiguousarray(np.asarray(inputs["L0_ffn_bsa"], np.float32).reshape(1, E)),
        "wsd_in": np.ascontiguousarray(inputs["L0_ffn_Wsd"], np.float32),
        "bsd_in": np.ascontiguousarray(np.asarray(inputs["L0_ffn_bsd"], np.float32).reshape(1, D)),
    }
    in_maps = []
    for c in range(8):
        b, pos = c // 4, c % 4
        main = x[b, pos * MAIN:(pos + 1) * MAIN]
        if pos == 0:
            lbp = np.zeros((LB, D), np.float32)
        else:
            lbp = x[b, pos * MAIN - LB: pos * MAIN]
        m = dict(base)
        m["x_ext"] = np.ascontiguousarray(np.concatenate([lbp, main], axis=0))
        in_maps.append(m)
    return in_maps


def kernel(**inputs):
    nc = _get_prog()
    in_maps = make_in_maps(inputs)
    res = run_bass_kernel_spmd(nc, in_maps, core_ids=list(range(8)))
    out = np.empty((2, 16384, D), np.float32)
    for c in range(8):
        out[c // 4, (c % 4) * MAIN:((c % 4) + 1) * MAIN] = res.results[c]["y"]
    return out


# version 10
# speedup vs baseline: 1.2649x; 1.1197x over previous
import numpy as np
import ml_dtypes
import concourse.bass as bass
import concourse.mybir as mybir
from concourse.bass_utils import run_bass_kernel_spmd
from contextlib import ExitStack

F32 = mybir.dt.float32
BF16 = mybir.dt.bfloat16
AF = mybir.ActivationFunctionType
ALU = mybir.AluOpType
AX = mybir.AxisListType
BF = ml_dtypes.bfloat16

D = 2048
HQ = 16
BLK = 256
MAIN = 4096
LB = 2048
EXT = MAIN + LB
NBM = MAIN // BLK
NBL = LB // BLK
NBE = NBM + NBL
E = 1024
NEXP = 6
NE = 8
SED = 2048
MT = 512
NMT = MAIN // MT

KEYS = ("tensor", "vector", "scalar", "sync", "gpsimd")


class SemC:
    __slots__ = ("h", "v")

    def __init__(self, h):
        self.h = h
        self.v = 0

    def bump(self, n=1):
        self.v += n
        return self.v


def build(dbg=(), stop=None):
    import os as _os0
    nc = bass.Bass("TRN2", target_bir_lowering=False, detect_race_conditions=(_os0.environ.get("NORACE") is None))
    top = ExitStack()
    Et = top.enter_context

    x_ext = nc.dram_tensor("x_ext", [EXT, D], F32, kind="ExternalInput").ap()
    slope = nc.dram_tensor("slope", [1, HQ], F32, kind="ExternalInput").ap()
    wqT = nc.dram_tensor("wqT", [D, 4 * D], F32, kind="ExternalInput").ap()
    woT = nc.dram_tensor("woT", [D, D], F32, kind="ExternalInput").ap()
    wg_in = nc.dram_tensor("wg_in", [2, D, NE], F32, kind="ExternalInput").ap()
    bg_in = nc.dram_tensor("bg_in", [2, NE], F32, kind="ExternalInput").ap()
    wup_in = nc.dram_tensor("wup_in", [NEXP, D, E], F32, kind="ExternalInput").ap()
    bup_in = nc.dram_tensor("bup_in", [NEXP, E], F32, kind="ExternalInput").ap()
    wsw_in = nc.dram_tensor("wsw_in", [NEXP, E, E], F32, kind="ExternalInput").ap()
    bsw_in = nc.dram_tensor("bsw_in", [NEXP, E], F32, kind="ExternalInput").ap()
    wdn_in = nc.dram_tensor("wdn_in", [NEXP, E, D], F32, kind="ExternalInput").ap()
    bdn_in = nc.dram_tensor("bdn_in", [NEXP, D], F32, kind="ExternalInput").ap()
    wsu_in = nc.dram_tensor("wsu_in", [D, SED], F32, kind="ExternalInput").ap()
    bsu_in = nc.dram_tensor("bsu_in", [1, SED], F32, kind="ExternalInput").ap()
    wsa_in = nc.dram_tensor("wsa_in", [E, E], F32, kind="ExternalInput").ap()
    bsa_in = nc.dram_tensor("bsa_in", [1, E], F32, kind="ExternalInput").ap()
    wsd_in = nc.dram_tensor("wsd_in", [SED, D], F32, kind="ExternalInput").ap()
    bsd_in = nc.dram_tensor("bsd_in", [1, D], F32, kind="ExternalInput").ap()
    y_out = nc.dram_tensor("y", [MAIN, D], F32, kind="ExternalOutput").ap()

    s_xln = nc.dram_tensor("s_xln", [EXT, D], BF16).ap()
    s_qfm = nc.dram_tensor("s_qfm", [D, MAIN], BF16).ap()
    s_kfm = nc.dram_tensor("s_kfm", [D, MAIN], BF16).ap()
    s_ktm = nc.dram_tensor("s_ktm", [HQ, 128, EXT // 128, 128], BF16).ap()
    s_vtm = nc.dram_tensor("s_vtm", [HQ, 128, EXT // 128, 128], BF16).ap()
    s_ufm = nc.dram_tensor("s_ufm", [D, MAIN], BF16).ap()
    s_ofm = nc.dram_tensor("s_ofm", [D, MAIN], F32).ap()
    s_xr2 = nc.dram_tensor("s_xr2", [MAIN, D], F32).ap()
    s_xl2 = nc.dram_tensor("s_xl2", [MAIN, D], BF16).ap()
    s_o1 = nc.dram_tensor("s_o1", [2, D, MT], BF16).ap()
    z_up = nc.dram_tensor("z_up", [NEXP, D, E], BF16).ap()
    z_sw = nc.dram_tensor("z_sw", [NEXP, E, E], BF16).ap()
    z_dn = nc.dram_tensor("z_dn", [NEXP, 2, E, E], BF16).ap()
    z_su = nc.dram_tensor("z_su", [D, SED], BF16).ap()
    z_sa = nc.dram_tensor("z_sa", [E, E], BF16).ap()
    z_sd = nc.dram_tensor("z_sd", [2, SED, E], BF16).ap()

    scratch_map = {
        "s_xln": s_xln, "s_qfm": s_qfm, "s_kfm": s_kfm, "s_ktm": s_ktm,
        "s_vtm": s_vtm, "s_ufm": s_ufm, "s_ofm": s_ofm, "s_xr2": s_xr2,
        "s_xl2": s_xl2,
    }
    dbg_outs = {}
    for name in dbg:
        t = scratch_map[name]
        dbg_outs[name] = nc.dram_tensor("dbg_" + name, list(t.shape),
                                        t.dtype, kind="ExternalOutput").ap()

    il_id_bf = nc.inline_tensor(np.eye(128, dtype=BF), "il_id_bf").ap()
    il_id_f = nc.inline_tensor(np.eye(128, dtype=np.float32), "il_id_f").ap()
    il_ones_r_f = nc.inline_tensor(np.ones((1, 128), np.float32), "il_ones_r_f").ap()
    il_ones_c_f = nc.inline_tensor(np.ones((128, 1), np.float32), "il_ones_c_f").ap()
    il_ar1_r = nc.inline_tensor(np.arange(1, 257, dtype=np.float32)[None, :], "il_ar1_r").ap()
    il_ar0_c = nc.inline_tensor(np.arange(0, 128, dtype=np.float32)[:, None], "il_ar0_c").ap()
    ltm_np = np.zeros((1, 64), np.float32)
    for e_ in range(8):
        for j_ in range(8):
            if j_ < e_:
                ltm_np[0, e_ * 8 + j_] = 1.0
    il_ltm = nc.inline_tensor(ltm_np, "il_ltm").ap()
    sel_np = np.zeros((8, NEXP * 128), BF)
    for n_ in range(NEXP):
        sel_np[n_, n_ * 128:(n_ + 1) * 128] = 1.0
    il_sel = nc.inline_tensor(sel_np, "il_sel").ap()

    id_bf = Et(nc.sbuf_tensor("c_id_bf", [128, 128], BF16))
    id_f = Et(nc.sbuf_tensor("c_id_f", [128, 128], F32))
    ones_r_f = Et(nc.sbuf_tensor("c_ones_r_f", [1, 128], F32))
    ones_c_f = Et(nc.sbuf_tensor("c_ones_c_f", [128, 1], F32))
    ar0_c = Et(nc.sbuf_tensor("c_ar0_c", [128, 1], F32))
    ltm_t = Et(nc.sbuf_tensor("c_ltm", [128, 64], F32))
    sel_t = Et(nc.sbuf_tensor("c_sel", [8, NEXP * 128], BF16))
    sigb = Et(nc.sbuf_tensor("c_sigb", [128, HQ], F32))
    lamb = Et(nc.sbuf_tensor("c_lamb", [128, HQ], F32))
    kdc = Et(nc.sbuf_tensor("c_kdc", [128, 2 * HQ], F32))
    eps5 = Et(nc.sbuf_tensor("c_eps5", [128, 1], F32))
    eps6 = Et(nc.sbuf_tensor("c_eps6", [128, 1], F32))
    bias_cols = Et(nc.sbuf_tensor("c_bias_cols", [128, 232], F32))
    bdn_bf = Et(nc.sbuf_tensor("c_bdn_bf", [NEXP, D], BF16))
    bg_bc = Et(nc.sbuf_tensor("c_bg_bc", [128, 2 * NE], F32))
    wg_sb = Et(nc.sbuf_tensor("c_wg_sb", [128, 2 * 16 * NE], BF16))
    cst = Et(nc.sbuf_tensor("c_cst", [128, 2 * 2048], BF16))
    BUP0, BSW0, BDN0, BSU0, BSA0, BSD0 = 0, 48, 96, 192, 208, 216

    qS = SemC(Et(nc.semaphore("sem_qS")))
    qP = SemC(Et(nc.semaphore("sem_qP")))
    qG = SemC(Et(nc.semaphore("sem_qG")))
    qY = SemC(Et(nc.semaphore("sem_qY")))
    eT = SemC(Et(nc.semaphore("sem_eT")))
    eV = SemC(Et(nc.semaphore("sem_eV")))
    eS = SemC(Et(nc.semaphore("sem_eS")))
    qC = SemC(Et(nc.semaphore("sem_qC")))
    qL = SemC(Et(nc.semaphore("sem_qL")))

    rec = {k: [] for k in KEYS}

    def _emit(key, f):
        rec[key].append(f)

    def T(f):
        _emit("tensor", f)

    def V(f):
        _emit("vector", f)

    def S(f):
        _emit("scalar", f)

    EMIT = {k: (lambda f, kk=k: _emit(kk, f)) for k in KEYS}

    def wt(key, sem, tgt):
        if tgt is None or tgt <= 0:
            return
        EMIT[key](lambda e, h=sem.h, v=tgt: e.wait_ge(h, v))

    def dma(key, out_ap, in_ap, sem, **kw):
        tgt = sem.bump(16)
        EMIT[key](lambda e, o=out_ap, i=in_ap, h=sem.h, k=dict(kw):
                  e.dma_start(out=o, in_=i, **k).then_inc(h, 16))
        return tgt

    def tnop_inc():
        f = rec["tensor"].pop()
        rec["tensor"].append(lambda e, f=f: f(e).then_inc(eT.h, 1))
        return eT.bump()

    def vinc_last():
        f = rec["vector"].pop()
        rec["vector"].append(lambda e, f=f: f(e).then_inc(eV.h, 1))
        return eV.bump()

    def flush(tag):
        nonlocal rec
        with nc.Block(tag) as blk:
            for key in KEYS:
                lst = rec[key]
                if lst:
                    getattr(blk, key)(lambda eng, L=tuple(lst): [f(eng) for f in L])
        rec = {k: [] for k in KEYS}

    nc._dbg_sems = {"qS": qS, "qP": qP, "qG": qG, "qY": qY, "eT": eT, "eV": eV, "eS": eS}

    def bar(tag, wait_qC=True):
        wt("sync", qS, qS.v)
        wt("scalar", qP, qP.v)
        wt("gpsimd", qG, qG.v)
        wt("gpsimd", qY, qY.v)
        if wait_qC:
            try:
                wt("gpsimd", qC, qC.v)
            except NameError:
                pass
        flush(tag)
        nc.all_engine_barrier()

    def finish():
        if dbg_outs:
            for name, out_ap in dbg_outs.items():
                t = scratch_map[name]
                dma("gpsimd", out_ap, t, qY)
            wt("gpsimd", qY, qY.v)
            flush("dbg")
            nc.all_engine_barrier()
        wt("gpsimd", qY, qY.v)
        flush("fin")

    cast_jobs = []

    def add_cast(csrc, cdst, rows, cols):
        step = 2 if cols <= 1024 else 1
        for r0 in range(0, rows, 128 * step):
            nchunk = min(step, (rows - r0) // 128)
            sap = csrc[r0:r0 + nchunk * 128, :].rearrange("(c p) e -> p c e", p=128) \
                if nchunk > 1 else csrc[r0:r0 + 128, :]
            dap = cdst[r0:r0 + nchunk * 128, :].rearrange("(c p) e -> p c e", p=128) \
                if nchunk > 1 else cdst[r0:r0 + 128, :]
            cast_jobs.append((sap, dap, nchunk * cols, nchunk, cols))

    for n in range(NEXP):
        add_cast(wup_in[n], z_up[n], D, E)
        add_cast(wsw_in[n], z_sw[n], E, E)
        for oc in range(2):
            add_cast(wdn_in[n][:, oc * E:(oc + 1) * E], z_dn[n, oc], E, E)
    add_cast(wsu_in, z_su, D, SED)
    add_cast(wsa_in, z_sa, E, E)
    for oc in range(2):
        add_cast(wsd_in[:, oc * E:(oc + 1) * E], z_sd[oc], SED, E)

    NSL = 2
    cload = [None] * len(cast_jobs)
    cstore = [None] * len(cast_jobs)

    def cst_view(sl, nchunk, cols):
        v = cst[:, sl * 2048: sl * 2048 + nchunk * cols]
        return v.rearrange("p (c e) -> p c e", c=nchunk) if nchunk > 1 else v

    cast_thunks = []
    for i in range(min(NSL, len(cast_jobs))):
        def th0(i=i):
            cload[i] = dma("gpsimd", cst_view(i % NSL, cast_jobs[i][3], cast_jobs[i][4]),
                           cast_jobs[i][0], qL)
        cast_thunks.append(th0)
    for i in range(len(cast_jobs)):
        def th1(i=i):
            wt("gpsimd", qL, cload[i])
            cstore[i] = dma("gpsimd", cast_jobs[i][1],
                            cst_view(i % NSL, cast_jobs[i][3], cast_jobs[i][4]), qC)
            j = i + NSL
            if j < len(cast_jobs):
                wt("gpsimd", qC, cstore[i])
                cload[j] = dma("gpsimd", cst_view(j % NSL, cast_jobs[j][3], cast_jobs[j][4]),
                               cast_jobs[j][0], qL)
        cast_thunks.append(th1)
    cast_pos = [0]

    def pump_casts(n):
        while n > 0 and cast_pos[0] < len(cast_thunks):
            cast_thunks[cast_pos[0]]()
            cast_pos[0] += 1
            n -= 1

    dstk = ExitStack()
    ar1_bc = dstk.enter_context(nc.sbuf_tensor("c_ar1_bc", [128, 256], F32))
    dc_t = dstk.enter_context(nc.sbuf_tensor("c_dc", [128, 512], F32))
    msk_t = dstk.enter_context(nc.sbuf_tensor("c_msk", [128, 512], F32))
    p0 = ExitStack()
    E0 = p0.enter_context
    ar1_r = E0(nc.sbuf_tensor("c_ar1_r", [1, 256], F32))
    slp_r = E0(nc.sbuf_tensor("c_slp_r", [1, HQ], F32))
    bgrow = E0(nc.sbuf_tensor("c_bgrow", [1, 2 * NE], F32))
    dma("sync", id_bf[:, :], il_id_bf, qS)
    dma("sync", id_f[:, :], il_id_f, qS)
    dma("sync", ones_r_f[:, :], il_ones_r_f, qS)
    dma("sync", ones_c_f[:, :], il_ones_c_f, qS)
    dma("sync", ar1_r[:, :], il_ar1_r, qS)
    dma("sync", ar0_c[:, :], il_ar0_c, qS)
    dma("sync", ltm_t[:, :], il_ltm.broadcast_to([128, 64]), qS)
    dma("sync", sel_t[:, :], il_sel, qS)
    dma("sync", slp_r[:, :], slope, qS)
    dma("sync", bgrow[:, :], bg_in.rearrange("a b -> (a b)").unsqueeze(0), qS)
    dma("gpsimd", bdn_bf[:, :], bdn_in, qG)
    for j in range(2):
        for c in range(16):
            dma("gpsimd", wg_sb[:, (j * 16 + c) * NE:(j * 16 + c + 1) * NE],
                wg_in[j, c * 128:(c + 1) * 128, :], qG)
    ld_all = qS.v
    wt("vector", qS, ld_all)
    wt("vector", qG, qG.v)
    V(lambda v: v.memset(eps5[:, :], 1e-5))
    V(lambda v: v.memset(eps6[:, :], 1e-6).then_inc(eV.h, 1))
    eV.bump()

    if stop == "S0":
        bar("S0")
        finish()
        return nc
    psP = E0(nc.psum_tensor("psP0", [128, 512], F32))
    psP2 = E0(nc.psum_tensor("psP0b", [128, 512], F32))

    wt("tensor", qS, ld_all)
    T(lambda t: t.matmul(psP[:, 0:HQ], ones_r_f[0:1, :], slp_r[0:1, :], start=True, stop=True))
    for j in range(2):
        T(lambda t, j=j: t.matmul(psP[:, 32 + j * NE:32 + (j + 1) * NE], ones_r_f[0:1, :],
                                  bgrow[0:1, j * NE:(j + 1) * NE], start=True, stop=True))
    T(lambda t: t.matmul(psP2[:, 0:256], ones_r_f[0:1, :], ar1_r[0:1, :], start=True, stop=True))
    t_p0 = tnop_inc()
    wt("vector", eT, t_p0)
    V(lambda v: v.tensor_copy(sigb[:, :], psP[:, 0:HQ]))
    V(lambda v: v.tensor_copy(bg_bc[:, :], psP[:, 32:32 + 2 * NE]))
    V(lambda v: v.tensor_copy(ar1_bc[:, :], psP2[:, 0:256]).then_inc(eV.h, 1))
    v_p0 = eV.bump()
    wt("vector", eV, v_p0)
    V(lambda v: v.tensor_scalar(lamb[:, :], sigb[:, :], -256.0, None, ALU.mult))
    for h in range(HQ):
        for par in range(2):
            col = kdc[:, h * 2 + par: h * 2 + par + 1]
            V(lambda v, c=col, p=par: v.tensor_scalar(c, ar0_c[:, :], -1.0, float(255 - p * 128), ALU.mult, ALU.add))
    v_p1 = vinc_last()
    wt("vector", eV, v_p1)
    for h in range(HQ):
        for par in range(2):
            col = kdc[:, h * 2 + par: h * 2 + par + 1]
            V(lambda v, c=col, hh=h: v.tensor_tensor(c, c, sigb[:, hh:hh + 1], ALU.mult))
    for par in range(2):
        sl = slice(par * 256, (par + 1) * 256)
        V(lambda v, sl=sl, p=par: v.tensor_scalar(dc_t[:, sl], ar1_bc[:, :], ar0_c[:, 0:1],
                                                  float(-1.0 - p * 128), ALU.subtract, ALU.add))
    v_p2 = vinc_last()
    wt("vector", eV, v_p2)
    for par in range(2):
        sl = slice(par * 256, (par + 1) * 256)
        V(lambda v, sl=sl: v.tensor_scalar(msk_t[:, sl], dc_t[:, sl], 0.0, None, ALU.is_ge))
    v_p3 = vinc_last()
    wt("vector", eV, v_p3)
    for par in range(2):
        sl = slice(par * 256, (par + 1) * 256)
        V(lambda v, sl=sl: v.tensor_scalar_max(dc_t[:, sl], dc_t[:, sl], 0.0))
    v_p4 = vinc_last()
    wt("scalar", eV, v_p4)
    S(lambda s: s.activation(lamb[:, :], lamb[:, :], AF.Exp))
    S(lambda s: s.activation(kdc[:, :], kdc[:, :], AF.Exp, scale=-1.0).then_inc(eS.h, 1))
    eS.bump()

    if stop == "S2":
        bar("S2")
        finish()
        return nc
    bstage = E0(nc.sbuf_tensor("c_bstage", [1, 12288], F32))
    bias_jobs = [
        (bup_in.rearrange("n e -> (n e)").unsqueeze(0), 48, BUP0),
        (bsw_in.rearrange("n e -> (n e)").unsqueeze(0), 48, BSW0),
        (bdn_in.rearrange("n d -> (n d)").unsqueeze(0), 96, BDN0),
        (bsu_in, 16, BSU0),
        (bsa_in, 8, BSA0),
        (bsd_in, 16, BSD0),
    ]
    _kb = _os0.environ.get("KBIAS")
    if _kb is not None:
        bias_jobs = bias_jobs[: int(_kb)]
    wt("tensor", eV, v_p0)
    _bm = _os0.environ.get("BIASMODE", "")
    for bsrc, ncols, base in bias_jobs:
        wt("sync", eT, eT.v)
        if _bm != "mmonly":
            tgt = dma("sync", bstage[:, 0:ncols * 128], bsrc, qS)
            wt("tensor", qS, tgt)
        off0 = 0
        k = 0
        while k < ncols:
            if _bm == "dmaonly":
                break
            n4 = min(4, ncols - k)
            for i in range(n4):
                T(lambda t, o=off0 + (k + i) * 128, g=i:
                  t.matmul(psP2[:, g:g + 1], bstage[0:1, o:o + 128], ones_c_f[0:1, 0:1],
                           start=True, stop=True))
            tv = tnop_inc()
            wt("vector", eT, tv)
            V(lambda v, b=base + k, n=n4: v.tensor_copy(
                bias_cols[:, b:b + n], psP2[:, 0:n]).then_inc(eV.h, 1))
            tvv = eV.bump()
            wt("tensor", eV, tvv)
            k += n4

    if stop == "S3":
        bar("S3")
        finish()
        return nc
    xt0 = E0(nc.sbuf_tensor("a0_x", [128, 2 * D], F32))
    xsq = E0(nc.sbuf_tensor("a0_sq", [128, 2 * D], F32))
    lstg = E0(nc.sbuf_tensor("a0_stg", [128, 2 * D], BF16))
    scol = E0(nc.sbuf_tensor("a0_scol", [128, 8], F32))
    x_used = [None, None]
    st_done = [None, None]
    for i in range(EXT // 128):
        sl = i % 2
        xv = xt0[:, sl * D:(sl + 1) * D]
        sq = xsq[:, sl * D:(sl + 1) * D]
        st = lstg[:, sl * D:(sl + 1) * D]
        c_s1 = scol[:, sl * 4 + 0: sl * 4 + 1]
        c_s2 = scol[:, sl * 4 + 1: sl * 4 + 2]
        c_m = scol[:, sl * 4 + 2: sl * 4 + 3]
        c_v = scol[:, sl * 4 + 3: sl * 4 + 4]
        if x_used[sl] is not None:
            wt("sync", eV, x_used[sl])
        xl = dma("sync", xv, x_ext[i * 128:(i + 1) * 128, :], qS)
        wt("vector", qS, xl)
        V(lambda v, a=xv, b=sq: v.tensor_tensor(b, a, a, ALU.mult).then_inc(eV.h, 1))
        t1 = eV.bump()
        wt("vector", eV, t1)
        V(lambda v, b=sq, c=c_s2: v.reduce_sum(c, b, axis=AX.X))
        V(lambda v, a=xv, c=c_s1: v.reduce_sum(c, a, axis=AX.X).then_inc(eV.h, 1))
        t2 = eV.bump()
        wt("vector", eV, t2)
        V(lambda v, c=c_s1, m=c_m: v.tensor_scalar(m, c, 1.0 / D, None, ALU.mult))
        V(lambda v, c=c_s2, vv=c_v: v.tensor_scalar(vv, c, 1.0 / D, None, ALU.mult).then_inc(eV.h, 1))
        t3 = eV.bump()
        wt("vector", eV, t3)
        V(lambda v, m=c_m, tmp=c_s1: v.tensor_tensor(tmp, m, m, ALU.mult).then_inc(eV.h, 1))
        t4 = eV.bump()
        wt("vector", eV, t4)
        V(lambda v, vv=c_v, tmp=c_s1: v.tensor_tensor(vv, vv, tmp, ALU.subtract).then_inc(eV.h, 1))
        t5 = eV.bump()
        wt("scalar", eV, t5)
        S(lambda s, vv=c_v: s.activation(vv, vv, AF.Ln, bias=eps5[:, 0:1]))
        S(lambda s, vv=c_v: s.activation(vv, vv, AF.Exp, scale=-0.5).then_inc(eS.h, 1))
        t6 = eS.bump()
        if st_done[sl] is not None:
            wt("vector", qP, st_done[sl])
        wt("vector", eS, t6)
        V(lambda v, a=xv, o=st, m=c_m, r=c_v:
          v.tensor_scalar(o, a, m[:, 0:1], r[:, 0:1], ALU.subtract, ALU.mult).then_inc(eV.h, 1))
        x_used[sl] = eV.bump()
        wt("scalar", eV, x_used[sl])
        st_done[sl] = dma("scalar", s_xln[i * 128:(i + 1) * 128, :], st, qP)
        pump_casts(1)
    bar("P1_ln1_casts")
    if stop == "P1_ln1_casts":
        finish()
        return nc
    p0.close()

    pa = ExitStack()
    Ea = pa.enter_context
    xfm = Ea(nc.sbuf_tensor("pa_xfm", [128, 16 * MAIN], BF16))
    wsl = Ea(nc.sbuf_tensor("pa_wsl", [128, 2 * 16 * 512], BF16))
    qstg = Ea(nc.sbuf_tensor("pa_stg", [128, 4 * 512], BF16))
    ktst = Ea(nc.sbuf_tensor("pa_ktst", [128, 2 * 512], BF16))
    vst2 = Ea(nc.sbuf_tensor("pa_vst2", [128, 2 * 4 * 512], BF16))
    psA = [Ea(nc.psum_tensor("pa_ps%d" % i, [128, 512], F32)) for i in range(4)]
    psKT = Ea(nc.psum_tensor("pa_pskt", [128, 1024], BF16))

    xfm_tgt = None
    for c in range(16):
        for tt8 in range(8):
            xfm_tgt = dma("sync", xfm[:, c * MAIN + tt8 * 512: c * MAIN + tt8 * 512 + 512],
                          s_xln[LB + tt8 * 512: LB + tt8 * 512 + 512, c * 128:(c + 1) * 128],
                          qS, transpose=True)

    wslot_used = [None, None]

    def emit_fm_slices(slices, xsrc, xstride, first_wait):
        w_used = wslot_used
        ev_done = {}
        stg_free = [None] * 4
        kt_free = [None, None]
        kt_ps = [None, None]
        fw = first_wait
        ntt = xstride // 512
        for si, (wc0, rb, kind, dstt) in enumerate(slices):
            sl = si % 2
            wv = wsl[:, sl * 16 * 512:(sl + 1) * 16 * 512]
            if w_used[sl] is not None:
                wt("gpsimd", eT, w_used[sl])
            wl = None
            for c in range(16):
                wl = dma("gpsimd", wv[:, c * 512:(c + 1) * 512],
                         wqT[c * 128:(c + 1) * 128, wc0:wc0 + 512], qG)
            wt("tensor", qG, wl)
            if fw is not None:
                wt("tensor", qS, fw)
                fw = None
            g = 0
            for m in range(4):
                for tt8 in range(ntt):
                    bank = psA[g % 4]
                    if ev_done.get(g - 4) is not None:
                        wt("tensor", eS, ev_done[g - 4])
                    for c in range(16):
                        T(lambda t, b=bank, w=wv, c=c, m=m, x=xsrc, tt=tt8, stx=xstride:
                          t.matmul(b[:, :], w[:, c * 512 + m * 128: c * 512 + m * 128 + 128],
                                   x[:, c * stx + tt * 512: c * stx + tt * 512 + 512],
                                   start=(c == 0), stop=(c == 15)))
                    mmv = tnop_inc()
                    stg = qstg[:, (g % 4) * 512:(g % 4 + 1) * 512]
                    wt("scalar", eT, mmv)
                    if stg_free[g % 4] is not None:
                        wt("scalar", qP, stg_free[g % 4])
                    if kind == "k":
                        if ev_done.get(g - 4) is not None:
                            wt("scalar", eT, kt_last_T[0] if kt_last_T[0] else 0)
                    fn = AF.Silu if kind in ("q", "k") else AF.Copy
                    S(lambda s, o=stg, b=bank, f=fn: s.activation(o, b[:, :], f).then_inc(eS.h, 1))
                    ev_done[g] = eS.bump()
                    wt("scalar", eS, ev_done[g])
                    stg_free[g % 4] = dma("scalar",
                                          dstt[rb + m * 128: rb + m * 128 + 128, tt8 * 512:(tt8 + 1) * 512],
                                          stg, qP)
                    if kind == "k":
                        hh = (rb + m * 128) // 128
                        ksl = g % 2
                        wt("tensor", eS, ev_done[g])
                        if kt_ps[ksl] is not None:
                            wt("tensor", eV, kt_ps[ksl])
                        for i4 in range(4):
                            T(lambda t, i4=i4, stg=stg, ksl=ksl:
                              t.transpose(psKT[:, ksl * 512 + i4 * 128: ksl * 512 + (i4 + 1) * 128],
                                          stg[:, i4 * 128:(i4 + 1) * 128], id_bf[:, :]))
                        trv = tnop_inc()
                        kt_last_T[0] = trv
                        wt("vector", eT, trv)
                        if kt_free[ksl] is not None:
                            wt("vector", qP, kt_free[ksl])
                        for i4 in range(4):
                            t2g = (LB // 128) + tt8 * 4 + i4
                            par = t2g % 2
                            V(lambda v, i4=i4, ksl=ksl, hh=hh, par=par:
                              v.tensor_scalar(ktst[:, ksl * 512 + i4 * 128: ksl * 512 + (i4 + 1) * 128],
                                              psKT[:, ksl * 512 + i4 * 128: ksl * 512 + (i4 + 1) * 128],
                                              kdc[:, hh * 2 + par: hh * 2 + par + 1], None, ALU.mult))
                        kt_ps[ksl] = vinc_last()
                        wt("scalar", eV, kt_ps[ksl])
                        kt_free[ksl] = dma("scalar",
                                           s_ktm[hh, :, (LB // 128) + tt8 * 4: (LB // 128) + tt8 * 4 + 4, :],
                                           ktst[:, ksl * 512:(ksl + 1) * 512].rearrange("p (t e) -> p t e", t=4),
                                           qP)
                    g += 1
            w_used[sl] = eT.v
            pump_casts(3)

    kt_last_T = [None]
    fm_slices = []
    for s4 in range(4):
        fm_slices.append((s4 * 512, s4 * 512, "q", s_qfm))
    for s4 in range(4):
        fm_slices.append((D + s4 * 512, s4 * 512, "k", s_kfm))
    for s4 in range(4):
        fm_slices.append((3 * D + s4 * 512, s4 * 512, "u", s_ufm))
    emit_fm_slices(fm_slices, xfm, MAIN, xfm_tgt)

    def emit_roleswap(xsrc, xstride, t2base, vcol0, dst, do_silu_kd, first_wait):
        w_used = wslot_used
        psum_guard = {}
        bfree = [None, None]
        fw = first_wait
        ntm = (LB if xstride == LB else MAIN) // 128
        for vs in range(4):
            sl = vs % 2
            wv = wsl[:, sl * 16 * 512:(sl + 1) * 16 * 512]
            if w_used[sl] is not None:
                wt("gpsimd", eT, w_used[sl])
            wl = None
            for c in range(16):
                wl = dma("gpsimd", wv[:, c * 512:(c + 1) * 512],
                         wqT[c * 128:(c + 1) * 128, vcol0 + vs * 512: vcol0 + vs * 512 + 512], qG)
            wt("tensor", qG, wl)
            if fw is not None:
                wt("tensor", qS, fw)
            g = 0
            for tm in range(ntm):
                bank = psA[g % 4]
                if psum_guard.get(g - 4) is not None:
                    wt("tensor", psum_guard[g - 4][0], psum_guard[g - 4][1])
                for c in range(16):
                    T(lambda t, b=bank, w=wv, c=c, tm=tm, x=xsrc, stx=xstride:
                      t.matmul(b[:, :], x[:, c * stx + tm * 128: c * stx + tm * 128 + 128],
                               w[:, c * 512:(c + 1) * 512], start=(c == 0), stop=(c == 15)))
                mmv = tnop_inc()
                bsl = (tm // 4) % 2
                vbuf = vst2[:, bsl * 2048 + (tm % 4) * 512: bsl * 2048 + (tm % 4) * 512 + 512]
                wt("scalar", eT, mmv)
                if tm % 4 == 0 and bfree[bsl] is not None:
                    wt("scalar", qP, bfree[bsl])
                fn = AF.Silu if do_silu_kd else AF.Copy
                S(lambda s, o=vbuf, b=bank, f=fn: s.activation(o, b[:, :], f).then_inc(eS.h, 1))
                sv = eS.bump()
                if do_silu_kd:
                    wt("vector", eS, sv)
                    par = tm % 2
                    for hh4 in range(4):
                        V(lambda v, o=vbuf, hh=vs * 4 + hh4, par=par, h4=hh4:
                          v.tensor_scalar(o[:, h4 * 128:(h4 + 1) * 128], o[:, h4 * 128:(h4 + 1) * 128],
                                          kdc[:, hh * 2 + par: hh * 2 + par + 1], None, ALU.mult))
                    done = vinc_last()
                    psum_guard[g] = (eV, done)
                else:
                    done = sv
                    psum_guard[g] = (eS, done)
                if tm % 4 == 3:
                    gsem, dval = psum_guard[g]
                    wt("scalar", gsem, dval)
                    last = None
                    for hh4 in range(4):
                        last = dma("scalar", dst[vs * 4 + hh4, :, t2base + tm - 3: t2base + tm + 1, :],
                                   vst2[:, bsl * 2048: bsl * 2048 + 2048]
                                   .rearrange("p (t e) -> p t e", t=4)[:, :, hh4 * 128:(hh4 + 1) * 128], qP)
                    bfree[bsl] = last
                g += 1
            w_used[sl] = eT.v
            pump_casts(3)

    emit_roleswap(xfm, MAIN, LB // 128, 2 * D, s_vtm, False, None)
    bar("PA_main")
    if stop == "PA_main":
        finish()
        return nc

    xlb_tgt = None
    for c in range(16):
        for tt4 in range(4):
            xlb_tgt = dma("sync", xfm[:, c * LB + tt4 * 512: c * LB + tt4 * 512 + 512],
                          s_xln[tt4 * 512: tt4 * 512 + 512, c * 128:(c + 1) * 128],
                          qS, transpose=True)
    emit_roleswap(xfm, LB, 0, D, s_ktm, True, xlb_tgt)
    emit_roleswap(xfm, LB, 0, 2 * D, s_vtm, False, None)
    bar("PA_lb")
    if stop == "PA_lb":
        finish()
        return nc

    pa.close()
    cstk = ExitStack()
    wo_sb = cstk.enter_context(nc.sbuf_tensor("pc_wo", [128, 16 * D], BF16))
    pc = ExitStack()
    Ec = pc.enter_context
    qh_b = Ec(nc.sbuf_tensor("pb_qh", [128, 2 * MAIN], BF16))
    kfh_b = Ec(nc.sbuf_tensor("pb_kfh", [128, 2 * MAIN], BF16))
    kth_b = Ec(nc.sbuf_tensor("pb_kth", [128, 2 * 48 * 128], BF16))
    vth_b = Ec(nc.sbuf_tensor("pb_vth", [128, 2 * 48 * 128], BF16))
    oh_b = Ec(nc.sbuf_tensor("pb_oh", [128, 2 * MAIN], F32))
    qd_b = Ec(nc.sbuf_tensor("pb_qd", [128, 2 * 256], F32))
    dd_b = Ec(nc.sbuf_tensor("pb_dd", [128, 2 * 512], BF16))
    tmpd = Ec(nc.sbuf_tensor("pb_tmpd", [128, 512], F32))
    qs_b = Ec(nc.sbuf_tensor("pb_qs", [128, 2 * 256], BF16))
    sc_b = Ec(nc.sbuf_tensor("pb_sc", [128, 2 * 512], BF16))
    S_b = Ec(nc.sbuf_tensor("pb_S", [128, 128], F32))
    Sbf_b = Ec(nc.sbuf_tensor("pb_Sbf", [128, 2 * 128], BF16))
    psSC = [Ec(nc.psum_tensor("pb_psc%d" % i, [128, 512], F32)) for i in range(2)]
    psO = [Ec(nc.psum_tensor("pb_pso%d" % i, [128, 512], F32)) for i in range(2)]
    psAA = [Ec(nc.psum_tensor("pb_psa%d" % i, [128, 512], F32)) for i in range(2)]

    head_T_done = [None] * (HQ + 2)
    head_V_done = [None] * (HQ + 2)
    head_S_done = [None] * (HQ + 2)
    head_spill = [None] * (HQ + 2)
    head_load = [None] * HQ

    def emit_head_load(h):
        hs = h % 2
        if h >= 2:
            wt("sync", eT, head_T_done[h - 2])
            wt("sync", eV, head_V_done[h - 2])
        t1 = dma("sync", qh_b[:, hs * MAIN:(hs + 1) * MAIN], s_qfm[h * 128:(h + 1) * 128, :], qS)
        t1 = dma("sync", kfh_b[:, hs * MAIN:(hs + 1) * MAIN], s_kfm[h * 128:(h + 1) * 128, :], qS)
        t1 = dma("sync", kth_b[:, hs * 48 * 128:(hs + 1) * 48 * 128],
                 s_ktm[h].rearrange("p t e -> p (t e)"), qS)
        t1 = dma("sync", vth_b[:, hs * 48 * 128:(hs + 1) * 48 * 128],
                 s_vtm[h].rearrange("p t e -> p (t e)"), qS)
        head_load[h] = t1

    emit_head_load(0)
    emit_head_load(1)

    wo_tgt = None
    for c in range(16):
        wo_tgt = dma("gpsimd", wo_sb[:, c * D:(c + 1) * D], woT[c * 128:(c + 1) * 128, :], qG)

    pump_casts(len(cast_thunks))
    cast_done_tgt = ("qC", qC.v)

    if stop == "S4":
        bar("S4")
        finish()
        return nc


    for h in range(HQ):
        hs = h % 2
        qh = qh_b[:, hs * MAIN:(hs + 1) * MAIN]
        kfh = kfh_b[:, hs * MAIN:(hs + 1) * MAIN]
        kth = kth_b[:, hs * 48 * 128:(hs + 1) * 48 * 128]
        vth = vth_b[:, hs * 48 * 128:(hs + 1) * 48 * 128]
        oh = oh_b[:, hs * MAIN:(hs + 1) * MAIN]
        qd = qd_b[:, hs * 256:(hs + 1) * 256]
        dd = dd_b[:, hs * 512:(hs + 1) * 512]

        if h >= 1:
            wt("vector", eT, head_T_done[h - 1])
            wt("vector", eS, head_S_done[h - 1])
            wt("tensor", eV, head_V_done[h - 1])
            wt("tensor", eS, head_S_done[h - 1])
        if h >= 2:
            wt("scalar", qP, head_spill[h - 2])
        V(lambda v, qd=qd, h=h: v.tensor_scalar(qd, ar1_bc[:, :], sigb[:, h:h + 1], None, ALU.mult))
        V(lambda v, h=h: v.tensor_scalar(tmpd[:, :], dc_t[:, :], sigb[:, h:h + 1], None, ALU.mult).then_inc(eV.h, 1))
        tgen = eV.bump()
        wt("scalar", eV, tgen)
        S(lambda s, qd=qd: s.activation(qd, qd, AF.Exp, scale=-1.0))
        S(lambda s: s.activation(tmpd[:, :], tmpd[:, :], AF.Exp, scale=-1.0).then_inc(eS.h, 1))
        tgen2 = eS.bump()
        wt("vector", eS, tgen2)
        V(lambda v, dd=dd: v.tensor_tensor(dd, tmpd[:, :], msk_t[:, :], ALU.mult))
        V(lambda v: v.memset(S_b[:, :], 0.0).then_inc(eV.h, 1))
        dd_done = eV.bump()

        wt("tensor", qS, head_load[h])
        wt("vector", qS, head_load[h])

        sc_done = {}
        mask_done = {}
        a_done = {}
        o_done = {}
        oevac_done = {}
        sbf_done = {}
        qs_done = {}
        upd_hist = {-1: dd_done}

        for t in range(NBE):
            par = t % 2
            lb = t < NBL
            if upd_hist.get(t - 2) is not None:
                wt("tensor", eV, upd_hist[t - 2])
            for i2 in range(2):
                T(lambda t_, par=par, kth=kth, vth=vth, t2=t * 2 + i2, i2=i2:
                  t_.matmul(psAA[par][:, 0:128], kth[:, t2 * 128:(t2 + 1) * 128],
                            vth[:, t2 * 128:(t2 + 1) * 128], start=(i2 == 0), stop=(i2 == 1)))
            a_done[t] = tnop_inc()

            if not lb:
                tm = t - NBL
                wt("vector", eS, tgen2)
                if o_done.get(t - 2) is not None:
                    wt("vector", eT, o_done[t - 2])
                V(lambda v, par=par, qh=qh, tm=tm, qd=qd:
                  v.tensor_tensor(qs_b[:, par * 256:(par + 1) * 256], qd,
                                  qh[:, tm * 256:(tm + 1) * 256], ALU.mult).then_inc(eV.h, 1))
                qs_done[t] = eV.bump()
                if mask_done.get(t - 2) is not None:
                    wt("tensor", eV, mask_done[t - 2])
                for i2 in range(2):
                    T(lambda t_, par=par, kfh=kfh, qh=qh, tm=tm, i2=i2:
                      t_.matmul(psSC[par][:, i2 * 256:(i2 + 1) * 256],
                                kfh[:, tm * 256 + i2 * 128: tm * 256 + i2 * 128 + 128],
                                qh[:, tm * 256:(tm + 1) * 256], start=True, stop=True))
                sc_done[t] = tnop_inc()
                wt("vector", eT, sc_done[t])
                if o_done.get(t - 2) is not None:
                    wt("vector", eT, o_done[t - 2])
                for i2 in range(2):
                    V(lambda v, par=par, i2=i2, dd=dd:
                      v.tensor_tensor(sc_b[:, par * 512 + i2 * 256: par * 512 + (i2 + 1) * 256],
                                      psSC[par][:, i2 * 256:(i2 + 1) * 256],
                                      dd[:, i2 * 256:(i2 + 1) * 256], ALU.mult))
                mask_done[t] = vinc_last()
                wt("vector", eV, upd_hist[t - 1])
                if o_done.get(t - 2) is not None:
                    wt("vector", eT, o_done[t - 2])
                V(lambda v, par=par: v.tensor_copy(Sbf_b[:, par * 128:(par + 1) * 128], S_b[:, :]).then_inc(eV.h, 1))
                sbf_done[t] = eV.bump()
                wt("tensor", eV, mask_done[t])
                wt("tensor", eV, sbf_done[t])
                wt("tensor", eV, qs_done[t])
                if oevac_done.get(t - 2) is not None:
                    wt("tensor", eS, oevac_done[t - 2])
                for i2 in range(2):
                    T(lambda t_, par=par, vth=vth, t2=t * 2 + i2, i2=i2:
                      t_.matmul(psO[par][:, 0:256], vth[:, t2 * 128:(t2 + 1) * 128],
                                sc_b[:, par * 512 + i2 * 256: par * 512 + (i2 + 1) * 256],
                                start=(i2 == 0), stop=False))
                T(lambda t_, par=par: t_.matmul(psO[par][:, 0:256], Sbf_b[:, par * 128:(par + 1) * 128],
                                                qs_b[:, par * 256:(par + 1) * 256], start=False, stop=True))
                o_done[t] = tnop_inc()
                wt("scalar", eT, o_done[t])
                S(lambda s, par=par, oh=oh, tm=tm:
                  s.activation(oh[:, tm * 256:(tm + 1) * 256], psO[par][:, 0:256], AF.Copy).then_inc(eS.h, 1))
                oevac_done[t] = eS.bump()

            wt("vector", eT, a_done[t])
            if not lb:
                wt("vector", eV, sbf_done[t])
            V(lambda v, par=par, h=h:
              v.scalar_tensor_tensor(S_b[:, :], S_b[:, :], lamb[:, h:h + 1], psAA[par][:, 0:128],
                                     ALU.mult, ALU.add).then_inc(eV.h, 1))
            upd_hist[t] = eV.bump()

        head_T_done[h] = eT.v
        head_V_done[h] = eV.v
        head_S_done[h] = eS.v
        wt("scalar", eS, oevac_done[NBE - 1])
        head_spill[h] = dma("scalar", s_ofm[h * 128:(h + 1) * 128, :], oh, qP)
        if h + 2 < HQ:
            emit_head_load(h + 2)
    bar("PB_scan", wait_qC=False)
    if stop == "PB_scan":
        finish()
        return nc

    pc.close()
    pd = ExitStack()
    Ed = pd.enter_context
    o_sb = Ed(nc.sbuf_tensor("pc_o", [128, 16 * 512], F32))
    u_sb = Ed(nc.sbuf_tensor("pc_u", [128, 16 * 512], BF16))
    oum = Ed(nc.sbuf_tensor("pc_oum", [128, 16 * 512], BF16))
    sq_c = Ed(nc.sbuf_tensor("pc_sq", [128, 2 * 512], F32))
    xm_c = Ed(nc.sbuf_tensor("pc_xm", [128, 2 * D], F32))
    xmid = Ed(nc.sbuf_tensor("pc_xmid", [128, 2 * D], F32))
    rrow = Ed(nc.sbuf_tensor("pc_rrow", [1, 512], F32))
    l2c = Ed(nc.sbuf_tensor("pc_l2c", [128, 4], F32))
    xl2f = Ed(nc.sbuf_tensor("pc_xl2f", [128, D], F32))
    xl2s = Ed(nc.sbuf_tensor("pc_xl2s", [128, 2 * D], BF16))
    psW = [Ed(nc.psum_tensor("pc_psw%d" % i, [128, 512], F32)) for i in range(4)]
    psQ = Ed(nc.psum_tensor("pc_psq", [128, 512], F32))
    psB2 = Ed(nc.psum_tensor("pc_psb", [128, 512], F32))

    ssqmm = {}
    prev_xm_done = [None]
    oum_used = None
    xm_used = [None, None]
    xmid_used = [None, None]
    stg_used = [None, None]
    for mt in range(NMT):
        wt("vector", eT, eT.v)
        if oum_used is not None:
            wt("sync", eV, oum_used)
        lo = None
        for c in range(16):
            lo = dma("sync", o_sb[:, c * 512:(c + 1) * 512],
                     s_ofm[c * 128:(c + 1) * 128, mt * 512:(mt + 1) * 512], qS)
            lo = dma("sync", u_sb[:, c * 512:(c + 1) * 512],
                     s_ufm[c * 128:(c + 1) * 128, mt * 512:(mt + 1) * 512], qS)
        wt("vector", qS, lo)
        for c in range(16):
            ssl = c % 2
            if ssqmm.get((mt, c - 2)) is not None:
                wt("vector", eT, ssqmm[(mt, c - 2)])
            V(lambda v, c=c, ssl=ssl: v.tensor_tensor(sq_c[:, ssl * 512:(ssl + 1) * 512],
                                                      o_sb[:, c * 512:(c + 1) * 512],
                                                      o_sb[:, c * 512:(c + 1) * 512], ALU.mult).then_inc(eV.h, 1))
            sqd = eV.bump()
            wt("tensor", eV, sqd)
            T(lambda t, c=c, ssl=ssl: t.matmul(psQ[0:1, :], ones_c_f[:, 0:1],
                                               sq_c[:, ssl * 512:(ssl + 1) * 512],
                                               start=(c == 0), stop=(c == 15)))
            ssqmm[(mt, c)] = tnop_inc()
        wt("scalar", eT, ssqmm[(mt, 15)])
        S(lambda s: s.activation(rrow[0:1, :], psQ[0:1, :], AF.Ln, bias=eps6[0:1, 0:1], scale=1.0 / D))
        S(lambda s: s.activation(rrow[0:1, :], rrow[0:1, :], AF.Exp, scale=-0.5).then_inc(eS.h, 1))
        rs_done = eS.bump()
        wt("tensor", eS, rs_done)
        T(lambda t: t.matmul(psB2[:, :], ones_r_f[0:1, :], rrow[0:1, :], start=True, stop=True))
        bc_done = tnop_inc()
        wt("vector", eT, bc_done)
        for c in range(16):
            V(lambda v, c=c: v.tensor_tensor(o_sb[:, c * 512:(c + 1) * 512],
                                             o_sb[:, c * 512:(c + 1) * 512],
                                             u_sb[:, c * 512:(c + 1) * 512], ALU.mult))
        ou1 = vinc_last()
        wt("vector", eV, ou1)
        for c in range(16):
            V(lambda v, c=c: v.tensor_tensor(oum[:, c * 512:(c + 1) * 512],
                                             o_sb[:, c * 512:(c + 1) * 512],
                                             psB2[:, :], ALU.mult))
        oum_done = vinc_last()
        oum_used = oum_done

        for t4 in range(4):
            xsl = t4 % 2
            glob_t = mt * 512 + t4 * 128
            if xm_used[xsl] is not None:
                wt("sync", eV, xm_used[xsl])
            xl = dma("sync", xm_c[:, xsl * D:(xsl + 1) * D],
                     x_ext[LB + glob_t: LB + glob_t + 128, :], qS)
            wt("tensor", eV, oum_done)
            wt("tensor", qG, wo_tgt)
            if prev_xm_done[0] is not None:
                wt("tensor", eV, prev_xm_done[0])
            for isl in range(4):
                for c in range(16):
                    T(lambda t, b=psW[isl], c=c, t4=t4, isl=isl:
                      t.matmul(b[:, :], oum[:, c * 512 + t4 * 128: c * 512 + t4 * 128 + 128],
                               wo_sb[:, c * D + isl * 512: c * D + isl * 512 + 512],
                               start=(c == 0), stop=(c == 15)))
            wmm = tnop_inc()
            wt("vector", eT, wmm)
            wt("vector", qS, xl)
            if xmid_used[xsl] is not None:
                wt("vector", qP, xmid_used[xsl])
            for isl in range(4):
                V(lambda v, isl=isl, xsl=xsl:
                  v.tensor_tensor(xmid[:, xsl * D + isl * 512: xsl * D + isl * 512 + 512],
                                  psW[isl][:, :], xm_c[:, xsl * D + isl * 512: xsl * D + isl * 512 + 512], ALU.add))
            xm_done = vinc_last()
            prev_xm_done[0] = xm_done
            xm_used[xsl] = xm_done
            xv = xmid[:, xsl * D:(xsl + 1) * D]
            wt("vector", eV, xm_done)
            V(lambda v, xv=xv: v.tensor_tensor(xl2f[:, :], xv, xv, ALU.mult).then_inc(eV.h, 1))
            lt1 = eV.bump()
            wt("vector", eV, lt1)
            V(lambda v: v.reduce_sum(l2c[:, 1:2], xl2f[:, :], axis=AX.X))
            V(lambda v, xv=xv: v.reduce_sum(l2c[:, 0:1], xv, axis=AX.X).then_inc(eV.h, 1))
            lt2 = eV.bump()
            wt("vector", eV, lt2)
            V(lambda v: v.tensor_scalar(l2c[:, 2:3], l2c[:, 0:1], 1.0 / D, None, ALU.mult))
            V(lambda v: v.tensor_scalar(l2c[:, 3:4], l2c[:, 1:2], 1.0 / D, None, ALU.mult).then_inc(eV.h, 1))
            lt3 = eV.bump()
            wt("vector", eV, lt3)
            V(lambda v: v.tensor_tensor(l2c[:, 0:1], l2c[:, 2:3], l2c[:, 2:3], ALU.mult).then_inc(eV.h, 1))
            lt4 = eV.bump()
            wt("vector", eV, lt4)
            V(lambda v: v.tensor_tensor(l2c[:, 3:4], l2c[:, 3:4], l2c[:, 0:1], ALU.subtract).then_inc(eV.h, 1))
            lt5 = eV.bump()
            wt("scalar", eV, lt5)
            S(lambda s: s.activation(l2c[:, 3:4], l2c[:, 3:4], AF.Ln, bias=eps5[:, 0:1]))
            S(lambda s: s.activation(l2c[:, 3:4], l2c[:, 3:4], AF.Exp, scale=-0.5).then_inc(eS.h, 1))
            lt6 = eS.bump()
            wt("vector", eS, lt6)
            V(lambda v, xv=xv: v.tensor_scalar(xl2f[:, :], xv, l2c[:, 2:3], l2c[:, 3:4],
                                               ALU.subtract, ALU.mult).then_inc(eV.h, 1))
            lt7 = eV.bump()
            wt("vector", eV, lt7)
            if stg_used[xsl] is not None:
                wt("vector", qP, stg_used[xsl])
            V(lambda v, xv=xv: v.tensor_tensor(xv, xv, xl2f[:, :], ALU.add))
            V(lambda v, xsl=xsl: v.tensor_copy(xl2s[:, xsl * D:(xsl + 1) * D], xl2f[:, :]).then_inc(eV.h, 1))
            lt8 = eV.bump()
            wt("scalar", eV, lt8)
            dma("scalar", s_xr2[glob_t: glob_t + 128, :], xv, qP)
            stg_used[xsl] = dma("scalar", s_xl2[glob_t: glob_t + 128, :], xl2s[:, xsl * D:(xsl + 1) * D], qP)
            xmid_used[xsl] = stg_used[xsl]
    bar("PC_epilogue")
    if stop == "PC_epilogue":
        finish()
        return nc

    pd.close()
    cstk.close()
    dstk.close()
    pe = ExitStack()
    Ee = pe.enter_context
    wsl_d = Ee(nc.sbuf_tensor("pd_wsl", [128, 3 * 8192], BF16))
    h2w_d = Ee(nc.sbuf_tensor("pd_h2w", [128, NEXP * 8 * 512], BF16))
    sha_d = Ee(nc.sbuf_tensor("pd_sha", [128, 16 * 512], BF16))
    u1_d = Ee(nc.sbuf_tensor("pd_u1", [128, 8192], BF16))
    xfm2_d = Ee(nc.sbuf_tensor("pd_xfm2", [128, 16 * 512], BF16))
    xt1_d = Ee(nc.sbuf_tensor("pd_xt1", [128, 16 * 512], BF16))
    wbc_d = Ee(nc.sbuf_tensor("pd_wbc", [128, NEXP * 512], BF16))
    g_d = Ee(nc.sbuf_tensor("pd_g", [128, 104], F32))
    cmp_d = Ee(nc.sbuf_tensor("pd_cmp", [128, 128], F32))
    wtm_d = Ee(nc.sbuf_tensor("pd_wtm", [128, 32], F32))
    wfm_d = Ee(nc.sbuf_tensor("pd_wfm", [8, 512], BF16))
    tmp_d = Ee(nc.sbuf_tensor("pd_tmp", [128, 2 * 512], F32))
    o1s_d = Ee(nc.sbuf_tensor("pd_o1s", [128, 2 * 512], BF16))
    o1tm_d = Ee(nc.sbuf_tensor("pd_o1tm", [128, D], BF16))
    xr2_d = Ee(nc.sbuf_tensor("pd_xr2", [128, D], F32))
    psD = [Ee(nc.psum_tensor("pd_ps%d" % i, [128, 512], F32)) for i in range(8)]

    sh_v = u1_d[:, 0:8192]
    h_v = u1_d[:, 0:4096]
    si_v = u1_d[:, 4096:8192]

    wt("sync", qC, cast_done_tgt[1])

    slab_load = {}
    slab_used = {}
    slab_ctr = [0]

    def slab(pieces):
        i = slab_ctr[0]
        slab_ctr[0] += 1
        sl = i % 3
        view = wsl_d[:, sl * 8192:(sl + 1) * 8192]
        if i >= 3:
            wt("sync", eT, slab_used[i - 3])
        tgt = None
        for (c0, c1), src in pieces:
            tgt = dma("sync", view[:, c0:c1], src, qS)
        slab_load[i] = tgt
        return view, i

    def slab_end(i, tgt=None):
        slab_used[i] = tgt if tgt is not None else tnop_inc()

    def fence_T():
        wt("tensor", eS, eS.v)
        wt("tensor", eV, eV.v)

    y_final = [None]
    o1tm_used = [None]
    o1s_used = [None, None]
    evd = {}
    xt1_ready = None

    o1_store_done = {}

    def emit_final(fmt):
        fosl = fmt % 2
        for t4 in range(4):
            wt("scalar", qP, o1_store_done[fmt])
            if o1tm_used[0] is not None:
                wt("scalar", eV, o1tm_used[0])
            o1t_tgt = None
            for hs4 in range(4):
                o1t_tgt = dma("scalar", o1tm_d[:, hs4 * 512:(hs4 + 1) * 512],
                              s_o1[fosl, hs4 * 512:(hs4 + 1) * 512, t4 * 128:(t4 + 1) * 128], qP, transpose=True)
            if y_final[0] is not None:
                wt("gpsimd", qY, y_final[0])
            xr_t = dma("gpsimd", xr2_d[:, :],
                       s_xr2[fmt * 512 + t4 * 128: fmt * 512 + (t4 + 1) * 128, :], qG)
            wt("vector", qP, o1t_tgt)
            wt("vector", qG, xr_t)
            V(lambda v: v.tensor_tensor(xr2_d[:, :], xr2_d[:, :],
                                        o1tm_d[:, :], ALU.add).then_inc(eV.h, 1))
            yv = eV.bump()
            o1tm_used[0] = yv
            wt("gpsimd", eV, yv)
            y_final[0] = dma("gpsimd", y_out[fmt * 512 + t4 * 128: fmt * 512 + (t4 + 1) * 128, :], xr2_d[:, :], qY)

    xf_next = [None]
    r0_end_T = [None]
    for mt in range(NMT):
        osl = mt % 2
        wt("scalar", eT, eT.v)
        wt("scalar", eV, eV.v)
        wt("vector", eT, eT.v)
        if mt == 0:
            xf_tgt = None
            for c in range(16):
                xf_tgt = dma("scalar", xfm2_d[:, c * 512:(c + 1) * 512],
                             s_xl2[mt * 512:(mt + 1) * 512, c * 128:(c + 1) * 128], qP, transpose=True)
        else:
            xf_tgt = xf_next[0]
        for r in range(2):
            xt = xfm2_d if r == 0 else xt1_d
            if r == 0:
                wt("tensor", qP, xf_tgt)
                wt("vector", qP, xf_tgt)
            else:
                wt("tensor", eV, xt1_ready)
            fence_T()
            for t4 in range(4):
                gb = 6 + (t4 % 2)
                gv = g_d[:, t4 * 8:(t4 + 1) * 8]
                if evd.get(("g", mt, r, t4 - 2)) is not None:
                    wt("tensor", eV, evd[("g", mt, r, t4 - 2)])
                for c in range(16):
                    T(lambda t, c=c, t4=t4, xt=xt, r=r, gb=gb:
                      t.matmul(psD[gb][:, t4 * 8:(t4 + 1) * 8],
                               xt[:, c * 512 + t4 * 128: c * 512 + t4 * 128 + 128],
                               wg_sb[:, (r * 16 + c) * NE:(r * 16 + c + 1) * NE],
                               start=(c == 0), stop=(c == 15)))
                gmm = tnop_inc()
                wt("vector", eT, gmm)
                V(lambda v, gv=gv, t4=t4, r=r, gb=gb: v.tensor_tensor(gv, psD[gb][:, t4 * 8:(t4 + 1) * 8],
                                                               bg_bc[:, r * NE:(r + 1) * NE], ALU.add).then_inc(eV.h, 1))
                evd[("g", mt, r, t4)] = eV.bump()
                gj = gv.unsqueeze(1).broadcast_to([128, 8, 8])
                ge = gv.unsqueeze(2).broadcast_to([128, 8, 8])
                rk = g_d[:, 32 + t4 * 8: 40 + t4 * 8]
                mk = g_d[:, 64 + t4 * 8: 72 + t4 * 8]
                ex = wtm_d[:, t4 * 8:(t4 + 1) * 8]
                mx = g_d[:, 96 + t4: 97 + t4]
                wt("vector", eV, evd[("g", mt, r, t4)])
                V(lambda v, gj=gj, ge=ge: v.tensor_tensor(
                    cmp_d[:, 0:64].rearrange("p (a b) -> p a b", a=8), gj, ge, ALU.is_gt))
                V(lambda v, gj=gj, ge=ge: v.tensor_tensor(
                    cmp_d[:, 64:128].rearrange("p (a b) -> p a b", a=8), gj, ge, ALU.is_equal).then_inc(eV.h, 1))
                c1 = eV.bump()
                wt("vector", eV, c1)
                V(lambda v: v.tensor_tensor(cmp_d[:, 64:128], cmp_d[:, 64:128], ltm_t[:, :], ALU.mult).then_inc(eV.h, 1))
                c2 = eV.bump()
                wt("vector", eV, c2)
                V(lambda v: v.tensor_tensor(cmp_d[:, 0:64], cmp_d[:, 0:64], cmp_d[:, 64:128], ALU.add).then_inc(eV.h, 1))
                c3 = eV.bump()
                wt("vector", eV, c3)
                V(lambda v, rk=rk: v.reduce_sum(rk, cmp_d[:, 0:64].rearrange("p (a b) -> p a b", a=8),
                                                axis=AX.X).then_inc(eV.h, 1))
                c4 = eV.bump()
                wt("vector", eV, c4)
                V(lambda v, rk=rk, mk=mk: v.tensor_scalar(mk, rk, 6.0, None, ALU.is_lt).then_inc(eV.h, 1))
                c5 = eV.bump()
                wt("vector", eV, c5)
                V(lambda v, mk=mk, ex=ex: v.tensor_scalar(ex, mk, 1.0, 1e30, ALU.subtract, ALU.mult).then_inc(eV.h, 1))
                c6 = eV.bump()
                wt("vector", eV, c6)
                V(lambda v, gv=gv, ex=ex: v.scalar_tensor_tensor(ex, gv, 1.0, ex, ALU.mult, ALU.add).then_inc(eV.h, 1))
                c7 = eV.bump()
                wt("vector", eV, c7)
                V(lambda v, ex=ex, mx=mx: v.reduce_max(mx, ex, axis=AX.X).then_inc(eV.h, 1))
                c8 = eV.bump()
                wt("vector", eV, c8)
                V(lambda v, ex=ex, mx=mx: v.tensor_scalar(ex, ex, mx[:, 0:1], None, ALU.subtract).then_inc(eV.h, 1))
                evd[("gp1", mt, r, t4)] = eV.bump()
            if r == 1 and mt + 1 < NMT:
                wt("scalar", eT, r0_end_T[0])
                wt("scalar", eV, xt1_ready)
                nxt = None
                for c in range(16):
                    nxt = dma("scalar", xfm2_d[:, c * 512:(c + 1) * 512],
                              s_xl2[(mt + 1) * 512:(mt + 2) * 512, c * 128:(c + 1) * 128], qP, transpose=True)
                xf_next[0] = nxt
            for mg in range(2):
                vwA, siA = slab([((kc * 1024, (kc + 1) * 1024),
                                  z_su[kc * 128:(kc + 1) * 128, mg * 1024:(mg + 1) * 1024]) for kc in range(8)])
                vwB, siB = slab([((kc * 1024, (kc + 1) * 1024),
                                  z_su[1024 + kc * 128: 1024 + (kc + 1) * 128, mg * 1024:(mg + 1) * 1024]) for kc in range(8)])
                wt("tensor", qS, slab_load[siA])
                for m8 in range(8):
                    m16 = mg * 8 + m8
                    bank = psD[m8 % 4]
                    if evd.get(("su", mt, r, m16 - 4)) is not None:
                        wt("tensor", eS, evd[("su", mt, r, m16 - 4)])
                    for kc in range(8):
                        T(lambda t, b=bank, vw=vwA, kc=kc, m8=m8, xt=xt:
                          t.matmul(b[:, :], vw[:, kc * 1024 + m8 * 128: kc * 1024 + m8 * 128 + 128],
                                   xt[:, kc * 512:(kc + 1) * 512], start=(kc == 0), stop=False))
                    if m8 == 0:
                        wt("tensor", qS, slab_load[siB])
                    for kc in range(8):
                        T(lambda t, b=bank, vw=vwB, kc=kc, m8=m8, xt=xt:
                          t.matmul(b[:, :], vw[:, kc * 1024 + m8 * 128: kc * 1024 + m8 * 128 + 128],
                                   xt[:, (8 + kc) * 512:(8 + kc + 1) * 512], start=False, stop=(kc == 7)))
                    suv = tnop_inc()
                    wt("scalar", eT, suv)
                    S(lambda s, m16=m16, b=bank: s.activation(sh_v[:, m16 * 512:(m16 + 1) * 512], b[:, :],
                                                              AF.Identity, bias=bias_cols[:, BSU0 + m16: BSU0 + m16 + 1]).then_inc(eS.h, 1))
                    evd[("su", mt, r, m16)] = eS.bump()
                slab_end(siA, eT.v)
                slab_end(siB, eT.v)
            if r == 0 and mt > 0:
                emit_final(mt - 1)
            for t4 in range(4):
                ex = wtm_d[:, t4 * 8:(t4 + 1) * 8]
                sm = g_d[:, 100 + t4: 101 + t4]
                wt("scalar", eV, evd[("gp1", mt, r, t4)])
                S(lambda s, ex=ex: s.activation(ex, ex, AF.Exp).then_inc(eS.h, 1))
                c10 = eS.bump()
                wt("vector", eS, c10)
                V(lambda v, ex=ex, sm=sm: v.reduce_sum(sm, ex, axis=AX.X).then_inc(eV.h, 1))
                c11 = eV.bump()
                wt("vector", eV, c11)
                V(lambda v, sm=sm: v.reciprocal(sm, sm).then_inc(eV.h, 1))
                c12 = eV.bump()
                wt("vector", eV, c12)
                V(lambda v, ex=ex, sm=sm: v.tensor_scalar(ex, ex, sm[:, 0:1], None, ALU.mult).then_inc(eV.h, 1))
                evd[("w", mt, r, t4)] = eV.bump()
            vw, si_ = slab([((kc * 1024, (kc + 1) * 1024),
                             z_sa[kc * 128:(kc + 1) * 128, :]) for kc in range(8)])
            wt("tensor", qS, slab_load[si_])
            wt("tensor", eS, evd[("su", mt, r, 15)])
            for sslot in range(2):
                for m8 in range(8):
                    bank = psD[4 + (m8 % 2)]
                    if evd.get(("sa", mt, r, sslot, m8 - 2)) is not None:
                        wt("tensor", eV, evd[("sa", mt, r, sslot, m8 - 2)])
                    elif sslot == 1 and m8 < 2:
                        wt("tensor", eV, evd[("sa", mt, r, 0, 6 + m8)])
                    for kc in range(8):
                        T(lambda t, b=bank, vw=vw, kc=kc, m8=m8, sslot=sslot:
                          t.matmul(b[:, :], vw[:, kc * 1024 + m8 * 128: kc * 1024 + m8 * 128 + 128],
                                   sh_v[:, (sslot * 8 + kc) * 512:(sslot * 8 + kc + 1) * 512],
                                   start=(kc == 0), stop=(kc == 7)))
                    sav = tnop_inc()
                    wt("scalar", eT, sav)
                    S(lambda s, m8=m8, b=bank:
                      s.activation(tmp_d[:, (m8 % 2) * 512:(m8 % 2 + 1) * 512], b[:, :], AF.Silu,
                                   bias=bias_cols[:, BSA0 + m8: BSA0 + m8 + 1]).then_inc(eS.h, 1))
                    sev = eS.bump()
                    wt("vector", eS, sev)
                    V(lambda v, m8=m8, sslot=sslot:
                      v.tensor_tensor(sha_d[:, (sslot * 8 + m8) * 512:(sslot * 8 + m8 + 1) * 512],
                                      tmp_d[:, (m8 % 2) * 512:(m8 % 2 + 1) * 512],
                                      sh_v[:, (sslot * 8 + m8) * 512:(sslot * 8 + m8 + 1) * 512],
                                      ALU.mult).then_inc(eV.h, 1))
                    evd[("sa", mt, r, sslot, m8)] = eV.bump()
            slab_end(si_, eT.v)
            sa_S_end = eS.v
            wt("tensor", eS, sa_S_end)
            wt("tensor", eV, evd[("g", mt, r, 3)])
            for t4 in range(4):
                wt("tensor", eV, evd[("w", mt, r, t4)])
                T(lambda t, t4=t4: t.transpose(psD[5][0:8, t4 * 128:(t4 + 1) * 128],
                                               wtm_d[:, t4 * 8:(t4 + 1) * 8], id_f[:, :]))
            wtr = tnop_inc()
            wt("vector", eT, wtr)
            V(lambda v: v.tensor_copy(wfm_d[:, :], psD[5][0:8, 0:512]).then_inc(eV.h, 1))
            wfm_done = eV.bump()
            wt("tensor", eV, wfm_done)
            for n in range(NEXP):
                T(lambda t, n=n: t.matmul(psD[7][:, :], sel_t[:, n * 128:(n + 1) * 128],
                                          wfm_d[0:8, :], start=True, stop=True))
                bcv = tnop_inc()
                wt("vector", eT, bcv)
                V(lambda v, n=n: v.tensor_copy(wbc_d[:, n * 512:(n + 1) * 512], psD[7][:, :]).then_inc(eV.h, 1))
                bev = eV.bump()
                wt("tensor", eV, bev)
            for n in range(NEXP):
                vwA, siA = slab([((kc * 1024, (kc + 1) * 1024),
                                  z_up[n, kc * 128:(kc + 1) * 128, :]) for kc in range(8)])
                vwB, siB = slab([((kc * 1024, (kc + 1) * 1024),
                                  z_up[n, 1024 + kc * 128: 1024 + (kc + 1) * 128, :]) for kc in range(8)])
                wt("tensor", qS, slab_load[siA])
                if n == 0:
                    wt("tensor", eV, evd[("sa", mt, r, 1, 7)])
                for m8 in range(8):
                    bank = psD[m8 % 4]
                    if evd.get(("up", mt, r, n, m8 - 4)) is not None:
                        wt("tensor", eS, evd[("up", mt, r, n, m8 - 4)])
                    elif n == 0 and m8 < 4:
                        wt("tensor", eS, evd[("su", mt, r, 12 + m8)])
                    elif n > 0 and m8 < 4:
                        wt("tensor", eS, evd[("up", mt, r, n - 1, 4 + m8)])
                    for kc in range(8):
                        T(lambda t, b=bank, vw=vwA, kc=kc, m8=m8, xt=xt:
                          t.matmul(b[:, :], vw[:, kc * 1024 + m8 * 128: kc * 1024 + m8 * 128 + 128],
                                   xt[:, kc * 512:(kc + 1) * 512], start=(kc == 0), stop=False))
                    if m8 == 0:
                        wt("tensor", qS, slab_load[siB])
                    for kc in range(8):
                        T(lambda t, b=bank, vw=vwB, kc=kc, m8=m8, xt=xt:
                          t.matmul(b[:, :], vw[:, kc * 1024 + m8 * 128: kc * 1024 + m8 * 128 + 128],
                                   xt[:, (8 + kc) * 512:(8 + kc + 1) * 512], start=False, stop=(kc == 7)))
                    upv = tnop_inc()
                    wt("scalar", eT, upv)
                    S(lambda s, m8=m8, n=n: s.activation(h_v[:, m8 * 512:(m8 + 1) * 512], psD[m8 % 4][:, :],
                                                         AF.Identity, bias=bias_cols[:, BUP0 + n * 8 + m8: BUP0 + n * 8 + m8 + 1]).then_inc(eS.h, 1))
                    evd[("up", mt, r, n, m8)] = eS.bump()
                slab_end(siA, eT.v)
                slab_end(siB, eT.v)
                vw, si_ = slab([((kc * 1024, (kc + 1) * 1024),
                                 z_sw[n, kc * 128:(kc + 1) * 128, :]) for kc in range(8)])
                wt("tensor", qS, slab_load[si_])
                wt("tensor", eS, evd[("up", mt, r, n, 7)])
                for m8 in range(8):
                    bank = psD[4 + (m8 % 2)]
                    if evd.get(("sw", mt, r, n, m8 - 2)) is not None:
                        wt("tensor", eV, evd[("sw", mt, r, n, m8 - 2)])
                    elif n == 0 and m8 < 2:
                        wt("tensor", eS, sa_S_end)
                    elif n > 0 and m8 < 2:
                        wt("tensor", eV, evd[("sw", mt, r, n - 1, 6 + m8)])
                    for kc in range(8):
                        T(lambda t, b=bank, vw=vw, kc=kc, m8=m8:
                          t.matmul(b[:, :], vw[:, kc * 1024 + m8 * 128: kc * 1024 + m8 * 128 + 128],
                                   h_v[:, kc * 512:(kc + 1) * 512], start=(kc == 0), stop=(kc == 7)))
                    swv = tnop_inc()
                    wt("scalar", eT, swv)
                    S(lambda s, m8=m8, n=n: s.activation(si_v[:, m8 * 512:(m8 + 1) * 512], psD[4 + (m8 % 2)][:, :],
                                                         AF.Silu, bias=bias_cols[:, BSW0 + n * 8 + m8: BSW0 + n * 8 + m8 + 1]).then_inc(eS.h, 1))
                    sev2 = eS.bump()
                    wt("vector", eS, sev2)
                    V(lambda v, m8=m8: v.tensor_tensor(si_v[:, m8 * 512:(m8 + 1) * 512],
                                                       si_v[:, m8 * 512:(m8 + 1) * 512],
                                                       h_v[:, m8 * 512:(m8 + 1) * 512], ALU.mult).then_inc(eV.h, 1))
                    h2a = eV.bump()
                    wt("vector", eV, h2a)
                    V(lambda v, m8=m8, n=n: v.tensor_tensor(h2w_d[:, (n * 8 + m8) * 512:(n * 8 + m8 + 1) * 512],
                                                            si_v[:, m8 * 512:(m8 + 1) * 512],
                                                            wbc_d[:, n * 512:(n + 1) * 512], ALU.mult).then_inc(eV.h, 1))
                    evd[("sw", mt, r, n, m8)] = eV.bump()
                slab_end(si_, eT.v)
            fence_T()
            tmp_used = {}
            for oc in range(2):
                for n in range(NEXP):
                    vw, si_ = slab([((kc * 1024, (kc + 1) * 1024),
                                     z_dn[n, oc, kc * 128:(kc + 1) * 128, :]) for kc in range(8)])
                    wt("tensor", qS, slab_load[si_])
                    if n == 0 and oc == 1:
                        wt("tensor", eS, evd[("dnS", mt, r, 0)])
                    for ht in range(8):
                        for kc in range(8):
                            T(lambda t, b=psD[ht], vw=vw, kc=kc, ht=ht, n=n:
                              t.matmul(b[:, :], vw[:, kc * 1024 + ht * 128: kc * 1024 + ht * 128 + 128],
                                       h2w_d[:, (n * 8 + kc) * 512:(n * 8 + kc + 1) * 512],
                                       start=(n == 0 and kc == 0), stop=False))
                    slab_end(si_)
                for half in range(2):
                    vw, si_ = slab([((kc * 1024, (kc + 1) * 1024),
                                     z_sd[oc, half * 1024 + kc * 128: half * 1024 + (kc) * 128 + 128, :]) for kc in range(8)])
                    wt("tensor", qS, slab_load[si_])
                    for ht in range(8):
                        for kc in range(8):
                            T(lambda t, b=psD[ht], vw=vw, kc=kc, ht=ht, half=half:
                              t.matmul(b[:, :], vw[:, kc * 1024 + ht * 128: kc * 1024 + ht * 128 + 128],
                                       sha_d[:, (half * 8 + kc) * 512:(half * 8 + kc + 1) * 512],
                                       start=False, stop=False))
                    slab_end(si_)
                for ht in range(8):
                    T(lambda t, ht=ht, oc=oc: t.matmul(psD[ht][:, :],
                                                       bdn_bf[0:6, oc * 1024 + ht * 128: oc * 1024 + ht * 128 + 128],
                                                       wfm_d[0:6, :], start=False, stop=True))
                dn_done = tnop_inc()
                for ht in range(8):
                    hglob = oc * 8 + ht
                    tsl = ht % 2
                    wt("scalar", eT, dn_done)
                    if r == 0:
                        if tmp_used.get(tsl) is not None:
                            wt("scalar", eV, tmp_used[tsl])
                        S(lambda s, ht=ht, hg=hglob, tsl=tsl:
                          s.activation(tmp_d[:, tsl * 512:(tsl + 1) * 512], psD[ht][:, :],
                                       AF.Identity, bias=bias_cols[:, BSD0 + hg: BSD0 + hg + 1]).then_inc(eS.h, 1))
                        sev3 = eS.bump()
                        wt("vector", eS, sev3)
                        V(lambda v, hg=hglob, tsl=tsl: v.tensor_tensor(xt1_d[:, hg * 512:(hg + 1) * 512],
                                                                       tmp_d[:, tsl * 512:(tsl + 1) * 512],
                                                                       xfm2_d[:, hg * 512:(hg + 1) * 512],
                                                                       ALU.add).then_inc(eV.h, 1))
                        tmp_used[tsl] = eV.bump()
                    else:
                        if o1s_used[tsl] is not None:
                            wt("scalar", qP, o1s_used[tsl])
                        S(lambda s, ht=ht, hg=hglob, tsl=tsl:
                          s.activation(o1s_d[:, tsl * 512:(tsl + 1) * 512], psD[ht][:, :],
                                       AF.Identity, bias=bias_cols[:, BSD0 + hg: BSD0 + hg + 1]).then_inc(eS.h, 1))
                        sev3 = eS.bump()
                        wt("scalar", eS, sev3)
                        o1s_used[tsl] = dma("scalar", s_o1[osl, hglob * 128:(hglob + 1) * 128, :],
                                            o1s_d[:, tsl * 512:(tsl + 1) * 512], qP)
                evd[("dnS", mt, r, oc)] = eS.v
            if r == 0:
                xt1_ready = eV.v
                r0_end_T[0] = eT.v
            else:
                o1_store_done[mt] = qP.v
    emit_final(NMT - 1)
    bar("PD_ffn")
    finish()
    return nc


_CACHE = {}


def _get_prog(dbg=()):
    key = tuple(dbg)
    if key not in _CACHE:
        _CACHE[key] = build(key)
    return _CACHE[key]


def make_in_maps(inputs):
    x = np.asarray(inputs["L0_attn_x"], dtype=np.float32)
    base = {
        "slope": np.ascontiguousarray(np.asarray(inputs["L0_attn_slope_rate"], np.float32).reshape(1, HQ)),
        "wqT": np.ascontiguousarray(np.asarray(inputs["L0_attn_W_qkvu"], np.float32).T),
        "woT": np.ascontiguousarray(np.asarray(inputs["L0_attn_W_out"], np.float32).T),
        "wg_in": np.ascontiguousarray(inputs["L0_ffn_Wg"], np.float32),
        "bg_in": np.ascontiguousarray(inputs["L0_ffn_bg"], np.float32),
        "wup_in": np.ascontiguousarray(inputs["L0_ffn_Wup"], np.float32),
        "bup_in": np.ascontiguousarray(inputs["L0_ffn_bup"], np.float32),
        "wsw_in": np.ascontiguousarray(inputs["L0_ffn_Wsw"], np.float32),
        "bsw_in": np.ascontiguousarray(inputs["L0_ffn_bsw"], np.float32),
        "wdn_in": np.ascontiguousarray(inputs["L0_ffn_Wdn"], np.float32),
        "bdn_in": np.ascontiguousarray(inputs["L0_ffn_bdn"], np.float32),
        "wsu_in": np.ascontiguousarray(inputs["L0_ffn_Wsu"], np.float32),
        "bsu_in": np.ascontiguousarray(np.asarray(inputs["L0_ffn_bsu"], np.float32).reshape(1, SED)),
        "wsa_in": np.ascontiguousarray(inputs["L0_ffn_Wsa"], np.float32),
        "bsa_in": np.ascontiguousarray(np.asarray(inputs["L0_ffn_bsa"], np.float32).reshape(1, E)),
        "wsd_in": np.ascontiguousarray(inputs["L0_ffn_Wsd"], np.float32),
        "bsd_in": np.ascontiguousarray(np.asarray(inputs["L0_ffn_bsd"], np.float32).reshape(1, D)),
    }
    in_maps = []
    for c in range(8):
        b, pos = c // 4, c % 4
        main = x[b, pos * MAIN:(pos + 1) * MAIN]
        if pos == 0:
            lbp = np.zeros((LB, D), np.float32)
        else:
            lbp = x[b, pos * MAIN - LB: pos * MAIN]
        m = dict(base)
        m["x_ext"] = np.ascontiguousarray(np.concatenate([lbp, main], axis=0))
        in_maps.append(m)
    return in_maps


def kernel(**inputs):
    nc = _get_prog()
    in_maps = make_in_maps(inputs)
    res = run_bass_kernel_spmd(nc, in_maps, core_ids=list(range(8)))
    out = np.empty((2, 16384, D), np.float32)
    for c in range(8):
        out[c // 4, (c % 4) * MAIN:((c % 4) + 1) * MAIN] = res.results[c]["y"]
    return out
